# version 04
# baseline (speedup 1.0000x reference)
import numpy as np
import concourse.bass as bass
import concourse.mybir as mybir
from concourse.bacc import Bacc
from concourse.bass_utils import run_bass_kernel_spmd
from concourse import library_config
from contextlib import ExitStack

F32 = mybir.dt.float32
F32R = mybir.dt.float32r
BF16 = mybir.dt.bfloat16
U32 = mybir.dt.uint32
AF = mybir.ActivationFunctionType
ALU = mybir.AluOpType

DM = 1024
DI = 1024
DS = 16
L = 4096
LH = 2048
TT = 512
NT = L // TT
NCH = DI // 128
HID = 2048
NE = 8
RD = [(0, 1)] + [None] * 7
DEBUG_SCRATCH = False


class Ctr:
    __slots__ = ("sem", "n")

    def __init__(self, sem):
        self.sem = sem
        self.n = 0

    def inc(self, ins, k=1):
        ins.then_inc(self.sem, k)
        self.n += k
        return self.n


def _rsend(gp, prep, data_out, data_in, rsem, lsem, sim_inc):
    if data_out is None:
        ins = gp.remote_sem_update_broadcast(rsem.sem, lsem.sem, rdests=RD,
                                             sim_remote_sem_inc=sim_inc)
    else:
        ins = gp.remote_dma_broadcast(data_out, data_in, rsem.sem, lsem.sem,
                                      rdests=RD, sim_remote_sem_inc=sim_inc)
    lsem.n += 16
    prep.inc(ins)
    rsem.n += sim_inc
    gp.wait_ge(prep.sem, prep.n)
    gp.trigger_dma(count=1)


def build_program():
    nc = Bacc("TRN2", target_bir_lowering=False, debug=False)

    def din(name, shape, dt=F32):
        return nc.dram_tensor(name, shape, dt, kind="ExternalInput").ap()

    def dint(name, shape, dt=F32):
        kind = "ExternalOutput" if DEBUG_SCRATCH else "Internal"
        return nc.dram_tensor(name, shape, dt, kind=kind).ap()

    x_true = din("x_true", [L, DM])
    x_my = din("x_my", [LH, DM])
    tb = din("tb", [1, 2], U32)
    ident_in = din("ident_in", [128, 128])
    zeros4 = din("zeros4", [128, 4])
    in_w = din("in_w", [128, NCH * 2048])
    norm_w = din("norm_w", [128, 8])
    conv_w = din("conv_w", [128, NCH * 4])
    conv_b = din("conv_b", [128, 8])
    xproj = din("xproj", [128, NCH * 128])
    dt_w = din("dt_w", [64, DI])
    dt_b = din("dt_b", [128, 8])
    log_A = din("log_A", [128, 128])
    D_in = din("D", [128, 8])
    out_w = din("out_w", [128, 16 * 512])
    enc_w = din("enc_w", [128, 8 * HID])
    enc_b = din("enc_b", [128, 16])
    s2c_w = din("s2c_w", [128, 16 * 64])
    s2c_b = din("s2c_b", [64, 1])
    r_w1 = din("r_w1", [64, 64])
    r_b1 = din("r_b1", [64, 1])
    r_w2 = din("r_w2", [64, 8])
    r_b2 = din("r_b2", [8, 1])
    e_w1 = din("e_w1", [64, NE * 1024])
    e_b1 = din("e_b1", [128, NE * 8])
    e_w2 = din("e_w2", [128, NE * 8 * 64])
    e_b2 = din("e_b2", [64, NE])
    c2s_w = din("c2s_w", [64, HID])
    c2s_b = din("c2s_b", [128, 16])
    dec_w = din("dec_w", [128, 16 * 1024])
    dec_b = din("dec_b", [128, 8])
    ln_g = din("ln_g", [128, 8])
    ln_b = din("ln_b", [128, 8])
    out = nc.dram_tensor("out", [L, 512], F32, kind="ExternalOutput").ap()

    xss_sp = dint("xss_sp", [128, NCH * L])
    sz_sp = dint("sz_sp", [128, NCH * L])
    bc_sp = dint("bc_sp", [96, L])
    yg_my_sp = dint("yg_my_sp", [128, NT * NCH * TT], BF16)
    yg_pr_sp = dint("yg_pr_sp", [128, NT * NCH * TT], BF16)
    mm_sp = dint("mm_sp", [128, 4 * L])
    dec_sp = dint("dec_sp", [128, 8 * LH])
    st2_sp = dint("st2_sp", [2, LH])
    comb_sp = dint("comb_sp", [8, LH])
    lg_sp = dint("lg_sp", [8, LH]) if DEBUG_SCRATCH else None
    spk_sp = dint("spk_sp", [128, 16 * LH], BF16) if DEBUG_SCRATCH else None
    xnf_sp = dint("xnf_sp", [128, 8 * LH], BF16) if DEBUG_SCRATCH else None
    cont_sp = dint("cont_sp", [64, LH], BF16) if DEBUG_SCRATCH else None
    g_my_sp = dint("g_my_sp", [128, 4 * LH])
    g_pr_sp = dint("g_pr_sp", [128, 4 * LH])
    dbg_st = dint("dbg_st", [128, 3 * 32]) if DEBUG_SCRATCH else None
    dbg_st2 = dint("dbg_st2", [128, 64]) if DEBUG_SCRATCH else None
    dbg_xnn = dint("dbg_xnn", [128, NCH * TT]) if DEBUG_SCRATCH else None
    dbg_xnt = dint("dbg_xnt", [128, DM]) if DEBUG_SCRATCH else None
    dbg_xtb = dint("dbg_xtb", [128, DM]) if DEBUG_SCRATCH else None
    dbg_str = dint("dbg_str", [128, NCH * (TT + 4)]) if DEBUG_SCRATCH else None
    dbg_w = dint("dbg_w", [128, 2048]) if DEBUG_SCRATCH else None

    sy, sc, ve, te, gp = nc.sync, nc.scalar, nc.vector, nc.tensor, nc.gpsimd

    es = ExitStack()
    with es:
        def sbuf(name, shape, dt=F32):
            return es.enter_context(nc.sbuf_tensor(name, shape, dt))

        def sem(name):
            return Ctr(es.enter_context(nc.semaphore(name)))

        ident_r = sbuf("ident_r", [128, 128], F32R)
        ident_f = sbuf("ident_f", [128, 128])
        m_st = sbuf("m_st", [128, 32])
        r_st = sbuf("r_st", [128, 32])
        rc_st = sbuf("rc_st", [128, 32])
        sums = sbuf("sums", [128, 32])
        sqs = sbuf("sqs", [128, 32])
        A_neg = sbuf("A_neg", [128, 128])
        D_col = sbuf("D_col", [128, 8])
        cvb_col = sbuf("cvb_col", [128, 8])
        dtb_col = sbuf("dtb_col", [128, 8])
        cw_col = sbuf("cw_col", [128, NCH * 4])
        nw_col = sbuf("nw_col", [128, 8])
        state = sbuf("state", [128, 128])
        tb_sb = sbuf("tb_sb", [1, 2], U32)
        ones_r = sbuf("ones_r", [128, 128], F32R)
        cst = sbuf("cst", [128, 4], F32)
        st_row = sbuf("st_row", [1, 2 * TT], F32)
        lng_c = sbuf("lng_c", [128, 8], F32)
        lnb_c = sbuf("lnb_c", [128, 8], F32)
        ve.memset(cst[:, 0:1], 1e-5)
        ve.memset(cst[:, 1:2], 1e-6)
        ve.memset(cst[:, 2:3], 1.0)
        ve.memset(cst[:, 3:4], 1e-11)

        ph_b = sem("ph_b")
        bar_t = sbuf("bar_t", [1, 8], F32)

        def phase_barrier():
            ve.memset(bar_t[0:1, 0:1], 0.0).then_inc(ph_b.sem, 1)
            sc.activation(bar_t[0:1, 1:2], bar_t[0:1, 2:3], AF.Copy
                          ).then_inc(ph_b.sem, 1)
            gp.memset(bar_t[0:1, 3:4], 0.0).then_inc(ph_b.sem, 1)
            sy.sem_inc(ph_b.sem, 1)
            te.sem_inc(ph_b.sem, 1)
            ph_b.n += 5
            for eng in (sy, sc, ve, te, gp):
                eng.wait_ge(ph_b.sem, ph_b.n)

        rs_xp = sem("rs_xp")
        rs_yg = sem("rs_yg")
        rs_ga = sem("rs_ga")
        rs_rdy = sem("rs_rdy")
        ls_loc = sem("ls_loc")
        prep = sem("prep")
        ack_yg = sem("ack_yg")
        d_w = sem("d_w")
        d_out = sem("d_out")
        d_sp = sem("d_sp")

        gp.load_library(library_config.proxy)

        for t_, s_ in ((tb_sb, tb), (ident_f, ident_in), (A_neg, log_A),
                       (D_col, D_in), (cvb_col, conv_b), (dtb_col, dt_b),
                       (cw_col, conv_w), (nw_col, norm_w)):
            gp.dma_start(out=t_[:], in_=s_).then_inc(d_w.sem, 16)
            d_w.n += 16
        n_w0 = d_w.n
        s_p0 = sem("s_p0")
        A_tmp = sbuf("A_tmp", [128, 128], F32)
        sc.wait_ge(d_w.sem, n_w0)
        s_p0.inc(sc.activation(A_tmp[:], A_neg[:], AF.Exp))
        v_cmp = sem("v_cmp")
        ve.wait_ge(d_w.sem, n_w0)
        ve.tensor_copy(ident_r[:], ident_f[:])
        ve.memset(state[:], 0.0)
        ones_f = sbuf("ones_f", [128, 128], F32)
        v_one = sem("v_one")
        v_one.inc(ve.memset(ones_f[:], 0.0))
        sc.wait_ge(v_one.sem, 1)
        v_one.inc(sc.activation(ones_r[:], ones_f[:], AF.Copy, scale=0.0,
                                bias=1.0))
        ve.wait_ge(s_p0.sem, 1)
        ve.tensor_scalar(A_neg[:], A_tmp[:], -1.0, None, op0=ALU.mult)
        mask_i = sbuf("mask_i", [128, 128], mybir.dt.int32)
        maskf = sbuf("maskf", [128, 128])
        g_cmp = sem("g_cmp")
        gp.iota(mask_i[:], pattern=[[1, 128]], base=0, channel_multiplier=-1)
        g_cmp.inc(gp.tensor_copy(mask_i[:], mask_i[:]))
        ve.wait_ge(g_cmp.sem, g_cmp.n)
        ve.tensor_scalar(maskf[:], mask_i[:], 0, None, op0=ALU.is_equal)

        stats_done = sem("stats_done")
        _es1 = ExitStack()
        xt_a = _es1.enter_context(nc.sbuf_tensor("xt_a", [128, 2 * DM], F32))
        sq_scr = _es1.enter_context(nc.sbuf_tensor("sq_scr", [128, 2 * DM], F32))
        var_t = _es1.enter_context(nc.sbuf_tensor("var_t", [128, 32], F32))
        tmp_t = _es1.enter_context(nc.sbuf_tensor("tmp_t", [128, 32], F32))
        d_xa = [sem("d_xa0"), sem("d_xa1")]
        v_st = sem("v_st")
        s_sq = sem("s_sq")
        for tbi in range(32):
            p = tbi % 2
            if tbi >= 2:
                sy.wait_ge(v_st.sem, 2 * tbi - 3)
                sy.wait_ge(s_sq.sem, tbi - 1)
            sy.dma_start(out=xt_a[:, p * DM:(p + 1) * DM],
                         in_=x_true[tbi * 128:(tbi + 1) * 128, :]
                         ).then_inc(d_xa[p].sem, 16)
            d_xa[p].n += 16
            ve.wait_ge(d_xa[p].sem, d_xa[p].n)
            v_st.inc(ve.tensor_reduce(sums[:, tbi:tbi + 1],
                                      xt_a[:, p * DM:(p + 1) * DM],
                                      axis=mybir.AxisListType.X,
                                      op=ALU.add))
            sc.wait_ge(d_xa[p].sem, d_xa[p].n)
            if tbi >= 2:
                sc.wait_ge(v_st.sem, 2 * tbi - 2)
            s_sq.inc(sc.activation(sq_scr[:, p * DM:(p + 1) * DM],
                                   xt_a[:, p * DM:(p + 1) * DM], AF.Square))
            ve.wait_ge(s_sq.sem, tbi + 1)
            v_st.inc(ve.tensor_reduce(sqs[:, tbi:tbi + 1],
                                      sq_scr[:, p * DM:(p + 1) * DM],
                                      axis=mybir.AxisListType.X,
                                      op=ALU.add))
        ve.wait_ge(s_sq.sem, 32)
        sc.wait_ge(v_st.sem, 64)
        s_st = sem("s_st")
        s_st.inc(sc.activation(m_st[:], sums[:], AF.Copy, scale=1.0 / DM))
        s_st.inc(sc.activation(tmp_t[:], m_st[:], AF.Square))
        s_st.inc(sc.activation(var_t[:], sqs[:], AF.Copy, scale=1.0 / DM))
        gp.wait_ge(s_st.sem, 3)
        g_cmp.inc(gp.tensor_tensor(sums[:], var_t[:], tmp_t[:],
                                   op=ALU.subtract))
        v_st2 = sem("v_st2")
        sc.wait_ge(g_cmp.sem, g_cmp.n)
        v_st2.inc(sc.activation(tmp_t[:], sums[:], AF.Ln, bias=cst[:, 0:1]))
        s_st.inc(sc.activation(r_st[:], tmp_t[:], AF.Exp, scale=-0.5))
        sc.activation(tmp_t[:], sums[:], AF.Ln, scale=1.0 + 1e-6,
                      bias=cst[:, 3:4])
        s_st.inc(sc.activation(rc_st[:], tmp_t[:], AF.Exp, scale=-0.5))
        ve.wait_ge(s_st.sem, 5)
        stats_done.inc(ve.tensor_copy(sqs[:], rc_st[:]))

        _es1.close()
        phase_barrier()
        s_dtr = sem("s_dtr")
        d_bcs = sem("d_bcs")
        _es2 = ExitStack()
        w_stg = _es2.enter_context(nc.sbuf_tensor("w_stg", [128, 2048], F32))
        in_w_r = _es2.enter_context(nc.sbuf_tensor("in_w_r", [128, NCH * 2048], F32R))
        xproj_r = _es2.enter_context(nc.sbuf_tensor("xproj_r", [128, NCH * 128], F32))
        xt_b = _es2.enter_context(nc.sbuf_tensor("xt_b", [128, 2 * DM], F32))
        xnn_t = _es2.enter_context(nc.sbuf_tensor("xnn_t", [128, 2 * DM], F32R))
        xnn_fm = _es2.enter_context(nc.sbuf_tensor("xnn_fm", [128, NCH * TT], F32R))
        strips = _es2.enter_context(nc.sbuf_tensor("strips", [128, NCH * (TT + 4)], F32R))
        xss_f = _es2.enter_context(nc.sbuf_tensor("xss_f", [128, 2 * TT], F32))
        xss_r = _es2.enter_context(nc.sbuf_tensor("xss_r", [128, 2 * TT], F32R))
        sz_f = _es2.enter_context(nc.sbuf_tensor("sz_f", [128, 2 * TT], F32))
        xp_sb = _es2.enter_context(nc.sbuf_tensor("xp_sb", [128, L], F32))
        xp_pr = _es2.enter_context(nc.sbuf_tensor("xp_pr", [128, L], F32))
        diag_w = _es2.enter_context(
            nc.sbuf_tensor("diag_w", [128, NCH * 4 * 128], F32R))
        ps_xz = _es2.enter_context(nc.psum_tensor("ps_xz", [128, 2 * TT], F32))
        ps_tr = _es2.enter_context(nc.psum_tensor("ps_tr", [128, 2 * TT], F32))
        ps_cv = _es2.enter_context(nc.psum_tensor("ps_cv", [128, 2 * TT], F32))
        ps_xp = _es2.enter_context(nc.psum_tensor("ps_xp", [128, TT], F32))
        for c in range(NCH):
            for k in range(4):
                i = c * 4 + k
                v_cmp.inc(ve.tensor_scalar(diag_w[:, i * 128:(i + 1) * 128],
                                           maskf[:], cw_col[:, i:i + 1], None,
                                           op0=ALU.mult))
        s_wp = sem("s_wp")
        v_rnd = sem("v_rnd")
        for c in range(NCH):
            gp.wait_ge(s_wp.sem, c)
            gp.dma_start(out=w_stg[:], in_=in_w[:, c * 2048:(c + 1) * 2048]
                         ).then_inc(d_w.sem, 16)
            d_w.n += 16
            sc.wait_ge(d_w.sem, d_w.n)
            s_wp.inc(sc.activation(in_w_r[:, c * 2048:(c + 1) * 2048],
                                   w_stg[:], AF.Copy,
                                   scale=nw_col[:, c:c + 1]))
        gp.dma_start(out=xproj_r[:], in_=xproj).then_inc(d_w.sem, 16)
        d_w.n += 16
        n_w_p1x = d_w.n
        v_rnd.inc(ve.tensor_copy(bar_t[0:1, 4:5], bar_t[0:1, 5:6]))

        d_xb = d_xa
        v_xnn = sem("v_xnn")
        t_tr = sem("t_tr")
        s_evk = sem("s_evk")
        t_mz = sem("t_mz")
        s_ev2 = sem("s_ev2")
        v_f32r = sem("v_f32r")
        t_cv = sem("t_cv")
        s_cvs = sem("s_cvs")
        v_hl = sem("v_hl")
        t_xp = sem("t_xp")
        s_xpe = sem("s_xpe")

        d_z4 = sem("d_z4")
        for _c in range(NCH):
            gp.dma_start(out=strips[:, _c * (TT + 4):_c * (TT + 4) + 4],
                         in_=zeros4).then_inc(d_z4.sem, 16)
            d_z4.n += 16
        ev2_hist = []
        for j in range(NT):
            p = j % 2
            for t4 in range(4):
                tbi = j * 4 + t4
                pp = t4 % 2
                if tbi >= 2:
                    sy.wait_ge(v_xnn.sem, tbi - 1)
                sy.dma_start(out=xt_b[:, pp * DM:(pp + 1) * DM],
                             in_=x_true[tbi * 128:(tbi + 1) * 128, :]
                             ).then_inc(d_xb[pp].sem, 16)
                d_xb[pp].n += 16
                ve.wait_ge(d_xb[pp].sem, d_xb[pp].n)
                ve.wait_ge(stats_done.sem, 1)
                if tbi >= 2:
                    ve.wait_ge(t_tr.sem, tbi - 1)
                v_xnn.inc(ve.tensor_scalar(
                    xnn_t[:, pp * DM:(pp + 1) * DM],
                    xt_b[:, pp * DM:(pp + 1) * DM],
                    m_st[:, tbi:tbi + 1], rc_st[:, tbi:tbi + 1],
                    op0=ALU.subtract, op1=ALU.mult))
                if DEBUG_SCRATCH and tbi == 12:
                    sy.wait_ge(v_xnn.sem, 1)
                    sy.dma_start(out=dbg_xnt,
                                 in_=xnn_t[:, pp * DM:(pp + 1) * DM
                                           ].bitcast(F32)
                                 ).then_inc(d_bcs.sem, 16)
                    sy.dma_start(out=dbg_xtb, in_=xt_b[:, 0:DM]
                                 ).then_inc(d_bcs.sem, 16)
                    ve.wait_ge(d_bcs.sem, 32)
                te.wait_ge(v_xnn.sem, v_xnn.n)
                if tbi >= 1:
                    te.wait_ge(s_evk.sem, tbi)
                for q in range(8):
                    ins = te.transpose(
                        ps_tr[:, q * 128:(q + 1) * 128].bitcast(F32R),
                        xnn_t[:, pp * DM + q * 128:pp * DM + (q + 1) * 128],
                        ident_r[:])
                t_tr.inc(ins)
                sc.wait_ge(t_tr.sem, t_tr.n)
                if j >= 1 and t4 == 0:
                    sc.wait_ge(t_mz.sem, j * 16)
                for q in range(8):
                    ins = sc.activation(
                        xnn_fm[:, q * TT + t4 * 128:
                               q * TT + (t4 + 1) * 128],
                        ps_tr[:, q * 128:(q + 1) * 128],
                        AF.Copy)
                s_evk.inc(ins)
            te.wait_ge(s_evk.sem, s_evk.n)
            te.wait_ge(s_wp.sem, NCH)
            for m in range(16):
                pb = m % 2
                if len(ev2_hist) >= 2:
                    te.wait_ge(s_ev2.sem, ev2_hist[-2])
                for k in range(NCH):
                    ins = te.matmul(
                        ps_xz[:, pb * TT:(pb + 1) * TT],
                        in_w_r[:, k * 2048 + m * 128:k * 2048 + (m + 1) * 128],
                        xnn_fm[:, k * TT:(k + 1) * TT],
                        start=(k == 0), stop=(k == NCH - 1))
                t_mz.inc(ins)
                sc.wait_ge(t_mz.sem, t_mz.n)
                if m < 8:
                    c = m
                    if j > 0:
                        sc.wait_ge(v_hl.sem, (j - 1) * 8 + c + 1)
                        sc.wait_ge(t_cv.sem, (j - 1) * 8 + c + 1)
                    s_ev2.inc(sc.activation(
                        strips[:, c * (TT + 4) + 4:(c + 1) * (TT + 4)],
                        ps_xz[:, pb * TT:(pb + 1) * TT], AF.Copy))
                else:
                    c = m - 8
                    pz = c % 2
                    if j * 8 + c >= 2:
                        sc.wait_ge(d_sp.sem, d_sp.n)
                    s_ev2.inc(sc.activation(sz_f[:, pz * TT:(pz + 1) * TT],
                                            ps_xz[:, pb * TT:(pb + 1) * TT],
                                            AF.Silu))
                    sy.wait_ge(s_ev2.sem, s_ev2.n)
                    sy.dma_start(
                        out=sz_sp[:, c * L + j * TT:c * L + (j + 1) * TT],
                        in_=sz_f[:, pz * TT:(pz + 1) * TT]
                    ).then_inc(d_sp.sem, 16)
                    d_sp.n += 16
                ev2_hist.append(s_ev2.n)
            if DEBUG_SCRATCH and j == 3:
                d_dbg = d_bcs
                sy.wait_ge(s_ev2.sem, s_ev2.n)
                sy.dma_start(out=dbg_str, in_=strips[:].bitcast(F32)
                             ).then_inc(d_dbg.sem, 16)
                sy.wait_ge(s_wp.sem, NCH)
                sy.dma_start(out=dbg_w, in_=in_w_r[:, 0:2048].bitcast(F32)
                             ).then_inc(d_dbg.sem, 16)
                d_dbg.n += 32
                sy.wait_ge(s_evk.sem, 4)
                sy.dma_start(out=dbg_xnn,
                             in_=xnn_fm[:].bitcast(F32)).then_inc(d_dbg.sem, 16)
                sy.dma_start(out=dbg_st[:, 0:32], in_=m_st[:]).then_inc(d_dbg.sem, 16)
                sy.dma_start(out=dbg_st[:, 32:64], in_=rc_st[:]).then_inc(d_dbg.sem, 16)
                sy.dma_start(out=dbg_st[:, 64:96], in_=r_st[:]).then_inc(d_dbg.sem, 16)
                d_dbg.n += 64
                sc.wait_ge(d_dbg.sem, 64)
            for c in range(NCH):
                pc = c % 2
                te.wait_ge(s_ev2.sem, ev2_hist[j * 16 + c])
                te.wait_ge(d_z4.sem, d_z4.n)
                te.wait_ge(v_cmp.sem, 32)
                if j * 8 + c >= 2:
                    te.wait_ge(s_cvs.sem, j * 8 + c - 1)
                for k in range(4):
                    ins = te.matmul(
                        ps_cv[:, pc * TT:(pc + 1) * TT],
                        diag_w[:, (c * 4 + k) * 128:(c * 4 + k + 1) * 128],
                        strips[:, c * (TT + 4) + k + 1:
                               c * (TT + 4) + k + 1 + TT],
                        start=(k == 0), stop=(k == 3))
                t_cv.inc(ins)
                sc.wait_ge(t_cv.sem, t_cv.n)
                v_hl.inc(sc.activation(
                    strips[:, c * (TT + 4) + 1:c * (TT + 4) + 4],
                    strips.ap().bitcast(F32)[:, c * (TT + 4) + 513:
                                             c * (TT + 4) + 516],
                    AF.Copy))
                sc.wait_ge(t_cv.sem, t_cv.n)
                if j * 8 + c >= 2:
                    sc.wait_ge(d_sp.sem, d_sp.n)
                    sc.wait_ge(t_xp.sem, 8 * j + c - 1)
                s_cvs.inc(sc.activation(xss_f[:, pc * TT:(pc + 1) * TT],
                                        ps_cv[:, pc * TT:(pc + 1) * TT],
                                        AF.Silu, bias=cvb_col[:, c:c + 1]))
                sy.wait_ge(s_cvs.sem, s_cvs.n)
                sy.dma_start(
                    out=xss_sp[:, c * L + j * TT:c * L + (j + 1) * TT],
                    in_=xss_f[:, pc * TT:(pc + 1) * TT]
                ).then_inc(d_sp.sem, 16)
                d_sp.n += 16
                v_f32r.inc(ve.tensor_copy(bar_t[0:1, 4:5],
                                           bar_t[0:1, 5:6]))
                te.wait_ge(s_cvs.sem, s_cvs.n)
                te.wait_ge(d_w.sem, n_w_p1x)
                if c == 0 and j > 0:
                    te.wait_ge(s_xpe.sem, j)
                t_xp.inc(te.matmul(ps_xp[:],
                                   xproj_r[:, c * 128:(c + 1) * 128],
                                   xss_f[:, pc * TT:(pc + 1) * TT],
                                   start=(c == 0), stop=(c == NCH - 1)))
            sc.wait_ge(t_xp.sem, t_xp.n)
            s_xpe.inc(sc.activation(xp_sb[:, j * TT:(j + 1) * TT],
                                    ps_xp[:], AF.Copy))

        gp.wait_ge(s_xpe.sem, NT)
        _rsend(gp, prep, None, None, rs_rdy, ls_loc, 2)
        gp.wait_ge(rs_rdy.sem, 2)
        _rsend(gp, prep, xp_pr[:], xp_sb[:], rs_xp, ls_loc, 2)
        ve.wait_ge(rs_xp.sem, 2)
        v_xpadd = sem("v_xpadd")
        v_xpadd.inc(ve.tensor_tensor(xp_pr[:], xp_sb[:], xp_pr[:],
                                     op=ALU.add))
        sy.wait_ge(v_xpadd.sem, 1)
        sy.dma_start(out=bc_sp[:, :], in_=xp_pr[0:96, :]
                     ).then_inc(d_bcs.sem, 16)
        d_bcs.n += 16
        gp.wait_ge(ls_loc.sem, ls_loc.n)

        _es2.close()
        phase_barrier()
        d_ygs = sem("d_ygs")
        _es3 = ExitStack()
        Brep = _es3.enter_context(nc.sbuf_tensor("Brep", [128, DS * TT], F32))
        Crep = _es3.enter_context(nc.sbuf_tensor("Crep", [128, DS * TT], F32))
        del_c = _es3.enter_context(nc.sbuf_tensor("del_c", [128, 2 * TT], F32))
        dx_c = _es3.enter_context(nc.sbuf_tensor("dx_c", [128, 2 * TT], F32))
        xs_c = _es3.enter_context(nc.sbuf_tensor("xs_c", [128, 2 * TT], F32))
        sz_c = _es3.enter_context(nc.sbuf_tensor("sz_c", [128, 2 * TT], F32))
        sp_a = _es3.enter_context(nc.sbuf_tensor("sp_a", [128, 2 * TT], F32))
        sp_l = _es3.enter_context(nc.sbuf_tensor("sp_l", [128, 2 * TT], F32))
        dA_b = _es3.enter_context(nc.sbuf_tensor("dA_b", [128, 4 * TT], F32))
        u_b = _es3.enter_context(nc.sbuf_tensor("u_b", [128, 2 * TT], F32))
        h_b = _es3.enter_context(nc.sbuf_tensor("h_b", [128, 2 * TT], F32))
        p_b = _es3.enter_context(nc.sbuf_tensor("p_b", [128, 2 * TT], F32R))
        yg_t = _es3.enter_context(nc.sbuf_tensor("yg_t", [128, 2 * NCH * TT], BF16))
        yg_rx = _es3.enter_context(nc.sbuf_tensor("yg_rx", [128, 2 * NCH * TT], BF16))
        ps_d = _es3.enter_context(nc.psum_tensor("ps_d", [128, TT], F32))
        ps_y = _es3.enter_context(nc.psum_tensor("ps_y", [128, 2 * TT], F32))
        dtw_r = _es3.enter_context(nc.sbuf_tensor("dtw_r", [64, DI], F32))
        dt_r = _es3.enter_context(nc.sbuf_tensor("dt_r", [64, L], F32))
        dt_stg = _es3.enter_context(nc.sbuf_tensor("dt_stg", [64, L], F32))
        d_dt = sem("d_dt")
        v_dtw = sem("v_dtw")
        gp.dma_start(out=dt_stg[0:64, 0:DI], in_=dt_w).then_inc(d_dt.sem, 16)
        d_dt.n += 16
        ve.wait_ge(d_dt.sem, 16)
        v_dtw.inc(ve.tensor_copy(dtw_r[:], dt_stg[0:64, 0:DI]))
        gp.wait_ge(d_bcs.sem, 16)
        gp.wait_ge(v_dtw.sem, 1)
        gp.dma_start(out=dt_stg[:], in_=bc_sp[0:64, :]).then_inc(d_dt.sem, 16)
        d_dt.n += 16
        ve.wait_ge(d_dt.sem, 32)
        s_dtr.n = 0
        s_dtr.inc(ve.tensor_copy(dt_r[:], dt_stg[:]))
        d_bc = sem("d_bc")
        d_dx = [sem("d_dx0"), sem("d_dx1")]
        s_dA = sem("s_dA")
        g_u = sem("g_u")
        v_h = sem("v_h")
        s_stc = sem("s_stc")
        v_p = sem("v_p")
        t_y = sem("t_y")
        t_dl = sem("t_dl")
        s_sp4 = sem("s_sp4")
        v_dl = sem("v_dl")
        v_dx = sem("v_dx")
        v_gt = sem("v_gt")
        d_ygd = sem("d_ygd")

        _rsend(gp, prep, None, None, ack_yg, ls_loc, 4)

        NTOT = NT * NCH * DS
        bcs_base = d_bcs.n

        def bcB_thr(j2, s2):
            return (j2 * DS + s2 + 1) * 16

        def bcC_thr(j2, s2):
            return bcs_base + (j2 * DS + s2 + 1) * 16

        def emit_bc(j2):
            for s2 in range(DS):
                if j2 == 0 and s2 == 0:
                    sy.wait_ge(d_bcs.sem, bcs_base)
                if j2 > 0:
                    sy.wait_ge(g_u.sem, (j2 - 1) * 128 + 7 * DS + s2 + 1)
                    sy.wait_ge(v_p.sem, (j2 - 1) * 128 + 7 * DS + s2 + 1)
                sy.dma_start(
                    out=Brep[:, s2 * TT:(s2 + 1) * TT],
                    in_=bc_sp[64 + s2, j2 * TT:(j2 + 1) * TT
                              ].partition_broadcast(128)
                ).then_inc(d_bc.sem, 16)
                d_bc.n += 16
                sy.dma_start(
                    out=Crep[:, s2 * TT:(s2 + 1) * TT],
                    in_=bc_sp[80 + s2, j2 * TT:(j2 + 1) * TT
                              ].partition_broadcast(128)
                ).then_inc(d_bcs.sem, 16)
                d_bcs.n += 16

        def emit_loads(jc2):
            j2, c2 = divmod(jc2, NCH)
            pc2 = c2 % 2
            if jc2 >= 2:
                sy.wait_ge(g_u.sem, (jc2 - 1) * DS)
                sy.wait_ge(v_gt.sem, jc2 - 1)
            sy.dma_start(out=xs_c[:, pc2 * TT:(pc2 + 1) * TT],
                         in_=xss_sp[:, c2 * L + j2 * TT:c2 * L + (j2 + 1) * TT]
                         ).then_inc(d_dx[pc2].sem, 16)
            d_dx[pc2].n += 16
            sy.dma_start(out=sz_c[:, pc2 * TT:(pc2 + 1) * TT],
                         in_=sz_sp[:, c2 * L + j2 * TT:c2 * L + (j2 + 1) * TT]
                         ).then_inc(d_dx[pc2].sem, 16)
            d_dx[pc2].n += 16

        def emit_prep(jc2):
            j2, c2 = divmod(jc2, NCH)
            pc2 = c2 % 2
            te.wait_ge(s_dtr.sem, 1)
            if jc2 >= 1:
                te.wait_ge(s_sp4.sem, 2 * jc2)
            t_dl.inc(te.matmul(ps_d[:], dtw_r[:, c2 * 128:(c2 + 1) * 128],
                               dt_r[:, j2 * TT:(j2 + 1) * TT],
                               start=True, stop=True))
            sc.wait_ge(t_dl.sem, t_dl.n)
            if jc2 >= 2:
                sc.wait_ge(v_dl.sem, jc2 - 1)
            sc.activation(sp_a[:, pc2 * TT:(pc2 + 1) * TT], ps_d[:],
                          AF.Abs, bias=dtb_col[:, c2:c2 + 1])
            sc.activation(sp_l[:, pc2 * TT:(pc2 + 1) * TT],
                          sp_a[:, pc2 * TT:(pc2 + 1) * TT],
                          AF.Exp, scale=-1.0)
            s_sp4.inc(sc.activation(sp_a[:, pc2 * TT:(pc2 + 1) * TT],
                                    sp_l[:, pc2 * TT:(pc2 + 1) * TT],
                                    AF.Ln, bias=cst[:, 2:3]))
            s_sp4.inc(sc.activation(sp_l[:, pc2 * TT:(pc2 + 1) * TT],
                                    ps_d[:], AF.Relu,
                                    bias=dtb_col[:, c2:c2 + 1]))
            ve.wait_ge(s_sp4.sem, s_sp4.n)
            if jc2 >= 2:
                ve.wait_ge(s_dA.sem, (jc2 - 1) * DS)
            v_dl.inc(ve.tensor_tensor(del_c[:, pc2 * TT:(pc2 + 1) * TT],
                                      sp_l[:, pc2 * TT:(pc2 + 1) * TT],
                                      sp_a[:, pc2 * TT:(pc2 + 1) * TT],
                                      op=ALU.add))
            ve.wait_ge(d_dx[pc2].sem, d_dx[pc2].n)
            if jc2 >= 2:
                ve.wait_ge(g_u.sem, (jc2 - 1) * DS)
            v_dx.inc(ve.tensor_tensor(dx_c[:, pc2 * TT:(pc2 + 1) * TT],
                                      del_c[:, pc2 * TT:(pc2 + 1) * TT],
                                      xs_c[:, pc2 * TT:(pc2 + 1) * TT],
                                      op=ALU.mult))

        def emit_dA(it2):
            jc2 = it2 // DS
            c2 = jc2 % NCH
            sl = it2 % 4
            col2 = c2 * DS + (it2 % DS)
            if it2 >= 4:
                sc.wait_ge(v_h.sem, it2 - 3)
            sc.wait_ge(v_dl.sem, jc2 + 1)
            s_dA.inc(sc.activation(dA_b[:, sl * TT:(sl + 1) * TT],
                                   del_c[:, (c2 % 2) * TT:(c2 % 2) * TT + TT],
                                   AF.Exp, scale=A_neg[:, col2:col2 + 1]))

        def emit_u(it2):
            jc2 = it2 // DS
            j2 = jc2 // NCH
            c2 = jc2 % NCH
            s2 = it2 % DS
            pb2 = it2 % 2
            gp.wait_ge(v_dx.sem, jc2 + 1)
            gp.wait_ge(d_bc.sem, bcB_thr(j2, s2))
            if it2 >= 2:
                gp.wait_ge(v_h.sem, it2 - 1)
            g_u.inc(gp.tensor_tensor(u_b[:, pb2 * TT:(pb2 + 1) * TT],
                                     dx_c[:, (c2 % 2) * TT:(c2 % 2) * TT + TT],
                                     Brep[:, s2 * TT:(s2 + 1) * TT],
                                     op=ALU.mult))

        for j in range(NT):
            if j == 0:
                emit_loads(0)
                emit_loads(1)
                emit_bc(0)
                emit_prep(0)
                for w in range(3):
                    emit_dA(w)
                emit_u(0)
            for c in range(NCH):
                jc = j * NCH + c
                pc = c % 2
                for s in range(DS):
                    it = jc * DS + s
                    pb = it % 2
                    col = c * DS + s
                    if it + 3 < NTOT:
                        emit_dA(it + 3)
                    if it + 1 < NTOT:
                        emit_u(it + 1)
                    ve.wait_ge(s_dA.sem, it + 1)
                    ve.wait_ge(g_u.sem, it + 1)
                    if it >= 2:
                        ve.wait_ge(s_stc.sem, it - 1)
                        ve.wait_ge(t_y.sem, it - 1)
                    v_h.inc(ve.tensor_tensor_scan(
                        h_b[:, pb * TT:(pb + 1) * TT],
                        dA_b[:, (it % 4) * TT:(it % 4) * TT + TT],
                        u_b[:, pb * TT:(pb + 1) * TT],
                        state[:, col:col + 1],
                        op0=ALU.mult, op1=ALU.add))
                    if it >= 1:
                        itp = it - 1
                        pbp = itp % 2
                        colp = ((itp // DS) % NCH) * DS + itp % DS
                        sc.wait_ge(v_h.sem, it + 1)
                        s_stc.inc(sc.activation(
                            state[:, colp:colp + 1],
                            h_b[:, pbp * TT + TT - 1:pbp * TT + TT], AF.Copy))
                    ve.wait_ge(d_bcs.sem, bcC_thr(j, s))
                    v_p.inc(ve.tensor_tensor(p_b[:, pb * TT:(pb + 1) * TT],
                                             h_b[:, pb * TT:(pb + 1) * TT],
                                             Crep[:, s * TT:(s + 1) * TT],
                                             op=ALU.mult))
                    te.wait_ge(v_p.sem, it + 1)
                    if s == 0 and jc >= 2:
                        te.wait_ge(v_gt.sem, jc - 1)
                    t_y.inc(te.matmul(ps_y[:, pc * TT:(pc + 1) * TT],
                                      ident_r[:],
                                      p_b[:, pb * TT:(pb + 1) * TT],
                                      start=(s == 0), stop=(s == DS - 1)))
                    if s == 7 and jc + 1 < NT * NCH:
                        emit_prep(jc + 1)
                ve.wait_ge(t_y.sem, (jc + 1) * DS)
                ve.scalar_tensor_tensor(dx_c[:, pc * TT:(pc + 1) * TT],
                                        xs_c[:, pc * TT:(pc + 1) * TT],
                                        D_col[:, c:c + 1],
                                        ps_y[:, pc * TT:(pc + 1) * TT],
                                        op0=ALU.mult, op1=ALU.add)
                if j >= 2 and c == 0:
                    ve.wait_ge(d_ygs.sem, (j - 1) * 16)
                    ve.wait_ge(ls_loc.sem, 16 + (j - 1) * 16)
                v_gt.inc(ve.tensor_tensor(
                    yg_t[:, (j % 2) * NCH * TT + c * TT:
                         (j % 2) * NCH * TT + (c + 1) * TT],
                    dx_c[:, pc * TT:(pc + 1) * TT],
                    sz_c[:, pc * TT:(pc + 1) * TT], op=ALU.mult))
                if jc + 2 < NT * NCH:
                    emit_loads(jc + 2)
                if c == 6 and j + 1 < NT:
                    emit_bc(j + 1)
            jm = j % 2
            sy.wait_ge(v_gt.sem, (j + 1) * NCH)
            sy.dma_start(out=yg_my_sp[:, j * NCH * TT:(j + 1) * NCH * TT],
                         in_=yg_t[:, jm * NCH * TT:(jm + 1) * NCH * TT]
                         ).then_inc(d_ygs.sem, 16)
            d_ygs.n += 16
            gp.wait_ge(v_gt.sem, (j + 1) * NCH)
            if j >= 2:
                gp.wait_ge(ack_yg.sem, 4 + 2 * (j - 1))
            _rsend(gp, prep,
                   yg_rx[:, jm * NCH * TT:(jm + 1) * NCH * TT],
                   yg_t[:, jm * NCH * TT:(jm + 1) * NCH * TT],
                   rs_yg, ls_loc, 2)
            gp.wait_ge(rs_yg.sem, 2 * (j + 1))
            gp.dma_start(out=yg_pr_sp[:, j * NCH * TT:(j + 1) * NCH * TT],
                         in_=yg_rx[:, jm * NCH * TT:(jm + 1) * NCH * TT]
                         ).then_inc(d_ygd.sem, 16)
            d_ygd.n += 16
            gp.wait_ge(d_ygd.sem, d_ygd.n)
            _rsend(gp, prep, None, None, ack_yg, ls_loc, 2)

        _es3.close()
        phase_barrier()
        d_mm = sem("d_mm")
        _es4 = ExitStack()
        ow_bf = _es4.enter_context(nc.sbuf_tensor("ow_bf", [128, 16 * 512], BF16))
        yg_in = _es4.enter_context(nc.sbuf_tensor("yg_in", [128, 2 * 16 * TT], BF16))
        mm_f = _es4.enter_context(nc.sbuf_tensor("mm_f", [128, 2 * 4 * TT], F32))
        ps_mm = _es4.enter_context(nc.psum_tensor("ps_mm", [128, 2 * TT], F32))
        gp.dma_start(out=ow_bf[:], in_=out_w).then_inc(d_w.sem, 16)
        d_w.n += 16
        n_w_ow = d_w.n
        d_ygr = [sem("d_ygr0"), sem("d_ygr1")]
        t_mm = sem("t_mm")
        s_mme = sem("s_mme")
        for j in range(NT):
            jp = j % 2
            if j >= 2:
                sy.wait_ge(t_mm.sem, (j - 1) * 4)
            sy.wait_ge(d_ygs.sem, 16 * (j + 1))
            sy.dma_start(out=yg_in[:, jp * 16 * TT:jp * 16 * TT + 8 * TT],
                         in_=yg_my_sp[:, j * NCH * TT:(j + 1) * NCH * TT]
                         ).then_inc(d_ygr[jp].sem, 16)
            d_ygr[jp].n += 16
            sy.dma_start(out=yg_in[:, jp * 16 * TT + 8 * TT:
                                   (jp + 1) * 16 * TT],
                         in_=yg_pr_sp[:, j * NCH * TT:(j + 1) * NCH * TT]
                         ).then_inc(d_ygr[jp].sem, 16)
            d_ygr[jp].n += 16
            te.wait_ge(d_ygr[jp].sem, d_ygr[jp].n)
            te.wait_ge(d_w.sem, n_w_ow)
            for m in range(4):
                pm = m % 2
                if j * 4 + m >= 2:
                    te.wait_ge(s_mme.sem, j * 4 + m - 1)
                for k in range(16):
                    ins = te.matmul(
                        ps_mm[:, pm * TT:(pm + 1) * TT],
                        ow_bf[:, k * 512 + m * 128:k * 512 + (m + 1) * 128],
                        yg_in[:, jp * 16 * TT + k * TT:
                              jp * 16 * TT + (k + 1) * TT],
                        start=(k == 0), stop=(k == 15))
                t_mm.inc(ins)
                sc.wait_ge(t_mm.sem, t_mm.n)
                if j >= 2 and m == 0:
                    sc.wait_ge(d_mm.sem, (j - 1) * 16)
                s_mme.inc(sc.activation(
                    mm_f[:, (jp * 4 + m) * TT:(jp * 4 + m + 1) * TT],
                    ps_mm[:, pm * TT:(pm + 1) * TT], AF.Copy))
            sy.wait_ge(s_mme.sem, s_mme.n)
            sy.dma_start(
                out=mm_sp.rearrange("p (q l) -> p q l", q=4
                                    )[:, :, j * TT:(j + 1) * TT],
                in_=mm_f[:, jp * 4 * TT:(jp + 1) * 4 * TT
                         ].rearrange("p (q t) -> p q t", q=4)
            ).then_inc(d_mm.sem, 16)
            d_mm.n += 16

        _es4.close()
        phase_barrier()
        d_gsp = sem("d_gsp")
        _es5 = ExitStack()
        enc_bf = _es5.enter_context(nc.sbuf_tensor("enc_bf", [128, 8 * HID], BF16))
        dec_bf = _es5.enter_context(nc.sbuf_tensor("dec_bf", [128, 16 * 1024], BF16))
        ew1_bf = _es5.enter_context(nc.sbuf_tensor("ew1_bf", [64, NE * 1024], BF16))
        ew2_bf = _es5.enter_context(nc.sbuf_tensor("ew2_bf", [128, NE * 8 * 64], BF16))
        s2c_bf = _es5.enter_context(nc.sbuf_tensor("s2c_bf", [128, 16 * 64], BF16))
        c2s_bf = _es5.enter_context(nc.sbuf_tensor("c2s_bf", [64, HID], BF16))
        r1_bf = _es5.enter_context(nc.sbuf_tensor("r1_bf", [64, 64], BF16))
        r2_bf = _es5.enter_context(nc.sbuf_tensor("r2_bf", [64, 8], BF16))
        encb_c = _es5.enter_context(nc.sbuf_tensor("encb_c", [128, 16], F32))
        s2cb_c = _es5.enter_context(nc.sbuf_tensor("s2cb_c", [64, 1], F32))
        rb1_c = _es5.enter_context(nc.sbuf_tensor("rb1_c", [64, 1], F32))
        rb2_c = _es5.enter_context(nc.sbuf_tensor("rb2_c", [8, 1], F32))
        eb1_c = _es5.enter_context(nc.sbuf_tensor("eb1_c", [128, NE * 8], F32))
        eb2_c = _es5.enter_context(nc.sbuf_tensor("eb2_c", [64, NE], F32))
        c2sb_c = _es5.enter_context(nc.sbuf_tensor("c2sb_c", [128, 16], F32))
        decb_c = _es5.enter_context(nc.sbuf_tensor("decb_c", [128, 8], F32))
        sums2 = _es5.enter_context(nc.sbuf_tensor("sums2", [128, 16], F32))
        sqs2 = _es5.enter_context(nc.sbuf_tensor("sqs2", [128, 16], F32))
        m2_st = _es5.enter_context(nc.sbuf_tensor("m2_st", [128, 16], F32))
        r2_st = _es5.enter_context(nc.sbuf_tensor("r2_st", [128, 16], F32))
        sq_sc2 = _es5.enter_context(nc.sbuf_tensor("sq_sc2", [128, 2 * DM], F32))
        xm_t = _es5.enter_context(nc.sbuf_tensor("xm_t", [128, 2 * DM], F32))
        xn2_t = _es5.enter_context(nc.sbuf_tensor("xn2_t", [128, 2 * DM], F32R))
        xnf_bf = _es5.enter_context(nc.sbuf_tensor("xnf_bf", [128, 8 * TT], BF16))
        spk_bf = _es5.enter_context(nc.sbuf_tensor("spk_bf", [128, 16 * TT], BF16))
        cont_bf = _es5.enter_context(nc.sbuf_tensor("cont_bf", [64, 2 * TT], BF16))
        rh_bf = _es5.enter_context(nc.sbuf_tensor("rh_bf", [64, 2 * TT], BF16))
        lgits = _es5.enter_context(nc.sbuf_tensor("lgits", [8, 2 * TT], F32))
        comb = _es5.enter_context(nc.sbuf_tensor("comb", [8, 2 * TT], F32))
        lg_sb = _es5.enter_context(nc.sbuf_tensor("lg_sb", [128, 4 * 8], F32))
        rt_m = _es5.enter_context(nc.sbuf_tensor("rt_m", [128, 4 * 8], F32))
        rt_g = _es5.enter_context(nc.sbuf_tensor("rt_g", [128, 4 * 8], F32))
        rt_s = _es5.enter_context(nc.sbuf_tensor("rt_s", [128, 4 * 4], F32))
        eh_bf = _es5.enter_context(nc.sbuf_tensor("eh_bf", [128, 2 * TT], BF16))
        crep_sb = _es5.enter_context(nc.sbuf_tensor("crep_sb", [64, 2 * TT], F32))
        pe_sb = _es5.enter_context(nc.sbuf_tensor("pe_sb", [64, 2 * TT], F32))
        moe_f = _es5.enter_context(nc.sbuf_tensor("moe_f", [64, 2 * TT], F32))
        moe_bf = _es5.enter_context(nc.sbuf_tensor("moe_bf", [64, 2 * TT], BF16))
        rat_bf = _es5.enter_context(nc.sbuf_tensor("rat_bf", [128, 16 * TT], BF16))
        dec_f = _es5.enter_context(nc.sbuf_tensor("dec_f", [128, 2 * TT], F32))
        dec_r = _es5.enter_context(nc.sbuf_tensor("dec_r", [128, 2 * TT], F32R))
        ps_a = _es5.enter_context(nc.psum_tensor("ps_a", [128, 2 * TT], F32))
        ps_eo = _es5.enter_context(nc.psum_tensor("ps_eo", [128, TT], F32))
        ps_cr = _es5.enter_context(nc.psum_tensor("ps_cr", [128, TT], F32))
        ps_r = _es5.enter_context(nc.psum_tensor("ps_r", [128, 2 * TT], F32))
        ps_rep = _es5.enter_context(nc.psum_tensor("ps_rep", [128, 2 * TT], F32))
        for t_, s_ in ((enc_bf, enc_w), (dec_bf, dec_w), (ew1_bf, e_w1),
                       (ew2_bf, e_w2), (s2c_bf, s2c_w), (c2s_bf, c2s_w),
                       (r1_bf, r_w1), (r2_bf, r_w2), (encb_c, enc_b),
                       (s2cb_c, s2c_b), (rb1_c, r_b1), (rb2_c, r_b2),
                       (eb1_c, e_b1), (eb2_c, e_b2), (c2sb_c, c2s_b),
                       (decb_c, dec_b), (lng_c, ln_g), (lnb_c, ln_b)):
            gp.dma_start(out=t_[:], in_=s_).then_inc(d_w.sem, 16)
            d_w.n += 16
        n_w_ffn = d_w.n

        d_xm = d_xa
        v_s2 = sem("v_s2")
        s_q2 = sem("s_q2")
        for tbi in range(16):
            p = tbi % 2
            if tbi >= 2:
                sy.wait_ge(v_s2.sem, 2 * tbi - 3)
                sy.wait_ge(s_q2.sem, tbi - 1)
            sy.dma_start(out=xm_t[:, p * DM:(p + 1) * DM],
                         in_=x_my[tbi * 128:(tbi + 1) * 128, :]
                         ).then_inc(d_xm[p].sem, 16)
            d_xm[p].n += 16
            ve.wait_ge(d_xm[p].sem, d_xm[p].n)
            v_s2.inc(ve.tensor_reduce(sums2[:, tbi:tbi + 1],
                                      xm_t[:, p * DM:(p + 1) * DM],
                                      axis=mybir.AxisListType.X,
                                      op=ALU.add))
            sc.wait_ge(d_xm[p].sem, d_xm[p].n)
            if tbi >= 2:
                sc.wait_ge(v_s2.sem, 2 * tbi - 2)
            s_q2.inc(sc.activation(sq_sc2[:, p * DM:(p + 1) * DM],
                                   xm_t[:, p * DM:(p + 1) * DM], AF.Square))
            ve.wait_ge(s_q2.sem, tbi + 1)
            v_s2.inc(ve.tensor_reduce(sqs2[:, tbi:tbi + 1],
                                      sq_sc2[:, p * DM:(p + 1) * DM],
                                      axis=mybir.AxisListType.X,
                                      op=ALU.add))
        ve.wait_ge(s_q2.sem, 16)
        sc.wait_ge(v_s2.sem, 32)
        v_s2b = sem("v_s2b")
        v_s2b.inc(sc.activation(m2_st[:], sums2[:], AF.Copy, scale=1.0 / DM))
        v_s2b.inc(sc.activation(r2_st[:], m2_st[:], AF.Square))
        v_s2b.inc(sc.activation(sq_sc2[:, 0:16], sqs2[:], AF.Copy,
                                scale=1.0 / DM))
        gp.wait_ge(v_s2b.sem, 3)
        g_cmp.inc(gp.tensor_tensor(sums2[:], sq_sc2[:, 0:16], r2_st[:],
                                   op=ALU.subtract))
        sc.wait_ge(g_cmp.sem, g_cmp.n)
        sc.activation(sq_sc2[:, 16:32], sums2[:], AF.Ln, bias=cst[:, 0:1])
        s_st2 = sem("s_st2")
        s_st2.inc(sc.activation(r2_st[:], sq_sc2[:, 16:32], AF.Exp,
                                scale=-0.5))
        if DEBUG_SCRATCH:
            ve.wait_ge(s_st2.sem, 1)
            ve.tensor_copy(sq_sc2[:, 64:80], m2_st[:])
            ve.tensor_copy(sq_sc2[:, 80:96], r2_st[:])
            ve.tensor_copy(sq_sc2[:, 96:112], sums2[:])
            v_s2b.inc(ve.tensor_copy(sq_sc2[:, 112:128], sqs2[:]))
            gp.wait_ge(v_s2b.sem, 4)
            gp.dma_start(out=dbg_st2[:, :], in_=sq_sc2[:, 64:128]
                         ).then_inc(d_xm[0].sem, 16)
            d_xm[0].n += 16
            ve.wait_ge(d_xm[0].sem, d_xm[0].n)

        v_xn2 = sem("v_xn2")
        t_tr2 = sem("t_tr2")
        s_ek2 = sem("s_ek2")
        t_enc = sem("t_enc")
        s_spk = sem("s_spk")
        t_s2c = sem("t_s2c")
        s_cnt = sem("s_cnt")
        t_rt = sem("t_rt")
        s_rt = sem("s_rt")
        v_rt = sem("v_rt")
        s_cmb = sem("s_cmb")
        t_eh = sem("t_eh")
        d_cmb = sem("d_cmb")
        d_crp = sem("d_crp")
        s_eh = sem("s_eh")
        t_eo = sem("t_eo")
        s_cre = sem("s_cre")
        v_moe = sem("v_moe")
        s_mbf = sem("s_mbf")
        t_c2s = sem("t_c2s")
        s_rat = sem("s_rat")
        t_dec = sem("t_dec")
        s_dec = sem("s_dec")
        s_dcr = sem("s_dcr")
        t_sum = sem("t_sum")
        v_lnf = sem("v_lnf")
        d_st = sem("d_st")
        d_dcs = sem("d_dcs")
        NTH = LH // TT
        for jj in range(NTH):
            pf = jj % 2
            for t4 in range(4):
                tbi = jj * 4 + t4
                pp = t4 % 2
                if v_xn2.n < 2:
                    sy.wait_ge(v_s2.sem, 32)
                    sy.wait_ge(s_q2.sem, 16)
                else:
                    sy.wait_ge(v_xn2.sem, v_xn2.n - 1)
                sy.dma_start(out=xm_t[:, pp * DM:(pp + 1) * DM],
                             in_=x_my[tbi * 128:(tbi + 1) * 128, :]
                             ).then_inc(d_xm[pp].sem, 16)
                d_xm[pp].n += 16
                ve.wait_ge(d_xm[pp].sem, d_xm[pp].n)
                ve.wait_ge(s_st2.sem, 1)
                if v_xn2.n >= 2:
                    ve.wait_ge(t_tr2.sem, 2 * (v_xn2.n - 1))
                v_xn2.inc(ve.tensor_scalar(
                    xn2_t[:, pp * DM:(pp + 1) * DM],
                    xm_t[:, pp * DM:(pp + 1) * DM],
                    m2_st[:, tbi:tbi + 1], r2_st[:, tbi:tbi + 1],
                    op0=ALU.subtract, op1=ALU.mult))
                te.wait_ge(v_xn2.sem, v_xn2.n)
                if t_tr2.n >= 2:
                    te.wait_ge(s_ek2.sem, t_tr2.n - 1)
                for q in range(8):
                    ins = te.transpose(
                        ps_rep[:, pp * TT + ((q % 4) * 128):
                               pp * TT + ((q % 4) + 1) * 128].bitcast(F32R)
                        if q < 4 else
                        ps_rep[:, pp * TT + ((q - 4) * 128):
                               pp * TT + ((q - 4) + 1) * 128].bitcast(F32R),
                        xn2_t[:, pp * DM + q * 128:pp * DM + (q + 1) * 128],
                        ident_r[:])
                    if q == 3 or q == 7:
                        t_tr2.inc(ins)
                    if q == 3:
                        sc.wait_ge(t_tr2.sem, t_tr2.n)
                        if jj >= 1 and t4 == 0:
                            sc.wait_ge(t_enc.sem, jj * 16)
                        for q2 in range(4):
                            ins2 = sc.activation(
                                xnf_bf[:, q2 * TT + t4 * 128:
                                       q2 * TT + (t4 + 1) * 128],
                                ps_rep[:, pp * TT + q2 * 128:
                                       pp * TT + (q2 + 1) * 128],
                                AF.Copy)
                        s_ek2.inc(ins2)
                        te.wait_ge(s_ek2.sem, s_ek2.n)
                sc.wait_ge(t_tr2.sem, t_tr2.n)
                if jj >= 1 and t4 == 0:
                    sc.wait_ge(t_enc.sem, jj * 16)
                for q2 in range(4):
                    ins2 = sc.activation(
                        xnf_bf[:, (4 + q2) * TT + t4 * 128:
                               (4 + q2) * TT + (t4 + 1) * 128],
                        ps_rep[:, pp * TT + q2 * 128:
                               pp * TT + (q2 + 1) * 128],
                        AF.Copy)
                s_ek2.inc(ins2)
            te.wait_ge(s_ek2.sem, s_ek2.n)
            te.wait_ge(d_w.sem, n_w_ffn)
            for m in range(16):
                pb = m % 2
                if m == 0 and jj >= 1:
                    te.wait_ge(s_dec.sem, 8 * jj)
                if jj * 16 + m >= 2:
                    te.wait_ge(s_spk.sem, jj * 16 + m - 1)
                for k in range(8):
                    ins = te.matmul(
                        ps_a[:, pb * TT:(pb + 1) * TT],
                        enc_bf[:, k * HID + m * 128:k * HID + (m + 1) * 128],
                        xnf_bf[:, k * TT:(k + 1) * TT],
                        start=(k == 0), stop=(k == 7))
                t_enc.inc(ins)
                sc.wait_ge(t_enc.sem, t_enc.n)
                if jj >= 1 and m == 0:
                    sc.wait_ge(t_s2c.sem, jj)
                s_spk.inc(sc.activation(spk_bf[:, m * TT:(m + 1) * TT],
                                        ps_a[:, pb * TT:(pb + 1) * TT],
                                        AF.Sigmoid,
                                        bias=encb_c[:, m:m + 1]))
            if DEBUG_SCRATCH:
                sy.wait_ge(s_spk.sem, s_spk.n)
                sy.dma_start(out=spk_sp[:, jj * 16 * TT:(jj + 1) * 16 * TT],
                             in_=spk_bf[:]).then_inc(d_cmb.sem, 16)
                sy.wait_ge(s_ek2.sem, s_ek2.n)
                sy.dma_start(out=xnf_sp[:, jj * 8 * TT:(jj + 1) * 8 * TT],
                             in_=xnf_bf[:]).then_inc(d_cmb.sem, 16)
                d_cmb.n += 32
            te.wait_ge(s_spk.sem, s_spk.n)
            if jj >= 1:
                te.wait_ge(v_lnf.sem, 2 * jj)
            for k in range(16):
                ins = te.matmul(ps_r[0:64, 0:TT],
                                s2c_bf[:, k * 64:(k + 1) * 64],
                                spk_bf[:, k * TT:(k + 1) * TT],
                                start=(k == 0), stop=(k == 15))
            t_s2c.inc(ins)
            sc.wait_ge(t_s2c.sem, t_s2c.n)
            if jj >= 2:
                sc.wait_ge(t_eh.sem, (jj - 1) * 64)
            s_cnt.inc(sc.activation(cont_bf[:, pf * TT:(pf + 1) * TT],
                                    ps_r[0:64, 0:TT], AF.Identity,
                                    bias=s2cb_c[:]))
            te.wait_ge(s_cnt.sem, s_cnt.n)
            if jj >= 1:
                te.wait_ge(v_lnf.sem, 2 * jj)
            if jj >= 1:
                te.wait_ge(s_rt.sem, 2 * jj)
            t_rt.inc(te.matmul(ps_r[0:64, TT:2 * TT], r1_bf[:],
                               cont_bf[:, pf * TT:(pf + 1) * TT],
                               start=True, stop=True))
            sc.wait_ge(t_rt.sem, t_rt.n)
            if jj >= 2:
                sc.wait_ge(t_rt.sem, t_rt.n)
            s_rt.inc(sc.activation(rh_bf[:, pf * TT:(pf + 1) * TT],
                                   ps_r[0:64, TT:2 * TT], AF.Relu,
                                   bias=rb1_c[:]))
            te.wait_ge(s_rt.sem, s_rt.n)
            t_rt.inc(te.matmul(ps_r[0:8, TT:2 * TT], r2_bf[:],
                               rh_bf[:, pf * TT:(pf + 1) * TT],
                               start=True, stop=True))
            sc.wait_ge(t_rt.sem, t_rt.n)
            if jj >= 2:
                sc.wait_ge(t_rt.sem, 10 * (jj - 1))
            s_rt.inc(sc.activation(lgits[:, pf * TT:(pf + 1) * TT],
                                   ps_r[0:8, TT:2 * TT], AF.Identity,
                                   bias=rb2_c[:]))
            if DEBUG_SCRATCH:
                sy.wait_ge(s_rt.sem, s_rt.n)
                sy.dma_start(out=lg_sp[0:8, jj * TT:(jj + 1) * TT],
                             in_=lgits[:, pf * TT:(pf + 1) * TT]
                             ).then_inc(d_cmb.sem, 16)
                d_cmb.n += 16
                sy.wait_ge(s_cnt.sem, s_cnt.n)
                sy.dma_start(out=cont_sp[0:64, jj * TT:(jj + 1) * TT],
                             in_=cont_bf[:, pf * TT:(pf + 1) * TT]
                             ).then_inc(d_cmb.sem, 16)
                d_cmb.n += 16
            for t4 in range(4):
                it4 = jj * 4 + t4
                te.wait_ge(s_rt.sem, s_rt.n)
                if t4 == 0 and jj >= 1:
                    te.wait_ge(v_moe.sem, NE * jj)
                if t4 == 0 and jj >= 2:
                    te.wait_ge(s_cmb.sem, 2 * (jj - 1))
                if t4 >= 1:
                    te.wait_ge(v_rt.sem, v_rt.n)
                t_rt.inc(te.transpose(
                    ps_eo[:, t4 * 8:(t4 + 1) * 8],
                    lgits[:, pf * TT + t4 * 128:pf * TT + (t4 + 1) * 128],
                    ident_f[0:8, 0:8]))
                ve.wait_ge(t_rt.sem, t_rt.n)
                if t4 == 0 and jj >= 1:
                    ve.wait_ge(s_cmb.sem, 2 * jj - 1)
                v_rt.inc(ve.tensor_copy(lg_sb[:, t4 * 8:(t4 + 1) * 8],
                                        ps_eo[:, t4 * 8:(t4 + 1) * 8]))
            for t4 in range(4):
                ve.reduce_max(rt_s[:, t4 * 4:t4 * 4 + 1],
                              lg_sb[:, t4 * 8:(t4 + 1) * 8],
                              axis=mybir.AxisListType.X)
            for t4 in range(4):
                ve.tensor_scalar(rt_m[:, t4 * 8:(t4 + 1) * 8],
                                 lg_sb[:, t4 * 8:(t4 + 1) * 8],
                                 rt_s[:, t4 * 4:t4 * 4 + 1], None,
                                 op0=ALU.is_equal)
            for t4 in range(4):
                ve.scalar_tensor_tensor(rt_g[:, t4 * 8:(t4 + 1) * 8],
                                        rt_m[:, t4 * 8:(t4 + 1) * 8],
                                        -1e30,
                                        lg_sb[:, t4 * 8:(t4 + 1) * 8],
                                        op0=ALU.mult, op1=ALU.add)
            for t4 in range(4):
                ve.reduce_max(rt_s[:, t4 * 4 + 1:t4 * 4 + 2],
                              rt_g[:, t4 * 8:(t4 + 1) * 8],
                              axis=mybir.AxisListType.X)
            for t4 in range(4):
                ve.tensor_tensor(rt_s[:, t4 * 4 + 2:t4 * 4 + 3],
                                 rt_s[:, t4 * 4:t4 * 4 + 1],
                                 rt_s[:, t4 * 4 + 1:t4 * 4 + 2],
                                 op=ALU.add)
            for t4 in range(4):
                ins = ve.tensor_scalar(rt_s[:, t4 * 4 + 3:t4 * 4 + 4],
                                       rt_s[:, t4 * 4 + 2:t4 * 4 + 3],
                                       -1.0, None, op0=ALU.mult)
            v_rt.inc(ins)
            sc.wait_ge(v_rt.sem, v_rt.n)
            for t4 in range(4):
                ins = sc.activation(rt_m[:, t4 * 8:(t4 + 1) * 8],
                                    lg_sb[:, t4 * 8:(t4 + 1) * 8],
                                    AF.Sigmoid, scale=2.0,
                                    bias=rt_s[:, t4 * 4 + 3:t4 * 4 + 4])
            s_cmb.inc(ins)
            ve.wait_ge(s_cmb.sem, s_cmb.n)
            for t4 in range(4):
                ve.tensor_scalar(rt_g[:, t4 * 8:(t4 + 1) * 8],
                                 lg_sb[:, t4 * 8:(t4 + 1) * 8],
                                 rt_s[:, t4 * 4 + 1:t4 * 4 + 2], None,
                                 op0=ALU.is_ge)
            for t4 in range(4):
                ins = ve.tensor_tensor(lg_sb[:, t4 * 8:(t4 + 1) * 8],
                                       rt_m[:, t4 * 8:(t4 + 1) * 8],
                                       rt_g[:, t4 * 8:(t4 + 1) * 8],
                                       op=ALU.mult)
            v_rt.inc(ins)
            te.wait_ge(v_rt.sem, v_rt.n)
            if jj >= 1:
                te.wait_ge(s_cmb.sem, 2 * jj)
            for t4 in range(4):
                ins = te.transpose(ps_cr[0:8, t4 * 128:(t4 + 1) * 128],
                                   lg_sb[:, t4 * 8:(t4 + 1) * 8],
                                   ident_f[:])
            t_rt.inc(ins)
            sc.wait_ge(t_rt.sem, t_rt.n)
            for t4 in range(4):
                if jj >= 2 and t4 == 0:
                    sc.wait_ge(d_cmb.sem, (jj - 1) * 16)
                ins = sc.activation(
                    comb[:, pf * TT + t4 * 128:pf * TT + (t4 + 1) * 128],
                    ps_cr[0:8, t4 * 128:(t4 + 1) * 128], AF.Copy)
            s_cmb.inc(ins)
            sy.wait_ge(s_cmb.sem, s_cmb.n)
            sy.dma_start(out=comb_sp[0:8, jj * TT:(jj + 1) * TT],
                         in_=comb[:, pf * TT:(pf + 1) * TT]
                         ).then_inc(d_cmb.sem, 16)
            d_cmb.n += 16
            for e in range(NE):
                pe2 = e % 2
                ie = jj * NE + e
                if ie >= 2:
                    sy.wait_ge(v_moe.sem, ie - 1)
                sy.wait_ge(d_cmb.sem, d_cmb.n)
                sy.dma_start(out=crep_sb[:, pe2 * TT:(pe2 + 1) * TT],
                             in_=comb_sp[e, jj * TT:(jj + 1) * TT
                                         ].partition_broadcast(64)
                             ).then_inc(d_crp.sem, 16)
                d_crp.n += 16
                for m in range(8):
                    pm = m % 2
                    iem = (jj * NE + e) * 8 + m
                    te.wait_ge(s_cnt.sem, s_cnt.n)
                    if iem >= 2:
                        te.wait_ge(s_eh.sem, iem - 1)
                    ins = te.matmul(
                        ps_a[:, pm * TT:(pm + 1) * TT],
                        ew1_bf[:, e * 1024 + m * 128:
                               e * 1024 + (m + 1) * 128],
                        cont_bf[:, pf * TT:(pf + 1) * TT],
                        start=True, stop=True)
                    t_eh.inc(ins)
                    sc.wait_ge(t_eh.sem, t_eh.n)
                    s_eh.inc(sc.activation(
                        eh_bf[:, pm * TT:(pm + 1) * TT],
                        ps_a[:, pm * TT:(pm + 1) * TT], AF.Relu,
                        bias=eb1_c[:, e * 8 + m:e * 8 + m + 1]))
                    te.wait_ge(s_eh.sem, s_eh.n)
                    if m == 0 and (jj * NE + e) >= 1:
                        te.wait_ge(v_moe.sem, jj * NE + e)
                    ins = te.matmul(
                        ps_eo[0:64, 0:TT],
                        ew2_bf[:, (e * 8 + m) * 64:(e * 8 + m + 1) * 64],
                        eh_bf[:, pm * TT:(pm + 1) * TT],
                        start=(m == 0), stop=(m == 7))
                    if m == 7:
                        t_eo.inc(ins)
                ve.wait_ge(d_crp.sem, d_crp.n)
                ve.wait_ge(t_eo.sem, t_eo.n)
                ve.scalar_tensor_tensor(pe_sb[:, pf * TT:(pf + 1) * TT],
                                        ps_eo[0:64, 0:TT],
                                        eb2_c[:, e:e + 1],
                                        crep_sb[:, pe2 * TT:(pe2 + 1) * TT],
                                        op0=ALU.add, op1=ALU.mult)
                if e == 0 and jj >= 2:
                    ve.wait_ge(s_mbf.sem, jj - 1)
                if e == 0:
                    v_moe.inc(ve.tensor_copy(
                        moe_f[:, pf * TT:(pf + 1) * TT],
                        pe_sb[:, pf * TT:(pf + 1) * TT]))
                else:
                    v_moe.inc(ve.tensor_tensor(
                        moe_f[:, pf * TT:(pf + 1) * TT],
                        moe_f[:, pf * TT:(pf + 1) * TT],
                        pe_sb[:, pf * TT:(pf + 1) * TT], op=ALU.add))
            sc.wait_ge(v_moe.sem, v_moe.n)
            s_mbf.inc(sc.activation(moe_bf[:, pf * TT:(pf + 1) * TT],
                                    moe_f[:, pf * TT:(pf + 1) * TT],
                                    AF.Copy))
            te.wait_ge(s_mbf.sem, s_mbf.n)
            for m in range(16):
                pb = m % 2
                if jj * 16 + m >= 2:
                    te.wait_ge(s_rat.sem, jj * 16 + m - 1)
                t_c2s.inc(te.matmul(ps_a[:, pb * TT:(pb + 1) * TT],
                                    c2s_bf[:, m * 128:(m + 1) * 128],
                                    moe_bf[:, pf * TT:(pf + 1) * TT],
                                    start=True, stop=True))
                sc.wait_ge(t_c2s.sem, t_c2s.n)
                if jj >= 1 and m == 0:
                    sc.wait_ge(t_dec.sem, jj * 8)
                s_rat.inc(sc.activation(rat_bf[:, m * TT:(m + 1) * TT],
                                        ps_a[:, pb * TT:(pb + 1) * TT],
                                        AF.Sigmoid,
                                        bias=c2sb_c[:, m:m + 1]))
            te.wait_ge(s_rat.sem, s_rat.n)
            for m in range(8):
                pb = m % 2
                if jj * 8 + m >= 2:
                    te.wait_ge(s_dec.sem, jj * 8 + m - 1)
                for k in range(16):
                    ins = te.matmul(
                        ps_a[:, pb * TT:(pb + 1) * TT],
                        dec_bf[:, k * 1024 + m * 128:
                               k * 1024 + (m + 1) * 128],
                        rat_bf[:, k * TT:(k + 1) * TT],
                        start=(k == 0), stop=(k == 15))
                t_dec.inc(ins)
                sc.wait_ge(t_dec.sem, t_dec.n)
                if jj * 8 + m >= 2:
                    sc.wait_ge(d_dcs.sem, d_dcs.n)
                    sc.wait_ge(t_sum.sem, t_sum.n)
                s_dec.inc(sc.activation(dec_f[:, pb * TT:(pb + 1) * TT],
                                        ps_a[:, pb * TT:(pb + 1) * TT],
                                        AF.Sigmoid,
                                        bias=decb_c[:, m:m + 1]))
                sy.wait_ge(s_dec.sem, s_dec.n)
                sy.dma_start(out=dec_sp[:, m * LH + jj * TT:
                                        m * LH + (jj + 1) * TT],
                             in_=dec_f[:, pb * TT:(pb + 1) * TT]
                             ).then_inc(d_dcs.sem, 16)
                d_dcs.n += 16
                s_dcr.inc(sc.activation(dec_r[:, pb * TT:(pb + 1) * TT],
                                        dec_f[:, pb * TT:(pb + 1) * TT],
                                        AF.Copy))
                te.wait_ge(s_dcr.sem, s_dcr.n)
                t_sum.inc(te.matmul(ps_r[:, 0:TT], ones_r[:, 0:128],
                                    dec_r[:, pb * TT:(pb + 1) * TT],
                                    start=(m == 0), stop=(m == 7)))
                sc.wait_ge(t_sum.sem, t_sum.n)
                s_dcr.inc(sc.activation(dec_r[:, pb * TT:(pb + 1) * TT],
                                        dec_f[:, pb * TT:(pb + 1) * TT],
                                        AF.Square))
                te.wait_ge(s_dcr.sem, s_dcr.n)
                t_sum.inc(te.matmul(ps_r[:, TT:2 * TT], ones_r[:, 0:128],
                                    dec_r[:, pb * TT:(pb + 1) * TT],
                                    start=(m == 0), stop=(m == 7)))
            ve.wait_ge(t_sum.sem, t_sum.n)
            if jj >= 1:
                ve.wait_ge(d_st.sem, d_st.n)
            v_lnf.inc(ve.tensor_copy(st_row[0:1, 0:TT], ps_r[0:1, 0:TT]))
            v_lnf.inc(ve.tensor_copy(st_row[0:1, TT:2 * TT],
                                     ps_r[0:1, TT:2 * TT]))
            sy.wait_ge(v_lnf.sem, v_lnf.n)
            sy.dma_start(out=st2_sp[0, jj * TT:(jj + 1) * TT],
                         in_=st_row[0:1, 0:TT]).then_inc(d_st.sem, 16)
            d_st.n += 16
            sy.dma_start(out=st2_sp[1, jj * TT:(jj + 1) * TT],
                         in_=st_row[0:1, TT:2 * TT]).then_inc(d_st.sem, 16)
            d_st.n += 16
        _es5.close()
        phase_barrier()
        _es7 = ExitStack()
        g_stg = _es7.enter_context(nc.sbuf_tensor("g_stg", [128, 2 * 4 * TT], F32))
        g_snd = _es7.enter_context(nc.sbuf_tensor("g_snd", [128, 4 * LH], F32))
        g_rcv = _es7.enter_context(nc.sbuf_tensor("g_rcv", [128, 4 * LH], F32))
        dec_f = _es7.enter_context(nc.sbuf_tensor("dec_fg", [128, 2 * TT], F32))
        wk_g = _es7.enter_context(nc.sbuf_tensor("wk_g", [128, 2 * TT], F32))
        ps_rep = _es7.enter_context(nc.psum_tensor("ps_rep2", [128, 2 * TT], F32))
        st2_f = _es7.enter_context(nc.sbuf_tensor("st2_f", [1, 2 * LH], F32))
        mr_r = _es7.enter_context(nc.sbuf_tensor("mr_r", [1, 2 * LH], F32R))
        v_ln2 = sem("v_ln2")
        d_st2 = d_dt
        scr_row = _es7.enter_context(nc.sbuf_tensor("scr_row", [1, LH], F32))
        scr2_row = _es7.enter_context(nc.sbuf_tensor("scr2_row", [1, LH], F32))
        gp.wait_ge(d_st.sem, d_st.n)
        gp.dma_start(out=st2_f[0:1, 0:LH], in_=st2_sp[0, :]
                     ).then_inc(d_st2.sem, 16)
        gp.dma_start(out=st2_f[0:1, LH:2 * LH], in_=st2_sp[1, :]
                     ).then_inc(d_st2.sem, 16)
        d_st2.n += 32
        sc.wait_ge(d_st2.sem, d_st2.n)
        s_ln2 = sem("s_ln2")
        s_ln2.inc(sc.activation(scr_row[:], st2_f[0:1, 0:LH], AF.Copy,
                                scale=1.0 / DM))
        s_ln2.inc(sc.activation(st2_f[0:1, 0:LH], scr_row[:], AF.Square))
        s_ln2.inc(sc.activation(scr2_row[:], st2_f[0:1, LH:2 * LH], AF.Copy,
                                scale=1.0 / DM))
        gp.wait_ge(s_ln2.sem, 3)
        g_cmp.inc(gp.tensor_tensor(st2_f[0:1, LH:2 * LH], scr2_row[:],
                                   st2_f[0:1, 0:LH], op=ALU.subtract))
        sc.wait_ge(g_cmp.sem, g_cmp.n)
        sc.activation(st2_f[0:1, 0:LH], st2_f[0:1, LH:2 * LH],
                      AF.Ln, bias=cst[0:1, 0:1])
        s_ln2.inc(sc.activation(scr2_row[:], st2_f[0:1, 0:LH], AF.Exp,
                                scale=-0.5))
        ve.wait_ge(s_ln2.sem, 4)
        v_ln2.inc(ve.tensor_copy(mr_r[0:1, 0:LH], scr_row[:]))
        v_ln2.inc(ve.tensor_copy(mr_r[0:1, LH:2 * LH], scr2_row[:]))
        _rsend(gp, prep, None, None, rs_rdy, ls_loc, 2)
        d_dcr = d_xa
        t_rep = sem("t_rep")
        v_g1 = sem("v_g1")
        s_g = sem("s_g")
        for jj in range(NTH):
            pf = jj % 2
            te.wait_ge(v_ln2.sem, 2)
            te.wait_ge(v_one.sem, 2)
            if jj >= 1:
                te.wait_ge(v_g1.sem, jj * 8)
            te.matmul(ps_rep[:, 0:TT], ones_r[0:1, :],
                      mr_r[0:1, jj * TT:(jj + 1) * TT],
                      start=True, stop=True)
            t_rep.inc(te.matmul(ps_rep[:, TT:2 * TT], ones_r[0:1, :],
                                mr_r[0:1, LH + jj * TT:LH + (jj + 1) * TT],
                                start=True, stop=True))
            for q in range(8):
                iq = jj * 8 + q
                pq = iq % 2
                if iq >= 2:
                    sy.wait_ge(v_g1.sem, iq - 1)
                sy.dma_start(out=dec_f[:, pq * TT:(pq + 1) * TT],
                             in_=dec_sp[:, q * LH + jj * TT:
                                        q * LH + (jj + 1) * TT]
                             ).then_inc(d_dcr[pq].sem, 16)
                d_dcr[pq].n += 16
                ve.wait_ge(d_dcr[pq].sem, d_dcr[pq].n)
                ve.wait_ge(t_rep.sem, t_rep.n)
                if iq >= 2:
                    ve.wait_ge(s_g.sem, iq - 1)
                ve.tensor_tensor(wk_g[:, pq * TT:(pq + 1) * TT],
                                 dec_f[:, pq * TT:(pq + 1) * TT],
                                 ps_rep[:, 0:TT], op=ALU.subtract)
                v_g1.inc(ve.tensor_tensor(
                    wk_g[:, pq * TT:(pq + 1) * TT],
                    wk_g[:, pq * TT:(pq + 1) * TT],
                    ps_rep[:, TT:2 * TT], op=ALU.mult))
                sc.wait_ge(v_g1.sem, v_g1.n)
                if q < 4:
                    dst = g_stg[:, (pf * 4 + q) * TT:(pf * 4 + q + 1) * TT]
                    if jj >= 2 and q == 0:
                        sc.wait_ge(d_gsp.sem, (jj - 1) * 16)
                else:
                    dst = g_snd[:, (q - 4) * LH + jj * TT:
                                (q - 4) * LH + (jj + 1) * TT]
                s_g.inc(sc.activation(dst,
                                      wk_g[:, pq * TT:(pq + 1) * TT],
                                      AF.Silu, scale=lng_c[:, q:q + 1],
                                      bias=lnb_c[:, q:q + 1]))
            sy.wait_ge(s_g.sem, s_g.n)
            sy.dma_start(
                out=g_my_sp.rearrange("p (q l) -> p q l", q=4
                                      )[:, :, jj * TT:(jj + 1) * TT],
                in_=g_stg[:, pf * 4 * TT:(pf + 1) * 4 * TT
                          ].rearrange("p (q t) -> p q t", q=4)
            ).then_inc(d_gsp.sem, 16)
            d_gsp.n += 16
        gp.wait_ge(s_g.sem, s_g.n)
        gp.wait_ge(rs_rdy.sem, 4)
        _rsend(gp, prep, g_rcv[:], g_snd[:], rs_ga, ls_loc, 2)
        gp.wait_ge(rs_ga.sem, 2)
        gp.dma_start(out=g_pr_sp[:, :], in_=g_rcv[:]).then_inc(d_gsp.sem, 16)
        d_gsp.n += 16
        gp.wait_ge(ls_loc.sem, ls_loc.n)

        _es7.close()
        phase_barrier()
        _es6 = ExitStack()
        g_all = _es6.enter_context(nc.sbuf_tensor("g_all", [128, 4 * L], F32))
        x_f = _es6.enter_context(nc.sbuf_tensor("x_f", [128, 2 * 512], F32))
        xn_f = _es6.enter_context(nc.sbuf_tensor("xn_f", [128, 2 * 512], F32))
        mm_in = _es6.enter_context(nc.sbuf_tensor("mm_in", [128, 2 * 512], F32))
        s_f = _es6.enter_context(nc.sbuf_tensor("s_f", [128, 2 * 512], F32))
        o_f = _es6.enter_context(nc.sbuf_tensor("o_f", [128, 2 * 512], F32))
        ps_mt = _es6.enter_context(nc.psum_tensor("ps_mt", [128, 2 * 512], F32))
        ps_gt = _es6.enter_context(nc.psum_tensor("ps_gt", [128, 2 * 512], F32))
        sg_f = _es6.enter_context(nc.sbuf_tensor("sg_f", [128, 2 * 512], F32))
        d_ga = sem("d_ga")
        gp.wait_ge(d_gsp.sem, d_gsp.n)
        v_tb = gp.value_load(tb_sb.ap()[0:1, 0:1])
        v_pb = gp.value_load(tb_sb.ap()[0:1, 1:2])
        g3 = g_all.ap().rearrange("p (q l) -> p q l", q=4)
        gp.dma_start(out=g3[:, :, bass.ds(v_tb, LH)],
                     in_=g_my_sp.rearrange("p (q l) -> p q l", q=4)
                     ).then_inc(d_ga.sem, 16)
        d_ga.n += 16
        gp.dma_start(out=g3[:, :, bass.ds(v_pb, LH)],
                     in_=g_pr_sp.rearrange("p (q l) -> p q l", q=4)
                     ).then_inc(d_ga.sem, 16)
        d_ga.n += 16
        d_xf = d_xa[0]
        d_mf = d_xa[1]
        v_xn = v_st
        t_tf = s_sq
        vo_b = v_st2.n
        v_o = v_st2
        ttf_b = t_tf.n
        for tbi in range(32):
            p = tbi % 2
            if tbi >= 2:
                sy.wait_ge(v_o.sem, (tbi - 1) * 3)
            sy.dma_start(out=x_f[:, p * 512:(p + 1) * 512],
                         in_=x_true[tbi * 128:(tbi + 1) * 128, 0:512]
                         ).then_inc(d_xf.sem, 16)
            d_xf.n += 16
            sy.wait_ge(d_mm.sem, d_mm.n)
            sy.dma_start(out=mm_in[:, p * 512:(p + 1) * 512],
                         in_=mm_sp.rearrange("p (q l) -> p q l", q=4
                                             )[:, :, tbi * 128:(tbi + 1) * 128]
                         ).then_inc(d_mf.sem, 16)
            d_mf.n += 16
            ve.wait_ge(d_xf.sem, d_xf.n)
            if tbi >= 2:
                ve.wait_ge(t_tf.sem, ttf_b + tbi - 1)
            v_xn.inc(ve.tensor_scalar(xn_f[:, p * 512:(p + 1) * 512],
                                      x_f[:, p * 512:(p + 1) * 512],
                                      m_st[:, tbi:tbi + 1],
                                      r_st[:, tbi:tbi + 1],
                                      op0=ALU.subtract, op1=ALU.mult))
            te.wait_ge(d_mf.sem, d_mf.n)
            te.wait_ge(d_ga.sem, 32)
            if tbi >= 2:
                te.wait_ge(v_o.sem, vo_b + (tbi - 1) * 3)
            for q in range(4):
                te.transpose(
                    ps_mt[:, p * 512 + q * 128:p * 512 + (q + 1) * 128],
                    mm_in[:, p * 512 + q * 128:p * 512 + (q + 1) * 128],
                    ident_f[:])
                ins = te.transpose(
                    ps_gt[:, p * 512 + q * 128:p * 512 + (q + 1) * 128],
                    g_all[:, q * L + tbi * 128:q * L + (tbi + 1) * 128],
                    ident_f[:])
            t_tf.inc(ins)
            ve.wait_ge(t_tf.sem, t_tf.n)
            ve.wait_ge(v_xn.sem, v_xn.n)
            v_o.inc(ve.tensor_tensor(sg_f[:, p * 512:(p + 1) * 512],
                                     xn_f[:, p * 512:(p + 1) * 512],
                                     ps_mt[:, p * 512:(p + 1) * 512],
                                     op=ALU.add))
            v_o.inc(ve.tensor_tensor(s_f[:, p * 512:(p + 1) * 512],
                                     sg_f[:, p * 512:(p + 1) * 512],
                                     ps_gt[:, p * 512:(p + 1) * 512],
                                     op=ALU.mult))
            if tbi >= 2:
                ve.wait_ge(d_out.sem, (tbi - 1) * 16)
            v_o.inc(ve.tensor_tensor(o_f[:, p * 512:(p + 1) * 512],
                                     s_f[:, p * 512:(p + 1) * 512],
                                     x_f[:, p * 512:(p + 1) * 512],
                                     op=ALU.add))
            sy.wait_ge(v_o.sem, v_o.n)
            sy.dma_start(out=out[tbi * 128:(tbi + 1) * 128, :],
                         in_=o_f[:, p * 512:(p + 1) * 512]
                         ).then_inc(d_out.sem, 16)
            d_out.n += 16
        gp.wait_ge(d_out.sem, d_out.n)
        gp.wait_ge(ls_loc.sem, ls_loc.n)

        _es6.close()
    nc.compile()
    return nc


def _prep_inputs(inputs):
    xw = {k: np.asarray(v) for k, v in inputs.items()}
    x = xw["L0_attn_x"]
    in_maps = []
    ident = np.eye(128, dtype=np.float32)

    def colmaj(v, nch):
        return np.ascontiguousarray(v.reshape(nch, 128).T)

    for c in range(8):
        b, h = c // 2, c % 2
        perm = (np.arange(1024) if h == 0
                else np.concatenate([np.arange(512, 1024), np.arange(512)]))
        ds = slice(h * 1024, (h + 1) * 1024)
        dsp = slice((1 - h) * 1024, (2 - h) * 1024)
        xb = np.ascontiguousarray(x[b][:, perm])
        inw = np.concatenate([xw["L0_attn_in_w"][perm][:, ds],
                              xw["L0_attn_in_w"][perm][:, 2048 + h * 1024:
                                                       2048 + (h + 1) * 1024]],
                             axis=1)
        m = {
            "x_true": xb,
            "x_my": np.ascontiguousarray(xb[h * 2048:(h + 1) * 2048]),
            "tb": np.array([[h * 2048, (1 - h) * 2048]], np.uint32),
            "ident_in": ident,
            "zeros4": np.zeros((128, 4), np.float32),
            "in_w": np.ascontiguousarray(
                inw.reshape(8, 128, 2048).transpose(1, 0, 2).reshape(128, -1)),
            "norm_w": colmaj(xw["L0_attn_norm_w"][perm], 8),
            "conv_w": np.ascontiguousarray(
                xw["L0_attn_conv_w"][ds, 0, :].reshape(8, 128, 4)
                .transpose(1, 0, 2).reshape(128, 32)),
            "conv_b": colmaj(xw["L0_attn_conv_b"][ds], 8),
            "xproj": np.ascontiguousarray(
                np.pad(xw["L0_attn_xproj_w"][ds], ((0, 0), (0, 32)))
                .reshape(8, 128, 128).transpose(1, 0, 2).reshape(128, -1)),
            "dt_w": np.ascontiguousarray(xw["L0_attn_dt_w"][:, ds]),
            "dt_b": colmaj(xw["L0_attn_dt_b"][ds], 8),
            "log_A": np.ascontiguousarray(
                xw["L0_attn_log_A"][ds].reshape(8, 128, 16)
                .transpose(1, 0, 2).reshape(128, 128)),
            "D": colmaj(xw["L0_attn_D_param"][ds], 8),
            "out_w": np.ascontiguousarray(
                np.concatenate([xw["L0_attn_out_w"][ds],
                                xw["L0_attn_out_w"][dsp]], axis=0
                               )[:, perm[:512]]
                .reshape(16, 128, 512).transpose(1, 0, 2).reshape(128, -1)),
            "enc_w": np.ascontiguousarray(
                xw["L0_ffn_enc_w"][perm].reshape(8, 128, 2048)
                .transpose(1, 0, 2).reshape(128, -1)),
            "enc_b": colmaj(xw["L0_ffn_enc_b"], 16),
            "s2c_w": np.ascontiguousarray(
                xw["L0_ffn_s2c_w"].reshape(16, 128, 64)
                .transpose(1, 0, 2).reshape(128, -1)),
            "s2c_b": np.ascontiguousarray(xw["L0_ffn_s2c_b"][:, None]),
            "r_w1": np.ascontiguousarray(xw["L0_ffn_r_w1"]),
            "r_b1": np.ascontiguousarray(xw["L0_ffn_r_b1"][:, None]),
            "r_w2": np.ascontiguousarray(xw["L0_ffn_r_w2"]),
            "r_b2": np.ascontiguousarray(xw["L0_ffn_r_b2"][:, None]),
            "e_w1": np.ascontiguousarray(
                xw["L0_ffn_e_w1"].transpose(1, 0, 2).reshape(64, -1)),
            "e_b1": np.ascontiguousarray(
                xw["L0_ffn_e_b1"].reshape(8, 8, 128)
                .transpose(2, 0, 1).reshape(128, 64)),
            "e_w2": np.ascontiguousarray(
                xw["L0_ffn_e_w2"].reshape(8, 8, 128, 64)
                .transpose(2, 0, 1, 3).reshape(128, -1)),
            "e_b2": np.ascontiguousarray(xw["L0_ffn_e_b2"].T),
            "c2s_w": np.ascontiguousarray(xw["L0_ffn_c2s_w"]),
            "c2s_b": colmaj(xw["L0_ffn_c2s_b"], 16),
            "dec_w": np.ascontiguousarray(
                xw["L0_ffn_dec_w"][:, perm].reshape(16, 128, 1024)
                .transpose(1, 0, 2).reshape(128, -1)),
            "dec_b": colmaj(xw["L0_ffn_dec_b"][perm], 8),
            "ln_g": colmaj(xw["L0_ffn_ln_g"][perm], 8),
            "ln_b": colmaj(xw["L0_ffn_ln_b"][perm], 8),
        }
        in_maps.append(m)
    return in_maps


def kernel(**inputs):
    in_maps = _prep_inputs(inputs)
    nc = build_program()
    res = run_bass_kernel_spmd(nc, in_maps, core_ids=list(range(8)))
    out = np.empty((4, L, DM), np.float32)
    for c in range(8):
        b, h = c // 2, c % 2
        out[b][:, h * 512:(h + 1) * 512] = res.results[c]["out"]
    return out


# version 06
# speedup vs baseline: 1.0444x; 1.0444x over previous
import numpy as np
import concourse.bass as bass
import concourse.mybir as mybir
from concourse.bacc import Bacc
from concourse.bass_utils import run_bass_kernel_spmd
from concourse import library_config
from contextlib import ExitStack

F32 = mybir.dt.float32
F32R = mybir.dt.float32r
BF16 = mybir.dt.bfloat16
U32 = mybir.dt.uint32
AF = mybir.ActivationFunctionType
ALU = mybir.AluOpType

DM = 1024
DI = 1024
DS = 16
L = 4096
LH = 2048
TT = 512
NT = L // TT
NCH = DI // 128
HID = 2048
NE = 8
RD = [(0, 1)] + [None] * 7
DEBUG_SCRATCH = False


class Ctr:
    __slots__ = ("sem", "n")

    def __init__(self, sem):
        self.sem = sem
        self.n = 0

    def inc(self, ins, k=1):
        ins.then_inc(self.sem, k)
        self.n += k
        return self.n


def _rsend(gp, prep, data_out, data_in, rsem, lsem, sim_inc):
    if data_out is None:
        ins = gp.remote_sem_update_broadcast(rsem.sem, lsem.sem, rdests=RD,
                                             sim_remote_sem_inc=sim_inc)
    else:
        ins = gp.remote_dma_broadcast(data_out, data_in, rsem.sem, lsem.sem,
                                      rdests=RD, sim_remote_sem_inc=sim_inc)
    lsem.n += 16
    prep.inc(ins)
    rsem.n += sim_inc
    gp.wait_ge(prep.sem, prep.n)
    gp.trigger_dma(count=1)


def build_program():
    nc = Bacc("TRN2", target_bir_lowering=False, debug=False)

    def din(name, shape, dt=F32):
        return nc.dram_tensor(name, shape, dt, kind="ExternalInput").ap()

    def dint(name, shape, dt=F32):
        kind = "ExternalOutput" if DEBUG_SCRATCH else "Internal"
        return nc.dram_tensor(name, shape, dt, kind=kind).ap()

    x_true = din("x_true", [L, DM])
    x_my = din("x_my", [LH, DM])
    tb = din("tb", [1, 2], U32)
    ident_in = din("ident_in", [128, 128])
    zeros4 = din("zeros4", [128, 4])
    in_w = din("in_w", [128, NCH * 2048])
    norm_w = din("norm_w", [128, 8])
    conv_w = din("conv_w", [128, NCH * 4])
    conv_b = din("conv_b", [128, 8])
    xproj = din("xproj", [128, NCH * 128])
    dt_w = din("dt_w", [64, DI])
    dt_b = din("dt_b", [128, 8])
    log_A = din("log_A", [128, 128])
    D_in = din("D", [128, 8])
    out_w = din("out_w", [128, 16 * 512])
    enc_w = din("enc_w", [128, 8 * HID])
    enc_b = din("enc_b", [128, 16])
    s2c_w = din("s2c_w", [128, 16 * 64])
    s2c_b = din("s2c_b", [64, 1])
    r_w1 = din("r_w1", [64, 64])
    r_b1 = din("r_b1", [64, 1])
    r_w2 = din("r_w2", [64, 8])
    r_b2 = din("r_b2", [8, 1])
    e_w1 = din("e_w1", [64, NE * 1024])
    e_b1 = din("e_b1", [128, NE * 8])
    e_w2 = din("e_w2", [128, NE * 8 * 64])
    e_b2 = din("e_b2", [64, NE])
    c2s_w = din("c2s_w", [64, HID])
    c2s_b = din("c2s_b", [128, 16])
    dec_w = din("dec_w", [128, 16 * 1024])
    dec_b = din("dec_b", [128, 8])
    ln_g = din("ln_g", [128, 8])
    ln_b = din("ln_b", [128, 8])
    out = nc.dram_tensor("out", [L, 512], F32, kind="ExternalOutput").ap()

    xss_sp = dint("xss_sp", [128, NCH * L])
    sz_sp = dint("sz_sp", [128, NCH * L])
    bc_sp = dint("bc_sp", [96, L])
    yg_my_sp = dint("yg_my_sp", [128, NT * NCH * TT], BF16)
    yg_pr_sp = dint("yg_pr_sp", [128, NT * NCH * TT], BF16)
    mm_sp = dint("mm_sp", [128, 4 * L])
    dec_sp = dint("dec_sp", [128, 8 * LH])
    st2_sp = dint("st2_sp", [2, LH])
    comb_sp = dint("comb_sp", [8, LH])
    lg_sp = dint("lg_sp", [8, LH]) if DEBUG_SCRATCH else None
    spk_sp = dint("spk_sp", [128, 16 * LH], BF16) if DEBUG_SCRATCH else None
    xnf_sp = dint("xnf_sp", [128, 8 * LH], BF16) if DEBUG_SCRATCH else None
    cont_sp = dint("cont_sp", [64, LH], BF16) if DEBUG_SCRATCH else None
    g_my_sp = dint("g_my_sp", [128, 4 * LH])
    g_pr_sp = dint("g_pr_sp", [128, 4 * LH])
    dbg_st = dint("dbg_st", [128, 3 * 32]) if DEBUG_SCRATCH else None
    dbg_st2 = dint("dbg_st2", [128, 64]) if DEBUG_SCRATCH else None
    dbg_xnn = dint("dbg_xnn", [128, NCH * TT]) if DEBUG_SCRATCH else None
    dbg_xnt = dint("dbg_xnt", [128, DM]) if DEBUG_SCRATCH else None
    dbg_xtb = dint("dbg_xtb", [128, DM]) if DEBUG_SCRATCH else None
    dbg_str = dint("dbg_str", [128, NCH * (TT + 4)]) if DEBUG_SCRATCH else None
    dbg_w = dint("dbg_w", [128, 2048]) if DEBUG_SCRATCH else None

    sy, sc, ve, te, gp = nc.sync, nc.scalar, nc.vector, nc.tensor, nc.gpsimd

    es = ExitStack()
    with es:
        def sbuf(name, shape, dt=F32):
            return es.enter_context(nc.sbuf_tensor(name, shape, dt))

        def sem(name):
            return Ctr(es.enter_context(nc.semaphore(name)))

        ident_r = sbuf("ident_r", [128, 128], F32R)
        ident_f = sbuf("ident_f", [128, 128])
        m_st = sbuf("m_st", [128, 32])
        r_st = sbuf("r_st", [128, 32])
        rc_st = sbuf("rc_st", [128, 32])
        sums = sbuf("sums", [128, 32])
        sqs = sbuf("sqs", [128, 32])
        A_neg = sbuf("A_neg", [128, 128])
        D_col = sbuf("D_col", [128, 8])
        cvb_col = sbuf("cvb_col", [128, 8])
        dtb_col = sbuf("dtb_col", [128, 8])
        cw_col = sbuf("cw_col", [128, NCH * 4])
        nw_col = sbuf("nw_col", [128, 8])
        state = sbuf("state", [128, 128])
        tb_sb = sbuf("tb_sb", [1, 2], U32)
        ones_r = sbuf("ones_r", [128, 128], F32R)
        cst = sbuf("cst", [128, 4], F32)
        st_row = sbuf("st_row", [1, 2 * TT], F32)
        lng_c = sbuf("lng_c", [128, 8], F32)
        lnb_c = sbuf("lnb_c", [128, 8], F32)
        ve.memset(cst[:, 0:1], 1e-5)
        ve.memset(cst[:, 1:2], 1e-6)
        ve.memset(cst[:, 2:3], 1.0)
        ve.memset(cst[:, 3:4], 1e-11)

        ph_b = sem("ph_b")
        bar_t = sbuf("bar_t", [1, 8], F32)

        def phase_barrier():
            ve.memset(bar_t[0:1, 0:1], 0.0).then_inc(ph_b.sem, 1)
            sc.activation(bar_t[0:1, 1:2], bar_t[0:1, 2:3], AF.Copy
                          ).then_inc(ph_b.sem, 1)
            gp.memset(bar_t[0:1, 3:4], 0.0).then_inc(ph_b.sem, 1)
            sy.sem_inc(ph_b.sem, 1)
            te.sem_inc(ph_b.sem, 1)
            ph_b.n += 5
            for eng in (sy, sc, ve, te, gp):
                eng.wait_ge(ph_b.sem, ph_b.n)

        rs_xp = sem("rs_xp")
        rs_yg = sem("rs_yg")
        rs_ga = sem("rs_ga")
        rs_rdy = sem("rs_rdy")
        ls_loc = sem("ls_loc")
        prep = sem("prep")
        ack_yg = sem("ack_yg")
        d_w = sem("d_w")
        d_out = sem("d_out")
        d_sp = sem("d_sp")

        gp.load_library(library_config.proxy)

        for t_, s_ in ((tb_sb, tb), (ident_f, ident_in), (A_neg, log_A),
                       (D_col, D_in), (cvb_col, conv_b), (dtb_col, dt_b),
                       (cw_col, conv_w), (nw_col, norm_w)):
            gp.dma_start(out=t_[:], in_=s_).then_inc(d_w.sem, 16)
            d_w.n += 16
        n_w0 = d_w.n
        s_p0 = sem("s_p0")
        A_tmp = sbuf("A_tmp", [128, 128], F32)
        sc.wait_ge(d_w.sem, n_w0)
        s_p0.inc(sc.activation(A_tmp[:], A_neg[:], AF.Exp))
        v_cmp = sem("v_cmp")
        ve.wait_ge(d_w.sem, n_w0)
        ve.tensor_copy(ident_r[:], ident_f[:])
        ve.memset(state[:], 0.0)
        ones_f = sbuf("ones_f", [128, 128], F32)
        v_one = sem("v_one")
        v_one.inc(ve.memset(ones_f[:], 0.0))
        sc.wait_ge(v_one.sem, 1)
        v_one.inc(sc.activation(ones_r[:], ones_f[:], AF.Copy, scale=0.0,
                                bias=1.0))
        ve.wait_ge(s_p0.sem, 1)
        ve.tensor_scalar(A_neg[:], A_tmp[:], -1.0, None, op0=ALU.mult)
        mask_i = sbuf("mask_i", [128, 128], mybir.dt.int32)
        maskf = sbuf("maskf", [128, 128])
        g_cmp = sem("g_cmp")
        gp.iota(mask_i[:], pattern=[[1, 128]], base=0, channel_multiplier=-1)
        g_cmp.inc(gp.tensor_copy(mask_i[:], mask_i[:]))
        ve.wait_ge(g_cmp.sem, g_cmp.n)
        ve.tensor_scalar(maskf[:], mask_i[:], 0, None, op0=ALU.is_equal)

        stats_done = sem("stats_done")
        _es1 = ExitStack()
        xt_a = _es1.enter_context(nc.sbuf_tensor("xt_a", [128, 2 * DM], F32))
        sq_scr = _es1.enter_context(nc.sbuf_tensor("sq_scr", [128, 2 * DM], F32))
        var_t = _es1.enter_context(nc.sbuf_tensor("var_t", [128, 32], F32))
        tmp_t = _es1.enter_context(nc.sbuf_tensor("tmp_t", [128, 32], F32))
        d_xa = [sem("d_xa0"), sem("d_xa1")]
        v_st = sem("v_st")
        s_sq = sem("s_sq")
        for tbi in range(32):
            p = tbi % 2
            if tbi >= 2:
                sy.wait_ge(v_st.sem, 2 * tbi - 3)
                sy.wait_ge(s_sq.sem, tbi - 1)
            sy.dma_start(out=xt_a[:, p * DM:(p + 1) * DM],
                         in_=x_true[tbi * 128:(tbi + 1) * 128, :]
                         ).then_inc(d_xa[p].sem, 16)
            d_xa[p].n += 16
            ve.wait_ge(d_xa[p].sem, d_xa[p].n)
            v_st.inc(ve.tensor_reduce(sums[:, tbi:tbi + 1],
                                      xt_a[:, p * DM:(p + 1) * DM],
                                      axis=mybir.AxisListType.X,
                                      op=ALU.add))
            sc.wait_ge(d_xa[p].sem, d_xa[p].n)
            if tbi >= 2:
                sc.wait_ge(v_st.sem, 2 * tbi - 2)
            s_sq.inc(sc.activation(sq_scr[:, p * DM:(p + 1) * DM],
                                   xt_a[:, p * DM:(p + 1) * DM], AF.Square))
            ve.wait_ge(s_sq.sem, tbi + 1)
            v_st.inc(ve.tensor_reduce(sqs[:, tbi:tbi + 1],
                                      sq_scr[:, p * DM:(p + 1) * DM],
                                      axis=mybir.AxisListType.X,
                                      op=ALU.add))
        ve.wait_ge(s_sq.sem, 32)
        sc.wait_ge(v_st.sem, 64)
        s_st = sem("s_st")
        s_st.inc(sc.activation(m_st[:], sums[:], AF.Copy, scale=1.0 / DM))
        s_st.inc(sc.activation(tmp_t[:], m_st[:], AF.Square))
        s_st.inc(sc.activation(var_t[:], sqs[:], AF.Copy, scale=1.0 / DM))
        gp.wait_ge(s_st.sem, 3)
        g_cmp.inc(gp.tensor_tensor(sums[:], var_t[:], tmp_t[:],
                                   op=ALU.subtract))
        v_st2 = sem("v_st2")
        sc.wait_ge(g_cmp.sem, g_cmp.n)
        v_st2.inc(sc.activation(tmp_t[:], sums[:], AF.Ln, bias=cst[:, 0:1]))
        s_st.inc(sc.activation(r_st[:], tmp_t[:], AF.Exp, scale=-0.5))
        sc.activation(tmp_t[:], sums[:], AF.Ln, scale=1.0 + 1e-6,
                      bias=cst[:, 3:4])
        s_st.inc(sc.activation(rc_st[:], tmp_t[:], AF.Exp, scale=-0.5))
        ve.wait_ge(s_st.sem, 5)
        stats_done.inc(ve.tensor_copy(sqs[:], rc_st[:]))

        _es1.close()
        phase_barrier()
        s_dtr = sem("s_dtr")
        d_bcs = sem("d_bcs")
        _es2 = ExitStack()
        w_stg = _es2.enter_context(nc.sbuf_tensor("w_stg", [128, 2048], F32))
        in_w_r = _es2.enter_context(nc.sbuf_tensor("in_w_r", [128, NCH * 2048], F32R))
        xproj_r = _es2.enter_context(nc.sbuf_tensor("xproj_r", [128, NCH * 128], F32))
        xt_b = _es2.enter_context(nc.sbuf_tensor("xt_b", [128, 2 * DM], F32))
        xnn_t = _es2.enter_context(nc.sbuf_tensor("xnn_t", [128, 2 * DM], F32R))
        xnn_fm = _es2.enter_context(nc.sbuf_tensor("xnn_fm", [128, NCH * TT], F32R))
        strips = _es2.enter_context(nc.sbuf_tensor("strips", [128, NCH * (TT + 4)], F32R))
        xss_f = _es2.enter_context(nc.sbuf_tensor("xss_f", [128, 2 * TT], F32))
        xss_r = _es2.enter_context(nc.sbuf_tensor("xss_r", [128, 2 * TT], F32R))
        sz_f = _es2.enter_context(nc.sbuf_tensor("sz_f", [128, 2 * TT], F32))
        xp_sb = _es2.enter_context(nc.sbuf_tensor("xp_sb", [128, L], F32))
        xp_pr = _es2.enter_context(nc.sbuf_tensor("xp_pr", [128, L], F32))
        diag_w = _es2.enter_context(
            nc.sbuf_tensor("diag_w", [128, NCH * 4 * 128], F32R))
        ps_xz = _es2.enter_context(nc.psum_tensor("ps_xz", [128, 2 * TT], F32))
        ps_tr = _es2.enter_context(nc.psum_tensor("ps_tr", [128, 2 * TT], F32))
        ps_cv = _es2.enter_context(nc.psum_tensor("ps_cv", [128, 2 * TT], F32))
        ps_xp = _es2.enter_context(nc.psum_tensor("ps_xp", [128, TT], F32))
        for c in range(NCH):
            for k in range(4):
                i = c * 4 + k
                v_cmp.inc(ve.tensor_scalar(diag_w[:, i * 128:(i + 1) * 128],
                                           maskf[:], cw_col[:, i:i + 1], None,
                                           op0=ALU.mult))
        s_wp = sem("s_wp")
        v_rnd = sem("v_rnd")
        for c in range(NCH):
            gp.wait_ge(s_wp.sem, c)
            gp.dma_start(out=w_stg[:], in_=in_w[:, c * 2048:(c + 1) * 2048]
                         ).then_inc(d_w.sem, 16)
            d_w.n += 16
            sc.wait_ge(d_w.sem, d_w.n)
            s_wp.inc(sc.activation(in_w_r[:, c * 2048:(c + 1) * 2048],
                                   w_stg[:], AF.Copy,
                                   scale=nw_col[:, c:c + 1]))
        gp.dma_start(out=xproj_r[:], in_=xproj).then_inc(d_w.sem, 16)
        d_w.n += 16
        n_w_p1x = d_w.n
        v_rnd.inc(ve.tensor_copy(bar_t[0:1, 4:5], bar_t[0:1, 5:6]))

        d_xb = d_xa
        v_xnn = sem("v_xnn")
        t_tr = sem("t_tr")
        s_evk = sem("s_evk")
        t_mz = sem("t_mz")
        s_ev2 = sem("s_ev2")
        v_f32r = sem("v_f32r")
        t_cv = sem("t_cv")
        s_cvs = sem("s_cvs")
        v_hl = sem("v_hl")
        t_xp = sem("t_xp")
        s_xpe = sem("s_xpe")

        d_z4 = sem("d_z4")
        for _c in range(NCH):
            gp.dma_start(out=strips[:, _c * (TT + 4):_c * (TT + 4) + 4],
                         in_=zeros4).then_inc(d_z4.sem, 16)
            d_z4.n += 16
        ev2_hist = []
        for j in range(NT):
            p = j % 2
            for t4 in range(4):
                tbi = j * 4 + t4
                pp = t4 % 2
                if tbi >= 2:
                    sy.wait_ge(v_xnn.sem, tbi - 1)
                sy.dma_start(out=xt_b[:, pp * DM:(pp + 1) * DM],
                             in_=x_true[tbi * 128:(tbi + 1) * 128, :]
                             ).then_inc(d_xb[pp].sem, 16)
                d_xb[pp].n += 16
                ve.wait_ge(d_xb[pp].sem, d_xb[pp].n)
                ve.wait_ge(stats_done.sem, 1)
                if tbi >= 2:
                    ve.wait_ge(t_tr.sem, tbi - 1)
                v_xnn.inc(ve.tensor_scalar(
                    xnn_t[:, pp * DM:(pp + 1) * DM],
                    xt_b[:, pp * DM:(pp + 1) * DM],
                    m_st[:, tbi:tbi + 1], rc_st[:, tbi:tbi + 1],
                    op0=ALU.subtract, op1=ALU.mult))
                if DEBUG_SCRATCH and tbi == 12:
                    sy.wait_ge(v_xnn.sem, 1)
                    sy.dma_start(out=dbg_xnt,
                                 in_=xnn_t[:, pp * DM:(pp + 1) * DM
                                           ].bitcast(F32)
                                 ).then_inc(d_bcs.sem, 16)
                    sy.dma_start(out=dbg_xtb, in_=xt_b[:, 0:DM]
                                 ).then_inc(d_bcs.sem, 16)
                    ve.wait_ge(d_bcs.sem, 32)
                te.wait_ge(v_xnn.sem, v_xnn.n)
                if tbi >= 1:
                    te.wait_ge(s_evk.sem, tbi)
                for q in range(8):
                    ins = te.transpose(
                        ps_tr[:, q * 128:(q + 1) * 128].bitcast(F32R),
                        xnn_t[:, pp * DM + q * 128:pp * DM + (q + 1) * 128],
                        ident_r[:])
                t_tr.inc(ins)
                sc.wait_ge(t_tr.sem, t_tr.n)
                if j >= 1 and t4 == 0:
                    sc.wait_ge(t_mz.sem, j * 16)
                for q in range(8):
                    ins = sc.activation(
                        xnn_fm[:, q * TT + t4 * 128:
                               q * TT + (t4 + 1) * 128],
                        ps_tr[:, q * 128:(q + 1) * 128],
                        AF.Copy)
                s_evk.inc(ins)
            te.wait_ge(s_evk.sem, s_evk.n)
            te.wait_ge(s_wp.sem, NCH)
            for m in range(16):
                pb = m % 2
                if len(ev2_hist) >= 2:
                    te.wait_ge(s_ev2.sem, ev2_hist[-2])
                for k in range(NCH):
                    ins = te.matmul(
                        ps_xz[:, pb * TT:(pb + 1) * TT],
                        in_w_r[:, k * 2048 + m * 128:k * 2048 + (m + 1) * 128],
                        xnn_fm[:, k * TT:(k + 1) * TT],
                        start=(k == 0), stop=(k == NCH - 1))
                t_mz.inc(ins)
                sc.wait_ge(t_mz.sem, t_mz.n)
                if m < 8:
                    c = m
                    if j > 0:
                        sc.wait_ge(v_hl.sem, (j - 1) * 8 + c + 1)
                        sc.wait_ge(t_cv.sem, (j - 1) * 8 + c + 1)
                    s_ev2.inc(sc.activation(
                        strips[:, c * (TT + 4) + 4:(c + 1) * (TT + 4)],
                        ps_xz[:, pb * TT:(pb + 1) * TT], AF.Copy))
                else:
                    c = m - 8
                    pz = c % 2
                    if j * 8 + c >= 2:
                        sc.wait_ge(d_sp.sem, d_sp.n)
                    s_ev2.inc(sc.activation(sz_f[:, pz * TT:(pz + 1) * TT],
                                            ps_xz[:, pb * TT:(pb + 1) * TT],
                                            AF.Silu))
                    sy.wait_ge(s_ev2.sem, s_ev2.n)
                    sy.dma_start(
                        out=sz_sp[:, c * L + j * TT:c * L + (j + 1) * TT],
                        in_=sz_f[:, pz * TT:(pz + 1) * TT]
                    ).then_inc(d_sp.sem, 16)
                    d_sp.n += 16
                ev2_hist.append(s_ev2.n)
            if DEBUG_SCRATCH and j == 3:
                d_dbg = d_bcs
                sy.wait_ge(s_ev2.sem, s_ev2.n)
                sy.dma_start(out=dbg_str, in_=strips[:].bitcast(F32)
                             ).then_inc(d_dbg.sem, 16)
                sy.wait_ge(s_wp.sem, NCH)
                sy.dma_start(out=dbg_w, in_=in_w_r[:, 0:2048].bitcast(F32)
                             ).then_inc(d_dbg.sem, 16)
                d_dbg.n += 32
                sy.wait_ge(s_evk.sem, 4)
                sy.dma_start(out=dbg_xnn,
                             in_=xnn_fm[:].bitcast(F32)).then_inc(d_dbg.sem, 16)
                sy.dma_start(out=dbg_st[:, 0:32], in_=m_st[:]).then_inc(d_dbg.sem, 16)
                sy.dma_start(out=dbg_st[:, 32:64], in_=rc_st[:]).then_inc(d_dbg.sem, 16)
                sy.dma_start(out=dbg_st[:, 64:96], in_=r_st[:]).then_inc(d_dbg.sem, 16)
                d_dbg.n += 64
                sc.wait_ge(d_dbg.sem, 64)
            for c in range(NCH):
                pc = c % 2
                te.wait_ge(s_ev2.sem, ev2_hist[j * 16 + c])
                te.wait_ge(d_z4.sem, d_z4.n)
                te.wait_ge(v_cmp.sem, 32)
                if j * 8 + c >= 2:
                    te.wait_ge(s_cvs.sem, j * 8 + c - 1)
                for k in range(4):
                    ins = te.matmul(
                        ps_cv[:, pc * TT:(pc + 1) * TT],
                        diag_w[:, (c * 4 + k) * 128:(c * 4 + k + 1) * 128],
                        strips[:, c * (TT + 4) + k + 1:
                               c * (TT + 4) + k + 1 + TT],
                        start=(k == 0), stop=(k == 3))
                t_cv.inc(ins)
                sc.wait_ge(t_cv.sem, t_cv.n)
                v_hl.inc(sc.activation(
                    strips[:, c * (TT + 4) + 1:c * (TT + 4) + 4],
                    strips.ap().bitcast(F32)[:, c * (TT + 4) + 513:
                                             c * (TT + 4) + 516],
                    AF.Copy))
                sc.wait_ge(t_cv.sem, t_cv.n)
                if j * 8 + c >= 2:
                    sc.wait_ge(d_sp.sem, d_sp.n)
                    sc.wait_ge(t_xp.sem, 8 * j + c - 1)
                s_cvs.inc(sc.activation(xss_f[:, pc * TT:(pc + 1) * TT],
                                        ps_cv[:, pc * TT:(pc + 1) * TT],
                                        AF.Silu, bias=cvb_col[:, c:c + 1]))
                sy.wait_ge(s_cvs.sem, s_cvs.n)
                sy.dma_start(
                    out=xss_sp[:, c * L + j * TT:c * L + (j + 1) * TT],
                    in_=xss_f[:, pc * TT:(pc + 1) * TT]
                ).then_inc(d_sp.sem, 16)
                d_sp.n += 16
                v_f32r.inc(ve.tensor_copy(bar_t[0:1, 4:5],
                                           bar_t[0:1, 5:6]))
                te.wait_ge(s_cvs.sem, s_cvs.n)
                te.wait_ge(d_w.sem, n_w_p1x)
                if c == 0 and j > 0:
                    te.wait_ge(s_xpe.sem, j)
                t_xp.inc(te.matmul(ps_xp[:],
                                   xproj_r[:, c * 128:(c + 1) * 128],
                                   xss_f[:, pc * TT:(pc + 1) * TT],
                                   start=(c == 0), stop=(c == NCH - 1)))
            sc.wait_ge(t_xp.sem, t_xp.n)
            s_xpe.inc(sc.activation(xp_sb[:, j * TT:(j + 1) * TT],
                                    ps_xp[:], AF.Copy))

        gp.wait_ge(s_xpe.sem, NT)
        _rsend(gp, prep, None, None, rs_rdy, ls_loc, 2)
        gp.wait_ge(rs_rdy.sem, 2)
        _rsend(gp, prep, xp_pr[:], xp_sb[:], rs_xp, ls_loc, 2)
        ve.wait_ge(rs_xp.sem, 2)
        v_xpadd = sem("v_xpadd")
        v_xpadd.inc(ve.tensor_tensor(xp_pr[:], xp_sb[:], xp_pr[:],
                                     op=ALU.add))
        sy.wait_ge(v_xpadd.sem, 1)
        sy.dma_start(out=bc_sp[:, :], in_=xp_pr[0:96, :]
                     ).then_inc(d_bcs.sem, 16)
        d_bcs.n += 16
        gp.wait_ge(ls_loc.sem, ls_loc.n)

        _es2.close()
        phase_barrier()
        d_ygs = sem("d_ygs")
        _es3 = ExitStack()
        Brep = _es3.enter_context(nc.sbuf_tensor("Brep", [128, DS * TT], F32))
        Crep = _es3.enter_context(nc.sbuf_tensor("Crep", [128, DS * TT], F32))
        del_c = _es3.enter_context(nc.sbuf_tensor("del_c", [128, 2 * TT], F32))
        dx_c = _es3.enter_context(nc.sbuf_tensor("dx_c", [128, 2 * TT], F32))
        xs_c = _es3.enter_context(nc.sbuf_tensor("xs_c", [128, 2 * TT], F32))
        sz_c = _es3.enter_context(nc.sbuf_tensor("sz_c", [128, 2 * TT], F32))
        sp_a = _es3.enter_context(nc.sbuf_tensor("sp_a", [128, 2 * TT], F32))
        sp_l = _es3.enter_context(nc.sbuf_tensor("sp_l", [128, 2 * TT], F32))
        dA_b = _es3.enter_context(nc.sbuf_tensor("dA_b", [128, 4 * TT], F32))
        u_b = _es3.enter_context(nc.sbuf_tensor("u_b", [128, 2 * TT], F32))
        h_b = _es3.enter_context(nc.sbuf_tensor("h_b", [128, 4 * TT], F32))
        p_b = _es3.enter_context(nc.sbuf_tensor("p_b", [128, 2 * TT], F32R))
        yg_t = _es3.enter_context(nc.sbuf_tensor("yg_t", [128, 2 * NCH * TT], BF16))
        yg_rx = _es3.enter_context(nc.sbuf_tensor("yg_rx", [128, 2 * NCH * TT], BF16))
        ps_d = _es3.enter_context(nc.psum_tensor("ps_d", [128, TT], F32))
        ps_y = _es3.enter_context(nc.psum_tensor("ps_y", [128, 2 * TT], F32))
        dtw_r = _es3.enter_context(nc.sbuf_tensor("dtw_r", [64, DI], F32))
        dt_r = _es3.enter_context(nc.sbuf_tensor("dt_r", [64, L], F32))
        dt_stg = _es3.enter_context(nc.sbuf_tensor("dt_stg", [64, L], F32))
        d_dt = sem("d_dt")
        v_dtw = sem("v_dtw")
        gp.dma_start(out=dt_stg[0:64, 0:DI], in_=dt_w).then_inc(d_dt.sem, 16)
        d_dt.n += 16
        ve.wait_ge(d_dt.sem, 16)
        v_dtw.inc(ve.tensor_copy(dtw_r[:], dt_stg[0:64, 0:DI]))
        gp.wait_ge(d_bcs.sem, 16)
        gp.wait_ge(v_dtw.sem, 1)
        gp.dma_start(out=dt_stg[:], in_=bc_sp[0:64, :]).then_inc(d_dt.sem, 16)
        d_dt.n += 16
        ve.wait_ge(d_dt.sem, 32)
        s_dtr.n = 0
        s_dtr.inc(ve.tensor_copy(dt_r[:], dt_stg[:]))
        d_bc = sem("d_bc")
        d_dx = [sem("d_dx0"), sem("d_dx1")]
        s_dA = sem("s_dA")
        g_u = sem("g_u")
        v_h = sem("v_h")
        s_stc = sem("s_stc")
        v_p = sem("v_p")
        t_y = sem("t_y")
        t_dl = sem("t_dl")
        s_sp4 = sem("s_sp4")
        v_dl = sem("v_dl")
        v_dx = sem("v_dx")
        v_gt = sem("v_gt")
        d_ygd = sem("d_ygd")

        _rsend(gp, prep, None, None, ack_yg, ls_loc, 4)

        NTOT = NT * NCH * DS
        bcs_base = d_bcs.n

        def bcB_thr(j2, s2):
            return (j2 * DS + s2 + 1) * 16

        def bcC_thr(j2, s2):
            return bcs_base + (j2 * DS + s2 + 1) * 16

        def emit_bc(j2):
            for s2 in range(DS):
                if j2 == 0 and s2 == 0:
                    sy.wait_ge(d_bcs.sem, bcs_base)
                if j2 > 0:
                    sy.wait_ge(g_u.sem, (j2 - 1) * 128 + 7 * DS + s2 + 1)
                    sy.wait_ge(v_p.sem, (j2 - 1) * 128 + 7 * DS + s2 + 1)
                sy.dma_start(
                    out=Brep[:, s2 * TT:(s2 + 1) * TT],
                    in_=bc_sp[64 + s2, j2 * TT:(j2 + 1) * TT
                              ].partition_broadcast(128)
                ).then_inc(d_bc.sem, 16)
                d_bc.n += 16
                sy.dma_start(
                    out=Crep[:, s2 * TT:(s2 + 1) * TT],
                    in_=bc_sp[80 + s2, j2 * TT:(j2 + 1) * TT
                              ].partition_broadcast(128)
                ).then_inc(d_bcs.sem, 16)
                d_bcs.n += 16

        def emit_loads(jc2):
            j2, c2 = divmod(jc2, NCH)
            pc2 = c2 % 2
            if jc2 >= 2:
                sy.wait_ge(g_u.sem, (jc2 - 1) * DS)
                sy.wait_ge(v_gt.sem, jc2 - 1)
            sy.dma_start(out=xs_c[:, pc2 * TT:(pc2 + 1) * TT],
                         in_=xss_sp[:, c2 * L + j2 * TT:c2 * L + (j2 + 1) * TT]
                         ).then_inc(d_dx[pc2].sem, 16)
            d_dx[pc2].n += 16
            sy.dma_start(out=sz_c[:, pc2 * TT:(pc2 + 1) * TT],
                         in_=sz_sp[:, c2 * L + j2 * TT:c2 * L + (j2 + 1) * TT]
                         ).then_inc(d_dx[pc2].sem, 16)
            d_dx[pc2].n += 16

        def emit_prep(jc2):
            j2, c2 = divmod(jc2, NCH)
            pc2 = c2 % 2
            te.wait_ge(s_dtr.sem, 1)
            if jc2 >= 1:
                te.wait_ge(s_sp4.sem, 2 * jc2)
            t_dl.inc(te.matmul(ps_d[:], dtw_r[:, c2 * 128:(c2 + 1) * 128],
                               dt_r[:, j2 * TT:(j2 + 1) * TT],
                               start=True, stop=True))
            sc.wait_ge(t_dl.sem, t_dl.n)
            if jc2 >= 2:
                sc.wait_ge(v_dl.sem, jc2 - 1)
            sc.activation(sp_a[:, pc2 * TT:(pc2 + 1) * TT], ps_d[:],
                          AF.Abs, bias=dtb_col[:, c2:c2 + 1])
            sc.activation(sp_l[:, pc2 * TT:(pc2 + 1) * TT],
                          sp_a[:, pc2 * TT:(pc2 + 1) * TT],
                          AF.Exp, scale=-1.0)
            s_sp4.inc(sc.activation(sp_a[:, pc2 * TT:(pc2 + 1) * TT],
                                    sp_l[:, pc2 * TT:(pc2 + 1) * TT],
                                    AF.Ln, bias=cst[:, 2:3]))
            s_sp4.inc(sc.activation(sp_l[:, pc2 * TT:(pc2 + 1) * TT],
                                    ps_d[:], AF.Relu,
                                    bias=dtb_col[:, c2:c2 + 1]))
            ve.wait_ge(s_sp4.sem, s_sp4.n)
            if jc2 >= 2:
                ve.wait_ge(s_dA.sem, (jc2 - 1) * DS)
            v_dl.inc(ve.tensor_tensor(del_c[:, pc2 * TT:(pc2 + 1) * TT],
                                      sp_l[:, pc2 * TT:(pc2 + 1) * TT],
                                      sp_a[:, pc2 * TT:(pc2 + 1) * TT],
                                      op=ALU.add))
            ve.wait_ge(d_dx[pc2].sem, d_dx[pc2].n)
            if jc2 >= 2:
                ve.wait_ge(g_u.sem, (jc2 - 1) * DS)
            v_dx.inc(ve.tensor_tensor(dx_c[:, pc2 * TT:(pc2 + 1) * TT],
                                      del_c[:, pc2 * TT:(pc2 + 1) * TT],
                                      xs_c[:, pc2 * TT:(pc2 + 1) * TT],
                                      op=ALU.mult))

        def emit_dA(it2):
            jc2 = it2 // DS
            c2 = jc2 % NCH
            sl = it2 % 4
            col2 = c2 * DS + (it2 % DS)
            if it2 >= 4:
                sc.wait_ge(v_h.sem, it2 - 3)
            sc.wait_ge(v_dl.sem, jc2 + 1)
            s_dA.inc(sc.activation(dA_b[:, sl * TT:(sl + 1) * TT],
                                   del_c[:, (c2 % 2) * TT:(c2 % 2) * TT + TT],
                                   AF.Exp, scale=A_neg[:, col2:col2 + 1]))

        def emit_u(it2):
            jc2 = it2 // DS
            j2 = jc2 // NCH
            c2 = jc2 % NCH
            s2 = it2 % DS
            pb2 = it2 % 2
            gp.wait_ge(v_dx.sem, jc2 + 1)
            gp.wait_ge(d_bc.sem, bcB_thr(j2, s2))
            if it2 >= 2:
                gp.wait_ge(v_h.sem, it2 - 1)
            g_u.inc(gp.tensor_tensor(u_b[:, pb2 * TT:(pb2 + 1) * TT],
                                     dx_c[:, (c2 % 2) * TT:(c2 % 2) * TT + TT],
                                     Brep[:, s2 * TT:(s2 + 1) * TT],
                                     op=ALU.mult))

        for j in range(NT):
            if j == 0:
                emit_loads(0)
                emit_loads(1)
                emit_bc(0)
                emit_prep(0)
                for w in range(3):
                    emit_dA(w)
                emit_u(0)
            for c in range(NCH):
                jc = j * NCH + c
                pc = c % 2
                for s in range(DS):
                    it = jc * DS + s
                    pb = it % 2
                    col = c * DS + s
                    if it + 3 < NTOT:
                        emit_dA(it + 3)
                    if it + 1 < NTOT:
                        emit_u(it + 1)
                    hl = it % 4
                    ve.wait_ge(s_dA.sem, it + 1)
                    ve.wait_ge(g_u.sem, it + 1)
                    if it >= 4:
                        ve.wait_ge(s_stc.sem, it - 3)
                    if it >= 2:
                        ve.wait_ge(t_y.sem, it - 1)
                    v_h.inc(ve.tensor_tensor_scan(
                        h_b[:, hl * TT:(hl + 1) * TT],
                        dA_b[:, (it % 4) * TT:(it % 4) * TT + TT],
                        u_b[:, pb * TT:(pb + 1) * TT],
                        state[:, col:col + 1],
                        op0=ALU.mult, op1=ALU.add))
                    if it >= 3:
                        itp = it - 3
                        hlp = itp % 4
                        colp = ((itp // DS) % NCH) * DS + itp % DS
                        sc.wait_ge(v_h.sem, it + 1)
                        s_stc.inc(sc.activation(
                            state[:, colp:colp + 1],
                            h_b[:, hlp * TT + TT - 1:hlp * TT + TT], AF.Copy))
                    ve.wait_ge(d_bcs.sem, bcC_thr(j, s))
                    v_p.inc(ve.tensor_tensor(p_b[:, pb * TT:(pb + 1) * TT],
                                             h_b[:, hl * TT:(hl + 1) * TT],
                                             Crep[:, s * TT:(s + 1) * TT],
                                             op=ALU.mult))
                    te.wait_ge(v_p.sem, it + 1)
                    if s == 0 and jc >= 2:
                        te.wait_ge(v_gt.sem, jc - 1)
                    t_y.inc(te.matmul(ps_y[:, pc * TT:(pc + 1) * TT],
                                      ident_r[:],
                                      p_b[:, pb * TT:(pb + 1) * TT],
                                      start=(s == 0), stop=(s == DS - 1)))
                    if s == 7 and jc + 1 < NT * NCH:
                        emit_prep(jc + 1)
                ve.wait_ge(t_y.sem, (jc + 1) * DS)
                ve.scalar_tensor_tensor(dx_c[:, pc * TT:(pc + 1) * TT],
                                        xs_c[:, pc * TT:(pc + 1) * TT],
                                        D_col[:, c:c + 1],
                                        ps_y[:, pc * TT:(pc + 1) * TT],
                                        op0=ALU.mult, op1=ALU.add)
                if j >= 2 and c == 0:
                    ve.wait_ge(d_ygs.sem, (j - 1) * 16)
                    ve.wait_ge(ls_loc.sem, 16 + (j - 1) * 16)
                v_gt.inc(ve.tensor_tensor(
                    yg_t[:, (j % 2) * NCH * TT + c * TT:
                         (j % 2) * NCH * TT + (c + 1) * TT],
                    dx_c[:, pc * TT:(pc + 1) * TT],
                    sz_c[:, pc * TT:(pc + 1) * TT], op=ALU.mult))
                if jc + 2 < NT * NCH:
                    emit_loads(jc + 2)
                if c == 6 and j + 1 < NT:
                    emit_bc(j + 1)
            jm = j % 2
            sy.wait_ge(v_gt.sem, (j + 1) * NCH)
            sy.dma_start(out=yg_my_sp[:, j * NCH * TT:(j + 1) * NCH * TT],
                         in_=yg_t[:, jm * NCH * TT:(jm + 1) * NCH * TT]
                         ).then_inc(d_ygs.sem, 16)
            d_ygs.n += 16
            gp.wait_ge(v_gt.sem, (j + 1) * NCH)
            if j >= 2:
                gp.wait_ge(ack_yg.sem, 4 + 2 * (j - 1))
            _rsend(gp, prep,
                   yg_rx[:, jm * NCH * TT:(jm + 1) * NCH * TT],
                   yg_t[:, jm * NCH * TT:(jm + 1) * NCH * TT],
                   rs_yg, ls_loc, 2)
            gp.wait_ge(rs_yg.sem, 2 * (j + 1))
            gp.dma_start(out=yg_pr_sp[:, j * NCH * TT:(j + 1) * NCH * TT],
                         in_=yg_rx[:, jm * NCH * TT:(jm + 1) * NCH * TT]
                         ).then_inc(d_ygd.sem, 16)
            d_ygd.n += 16
            gp.wait_ge(d_ygd.sem, d_ygd.n)
            _rsend(gp, prep, None, None, ack_yg, ls_loc, 2)

        _es3.close()
        phase_barrier()
        d_mm = sem("d_mm")
        _es4 = ExitStack()
        ow_bf = _es4.enter_context(nc.sbuf_tensor("ow_bf", [128, 16 * 512], BF16))
        yg_in = _es4.enter_context(nc.sbuf_tensor("yg_in", [128, 2 * 16 * TT], BF16))
        mm_f = _es4.enter_context(nc.sbuf_tensor("mm_f", [128, 2 * 4 * TT], F32))
        ps_mm = _es4.enter_context(nc.psum_tensor("ps_mm", [128, 2 * TT], F32))
        gp.dma_start(out=ow_bf[:], in_=out_w).then_inc(d_w.sem, 16)
        d_w.n += 16
        n_w_ow = d_w.n
        d_ygr = [sem("d_ygr0"), sem("d_ygr1")]
        t_mm = sem("t_mm")
        s_mme = sem("s_mme")
        for j in range(NT):
            jp = j % 2
            if j >= 2:
                sy.wait_ge(t_mm.sem, (j - 1) * 4)
            sy.wait_ge(d_ygs.sem, 16 * (j + 1))
            sy.dma_start(out=yg_in[:, jp * 16 * TT:jp * 16 * TT + 8 * TT],
                         in_=yg_my_sp[:, j * NCH * TT:(j + 1) * NCH * TT]
                         ).then_inc(d_ygr[jp].sem, 16)
            d_ygr[jp].n += 16
            sy.dma_start(out=yg_in[:, jp * 16 * TT + 8 * TT:
                                   (jp + 1) * 16 * TT],
                         in_=yg_pr_sp[:, j * NCH * TT:(j + 1) * NCH * TT]
                         ).then_inc(d_ygr[jp].sem, 16)
            d_ygr[jp].n += 16
            te.wait_ge(d_ygr[jp].sem, d_ygr[jp].n)
            te.wait_ge(d_w.sem, n_w_ow)
            for m in range(4):
                pm = m % 2
                if j * 4 + m >= 2:
                    te.wait_ge(s_mme.sem, j * 4 + m - 1)
                for k in range(16):
                    ins = te.matmul(
                        ps_mm[:, pm * TT:(pm + 1) * TT],
                        ow_bf[:, k * 512 + m * 128:k * 512 + (m + 1) * 128],
                        yg_in[:, jp * 16 * TT + k * TT:
                              jp * 16 * TT + (k + 1) * TT],
                        start=(k == 0), stop=(k == 15))
                t_mm.inc(ins)
                sc.wait_ge(t_mm.sem, t_mm.n)
                if j >= 2 and m == 0:
                    sc.wait_ge(d_mm.sem, (j - 1) * 16)
                s_mme.inc(sc.activation(
                    mm_f[:, (jp * 4 + m) * TT:(jp * 4 + m + 1) * TT],
                    ps_mm[:, pm * TT:(pm + 1) * TT], AF.Copy))
            sy.wait_ge(s_mme.sem, s_mme.n)
            sy.dma_start(
                out=mm_sp.rearrange("p (q l) -> p q l", q=4
                                    )[:, :, j * TT:(j + 1) * TT],
                in_=mm_f[:, jp * 4 * TT:(jp + 1) * 4 * TT
                         ].rearrange("p (q t) -> p q t", q=4)
            ).then_inc(d_mm.sem, 16)
            d_mm.n += 16

        _es4.close()
        phase_barrier()
        d_gsp = sem("d_gsp")
        _es5 = ExitStack()
        enc_bf = _es5.enter_context(nc.sbuf_tensor("enc_bf", [128, 8 * HID], BF16))
        dec_bf = _es5.enter_context(nc.sbuf_tensor("dec_bf", [128, 16 * 1024], BF16))
        ew1_bf = _es5.enter_context(nc.sbuf_tensor("ew1_bf", [64, NE * 1024], BF16))
        ew2_bf = _es5.enter_context(nc.sbuf_tensor("ew2_bf", [128, NE * 8 * 64], BF16))
        s2c_bf = _es5.enter_context(nc.sbuf_tensor("s2c_bf", [128, 16 * 64], BF16))
        c2s_bf = _es5.enter_context(nc.sbuf_tensor("c2s_bf", [64, HID], BF16))
        r1_bf = _es5.enter_context(nc.sbuf_tensor("r1_bf", [64, 64], BF16))
        r2_bf = _es5.enter_context(nc.sbuf_tensor("r2_bf", [64, 8], BF16))
        encb_c = _es5.enter_context(nc.sbuf_tensor("encb_c", [128, 16], F32))
        s2cb_c = _es5.enter_context(nc.sbuf_tensor("s2cb_c", [64, 1], F32))
        rb1_c = _es5.enter_context(nc.sbuf_tensor("rb1_c", [64, 1], F32))
        rb2_c = _es5.enter_context(nc.sbuf_tensor("rb2_c", [8, 1], F32))
        eb1_c = _es5.enter_context(nc.sbuf_tensor("eb1_c", [128, NE * 8], F32))
        eb2_c = _es5.enter_context(nc.sbuf_tensor("eb2_c", [64, NE], F32))
        c2sb_c = _es5.enter_context(nc.sbuf_tensor("c2sb_c", [128, 16], F32))
        decb_c = _es5.enter_context(nc.sbuf_tensor("decb_c", [128, 8], F32))
        sums2 = _es5.enter_context(nc.sbuf_tensor("sums2", [128, 16], F32))
        sqs2 = _es5.enter_context(nc.sbuf_tensor("sqs2", [128, 16], F32))
        m2_st = _es5.enter_context(nc.sbuf_tensor("m2_st", [128, 16], F32))
        r2_st = _es5.enter_context(nc.sbuf_tensor("r2_st", [128, 16], F32))
        sq_sc2 = _es5.enter_context(nc.sbuf_tensor("sq_sc2", [128, 2 * DM], F32))
        xm_t = _es5.enter_context(nc.sbuf_tensor("xm_t", [128, 2 * DM], F32))
        xn2_t = _es5.enter_context(nc.sbuf_tensor("xn2_t", [128, 2 * DM], F32R))
        xnf_bf = _es5.enter_context(nc.sbuf_tensor("xnf_bf", [128, 8 * TT], BF16))
        spk_bf = _es5.enter_context(nc.sbuf_tensor("spk_bf", [128, 16 * TT], BF16))
        cont_bf = _es5.enter_context(nc.sbuf_tensor("cont_bf", [64, 2 * TT], BF16))
        rh_bf = _es5.enter_context(nc.sbuf_tensor("rh_bf", [64, 2 * TT], BF16))
        lgits = _es5.enter_context(nc.sbuf_tensor("lgits", [8, 2 * TT], F32))
        comb = _es5.enter_context(nc.sbuf_tensor("comb", [8, 2 * TT], F32))
        lg_sb = _es5.enter_context(nc.sbuf_tensor("lg_sb", [128, 4 * 8], F32))
        rt_m = _es5.enter_context(nc.sbuf_tensor("rt_m", [128, 4 * 8], F32))
        rt_g = _es5.enter_context(nc.sbuf_tensor("rt_g", [128, 4 * 8], F32))
        rt_s = _es5.enter_context(nc.sbuf_tensor("rt_s", [128, 4 * 4], F32))
        eh_bf = _es5.enter_context(nc.sbuf_tensor("eh_bf", [128, 2 * TT], BF16))
        crep_sb = _es5.enter_context(nc.sbuf_tensor("crep_sb", [64, 2 * TT], F32))
        pe_sb = _es5.enter_context(nc.sbuf_tensor("pe_sb", [64, 2 * TT], F32))
        moe_f = _es5.enter_context(nc.sbuf_tensor("moe_f", [64, 2 * TT], F32))
        moe_bf = _es5.enter_context(nc.sbuf_tensor("moe_bf", [64, 2 * TT], BF16))
        rat_bf = _es5.enter_context(nc.sbuf_tensor("rat_bf", [128, 16 * TT], BF16))
        dec_f = _es5.enter_context(nc.sbuf_tensor("dec_f", [128, 2 * TT], F32))
        dec_r = _es5.enter_context(nc.sbuf_tensor("dec_r", [128, 2 * TT], F32R))
        ps_a = _es5.enter_context(nc.psum_tensor("ps_a", [128, 2 * TT], F32))
        ps_eo = _es5.enter_context(nc.psum_tensor("ps_eo", [128, TT], F32))
        ps_cr = _es5.enter_context(nc.psum_tensor("ps_cr", [128, TT], F32))
        ps_r = _es5.enter_context(nc.psum_tensor("ps_r", [128, 2 * TT], F32))
        ps_rep = _es5.enter_context(nc.psum_tensor("ps_rep", [128, 2 * TT], F32))
        for t_, s_ in ((enc_bf, enc_w), (dec_bf, dec_w), (ew1_bf, e_w1),
                       (ew2_bf, e_w2), (s2c_bf, s2c_w), (c2s_bf, c2s_w),
                       (r1_bf, r_w1), (r2_bf, r_w2), (encb_c, enc_b),
                       (s2cb_c, s2c_b), (rb1_c, r_b1), (rb2_c, r_b2),
                       (eb1_c, e_b1), (eb2_c, e_b2), (c2sb_c, c2s_b),
                       (decb_c, dec_b), (lng_c, ln_g), (lnb_c, ln_b)):
            gp.dma_start(out=t_[:], in_=s_).then_inc(d_w.sem, 16)
            d_w.n += 16
        n_w_ffn = d_w.n

        d_xm = d_xa
        v_s2 = sem("v_s2")
        s_q2 = sem("s_q2")
        for tbi in range(16):
            p = tbi % 2
            if tbi >= 2:
                sy.wait_ge(v_s2.sem, 2 * tbi - 3)
                sy.wait_ge(s_q2.sem, tbi - 1)
            sy.dma_start(out=xm_t[:, p * DM:(p + 1) * DM],
                         in_=x_my[tbi * 128:(tbi + 1) * 128, :]
                         ).then_inc(d_xm[p].sem, 16)
            d_xm[p].n += 16
            ve.wait_ge(d_xm[p].sem, d_xm[p].n)
            v_s2.inc(ve.tensor_reduce(sums2[:, tbi:tbi + 1],
                                      xm_t[:, p * DM:(p + 1) * DM],
                                      axis=mybir.AxisListType.X,
                                      op=ALU.add))
            sc.wait_ge(d_xm[p].sem, d_xm[p].n)
            if tbi >= 2:
                sc.wait_ge(v_s2.sem, 2 * tbi - 2)
            s_q2.inc(sc.activation(sq_sc2[:, p * DM:(p + 1) * DM],
                                   xm_t[:, p * DM:(p + 1) * DM], AF.Square))
            ve.wait_ge(s_q2.sem, tbi + 1)
            v_s2.inc(ve.tensor_reduce(sqs2[:, tbi:tbi + 1],
                                      sq_sc2[:, p * DM:(p + 1) * DM],
                                      axis=mybir.AxisListType.X,
                                      op=ALU.add))
        ve.wait_ge(s_q2.sem, 16)
        sc.wait_ge(v_s2.sem, 32)
        v_s2b = sem("v_s2b")
        v_s2b.inc(sc.activation(m2_st[:], sums2[:], AF.Copy, scale=1.0 / DM))
        v_s2b.inc(sc.activation(r2_st[:], m2_st[:], AF.Square))
        v_s2b.inc(sc.activation(sq_sc2[:, 0:16], sqs2[:], AF.Copy,
                                scale=1.0 / DM))
        gp.wait_ge(v_s2b.sem, 3)
        g_cmp.inc(gp.tensor_tensor(sums2[:], sq_sc2[:, 0:16], r2_st[:],
                                   op=ALU.subtract))
        sc.wait_ge(g_cmp.sem, g_cmp.n)
        sc.activation(sq_sc2[:, 16:32], sums2[:], AF.Ln, bias=cst[:, 0:1])
        s_st2 = sem("s_st2")
        s_st2.inc(sc.activation(r2_st[:], sq_sc2[:, 16:32], AF.Exp,
                                scale=-0.5))
        if DEBUG_SCRATCH:
            ve.wait_ge(s_st2.sem, 1)
            ve.tensor_copy(sq_sc2[:, 64:80], m2_st[:])
            ve.tensor_copy(sq_sc2[:, 80:96], r2_st[:])
            ve.tensor_copy(sq_sc2[:, 96:112], sums2[:])
            v_s2b.inc(ve.tensor_copy(sq_sc2[:, 112:128], sqs2[:]))
            gp.wait_ge(v_s2b.sem, 4)
            gp.dma_start(out=dbg_st2[:, :], in_=sq_sc2[:, 64:128]
                         ).then_inc(d_xm[0].sem, 16)
            d_xm[0].n += 16
            ve.wait_ge(d_xm[0].sem, d_xm[0].n)

        v_xn2 = sem("v_xn2")
        t_tr2 = sem("t_tr2")
        s_ek2 = sem("s_ek2")
        t_enc = sem("t_enc")
        s_spk = sem("s_spk")
        t_s2c = sem("t_s2c")
        s_cnt = sem("s_cnt")
        t_rt = sem("t_rt")
        s_rt = sem("s_rt")
        v_rt = sem("v_rt")
        s_cmb = sem("s_cmb")
        t_eh = sem("t_eh")
        d_cmb = sem("d_cmb")
        d_crp = sem("d_crp")
        s_eh = sem("s_eh")
        t_eo = sem("t_eo")
        s_cre = sem("s_cre")
        v_moe = sem("v_moe")
        s_mbf = sem("s_mbf")
        t_c2s = sem("t_c2s")
        s_rat = sem("s_rat")
        t_dec = sem("t_dec")
        s_dec = sem("s_dec")
        s_dcr = sem("s_dcr")
        t_sum = sem("t_sum")
        v_lnf = sem("v_lnf")
        d_st = sem("d_st")
        d_dcs = sem("d_dcs")
        NTH = LH // TT
        for jj in range(NTH):
            pf = jj % 2
            for t4 in range(4):
                tbi = jj * 4 + t4
                pp = t4 % 2
                if v_xn2.n < 2:
                    sy.wait_ge(v_s2.sem, 32)
                    sy.wait_ge(s_q2.sem, 16)
                else:
                    sy.wait_ge(v_xn2.sem, v_xn2.n - 1)
                sy.dma_start(out=xm_t[:, pp * DM:(pp + 1) * DM],
                             in_=x_my[tbi * 128:(tbi + 1) * 128, :]
                             ).then_inc(d_xm[pp].sem, 16)
                d_xm[pp].n += 16
                ve.wait_ge(d_xm[pp].sem, d_xm[pp].n)
                ve.wait_ge(s_st2.sem, 1)
                if v_xn2.n >= 2:
                    ve.wait_ge(t_tr2.sem, 2 * (v_xn2.n - 1))
                v_xn2.inc(ve.tensor_scalar(
                    xn2_t[:, pp * DM:(pp + 1) * DM],
                    xm_t[:, pp * DM:(pp + 1) * DM],
                    m2_st[:, tbi:tbi + 1], r2_st[:, tbi:tbi + 1],
                    op0=ALU.subtract, op1=ALU.mult))
                te.wait_ge(v_xn2.sem, v_xn2.n)
                if t_tr2.n >= 2:
                    te.wait_ge(s_ek2.sem, t_tr2.n - 1)
                for q in range(8):
                    ins = te.transpose(
                        ps_rep[:, pp * TT + ((q % 4) * 128):
                               pp * TT + ((q % 4) + 1) * 128].bitcast(F32R)
                        if q < 4 else
                        ps_rep[:, pp * TT + ((q - 4) * 128):
                               pp * TT + ((q - 4) + 1) * 128].bitcast(F32R),
                        xn2_t[:, pp * DM + q * 128:pp * DM + (q + 1) * 128],
                        ident_r[:])
                    if q == 3 or q == 7:
                        t_tr2.inc(ins)
                    if q == 3:
                        sc.wait_ge(t_tr2.sem, t_tr2.n)
                        if jj >= 1 and t4 == 0:
                            sc.wait_ge(t_enc.sem, jj * 16)
                        for q2 in range(4):
                            ins2 = sc.activation(
                                xnf_bf[:, q2 * TT + t4 * 128:
                                       q2 * TT + (t4 + 1) * 128],
                                ps_rep[:, pp * TT + q2 * 128:
                                       pp * TT + (q2 + 1) * 128],
                                AF.Copy)
                        s_ek2.inc(ins2)
                        te.wait_ge(s_ek2.sem, s_ek2.n)
                sc.wait_ge(t_tr2.sem, t_tr2.n)
                if jj >= 1 and t4 == 0:
                    sc.wait_ge(t_enc.sem, jj * 16)
                for q2 in range(4):
                    ins2 = sc.activation(
                        xnf_bf[:, (4 + q2) * TT + t4 * 128:
                               (4 + q2) * TT + (t4 + 1) * 128],
                        ps_rep[:, pp * TT + q2 * 128:
                               pp * TT + (q2 + 1) * 128],
                        AF.Copy)
                s_ek2.inc(ins2)
            te.wait_ge(s_ek2.sem, s_ek2.n)
            te.wait_ge(d_w.sem, n_w_ffn)
            for m in range(16):
                pb = m % 2
                if m == 0 and jj >= 1:
                    te.wait_ge(s_dec.sem, 8 * jj)
                if jj * 16 + m >= 2:
                    te.wait_ge(s_spk.sem, jj * 16 + m - 1)
                for k in range(8):
                    ins = te.matmul(
                        ps_a[:, pb * TT:(pb + 1) * TT],
                        enc_bf[:, k * HID + m * 128:k * HID + (m + 1) * 128],
                        xnf_bf[:, k * TT:(k + 1) * TT],
                        start=(k == 0), stop=(k == 7))
                t_enc.inc(ins)
                sc.wait_ge(t_enc.sem, t_enc.n)
                if jj >= 1 and m == 0:
                    sc.wait_ge(t_s2c.sem, jj)
                s_spk.inc(sc.activation(spk_bf[:, m * TT:(m + 1) * TT],
                                        ps_a[:, pb * TT:(pb + 1) * TT],
                                        AF.Sigmoid,
                                        bias=encb_c[:, m:m + 1]))
            if DEBUG_SCRATCH:
                sy.wait_ge(s_spk.sem, s_spk.n)
                sy.dma_start(out=spk_sp[:, jj * 16 * TT:(jj + 1) * 16 * TT],
                             in_=spk_bf[:]).then_inc(d_cmb.sem, 16)
                sy.wait_ge(s_ek2.sem, s_ek2.n)
                sy.dma_start(out=xnf_sp[:, jj * 8 * TT:(jj + 1) * 8 * TT],
                             in_=xnf_bf[:]).then_inc(d_cmb.sem, 16)
                d_cmb.n += 32
            te.wait_ge(s_spk.sem, s_spk.n)
            if jj >= 1:
                te.wait_ge(v_lnf.sem, 2 * jj)
            for k in range(16):
                ins = te.matmul(ps_r[0:64, 0:TT],
                                s2c_bf[:, k * 64:(k + 1) * 64],
                                spk_bf[:, k * TT:(k + 1) * TT],
                                start=(k == 0), stop=(k == 15))
            t_s2c.inc(ins)
            sc.wait_ge(t_s2c.sem, t_s2c.n)
            if jj >= 2:
                sc.wait_ge(t_eh.sem, (jj - 1) * 64)
            s_cnt.inc(sc.activation(cont_bf[:, pf * TT:(pf + 1) * TT],
                                    ps_r[0:64, 0:TT], AF.Identity,
                                    bias=s2cb_c[:]))
            te.wait_ge(s_cnt.sem, s_cnt.n)
            if jj >= 1:
                te.wait_ge(v_lnf.sem, 2 * jj)
            if jj >= 1:
                te.wait_ge(s_rt.sem, 2 * jj)
            t_rt.inc(te.matmul(ps_r[0:64, TT:2 * TT], r1_bf[:],
                               cont_bf[:, pf * TT:(pf + 1) * TT],
                               start=True, stop=True))
            sc.wait_ge(t_rt.sem, t_rt.n)
            if jj >= 2:
                sc.wait_ge(t_rt.sem, t_rt.n)
            s_rt.inc(sc.activation(rh_bf[:, pf * TT:(pf + 1) * TT],
                                   ps_r[0:64, TT:2 * TT], AF.Relu,
                                   bias=rb1_c[:]))
            te.wait_ge(s_rt.sem, s_rt.n)
            t_rt.inc(te.matmul(ps_r[0:8, TT:2 * TT], r2_bf[:],
                               rh_bf[:, pf * TT:(pf + 1) * TT],
                               start=True, stop=True))
            sc.wait_ge(t_rt.sem, t_rt.n)
            if jj >= 2:
                sc.wait_ge(t_rt.sem, 10 * (jj - 1))
            s_rt.inc(sc.activation(lgits[:, pf * TT:(pf + 1) * TT],
                                   ps_r[0:8, TT:2 * TT], AF.Identity,
                                   bias=rb2_c[:]))
            if DEBUG_SCRATCH:
                sy.wait_ge(s_rt.sem, s_rt.n)
                sy.dma_start(out=lg_sp[0:8, jj * TT:(jj + 1) * TT],
                             in_=lgits[:, pf * TT:(pf + 1) * TT]
                             ).then_inc(d_cmb.sem, 16)
                d_cmb.n += 16
                sy.wait_ge(s_cnt.sem, s_cnt.n)
                sy.dma_start(out=cont_sp[0:64, jj * TT:(jj + 1) * TT],
                             in_=cont_bf[:, pf * TT:(pf + 1) * TT]
                             ).then_inc(d_cmb.sem, 16)
                d_cmb.n += 16
            for t4 in range(4):
                it4 = jj * 4 + t4
                te.wait_ge(s_rt.sem, s_rt.n)
                if t4 == 0 and jj >= 1:
                    te.wait_ge(v_moe.sem, NE * jj)
                if t4 == 0 and jj >= 2:
                    te.wait_ge(s_cmb.sem, 2 * (jj - 1))
                if t4 >= 1:
                    te.wait_ge(v_rt.sem, v_rt.n)
                t_rt.inc(te.transpose(
                    ps_eo[:, t4 * 8:(t4 + 1) * 8],
                    lgits[:, pf * TT + t4 * 128:pf * TT + (t4 + 1) * 128],
                    ident_f[0:8, 0:8]))
                ve.wait_ge(t_rt.sem, t_rt.n)
                if t4 == 0 and jj >= 1:
                    ve.wait_ge(s_cmb.sem, 2 * jj - 1)
                v_rt.inc(ve.tensor_copy(lg_sb[:, t4 * 8:(t4 + 1) * 8],
                                        ps_eo[:, t4 * 8:(t4 + 1) * 8]))
            for t4 in range(4):
                ve.reduce_max(rt_s[:, t4 * 4:t4 * 4 + 1],
                              lg_sb[:, t4 * 8:(t4 + 1) * 8],
                              axis=mybir.AxisListType.X)
            for t4 in range(4):
                ve.tensor_scalar(rt_m[:, t4 * 8:(t4 + 1) * 8],
                                 lg_sb[:, t4 * 8:(t4 + 1) * 8],
                                 rt_s[:, t4 * 4:t4 * 4 + 1], None,
                                 op0=ALU.is_equal)
            for t4 in range(4):
                ve.scalar_tensor_tensor(rt_g[:, t4 * 8:(t4 + 1) * 8],
                                        rt_m[:, t4 * 8:(t4 + 1) * 8],
                                        -1e30,
                                        lg_sb[:, t4 * 8:(t4 + 1) * 8],
                                        op0=ALU.mult, op1=ALU.add)
            for t4 in range(4):
                ve.reduce_max(rt_s[:, t4 * 4 + 1:t4 * 4 + 2],
                              rt_g[:, t4 * 8:(t4 + 1) * 8],
                              axis=mybir.AxisListType.X)
            for t4 in range(4):
                ve.tensor_tensor(rt_s[:, t4 * 4 + 2:t4 * 4 + 3],
                                 rt_s[:, t4 * 4:t4 * 4 + 1],
                                 rt_s[:, t4 * 4 + 1:t4 * 4 + 2],
                                 op=ALU.add)
            for t4 in range(4):
                ins = ve.tensor_scalar(rt_s[:, t4 * 4 + 3:t4 * 4 + 4],
                                       rt_s[:, t4 * 4 + 2:t4 * 4 + 3],
                                       -1.0, None, op0=ALU.mult)
            v_rt.inc(ins)
            sc.wait_ge(v_rt.sem, v_rt.n)
            for t4 in range(4):
                ins = sc.activation(rt_m[:, t4 * 8:(t4 + 1) * 8],
                                    lg_sb[:, t4 * 8:(t4 + 1) * 8],
                                    AF.Sigmoid, scale=2.0,
                                    bias=rt_s[:, t4 * 4 + 3:t4 * 4 + 4])
            s_cmb.inc(ins)
            ve.wait_ge(s_cmb.sem, s_cmb.n)
            for t4 in range(4):
                ve.tensor_scalar(rt_g[:, t4 * 8:(t4 + 1) * 8],
                                 lg_sb[:, t4 * 8:(t4 + 1) * 8],
                                 rt_s[:, t4 * 4 + 1:t4 * 4 + 2], None,
                                 op0=ALU.is_ge)
            for t4 in range(4):
                ins = ve.tensor_tensor(lg_sb[:, t4 * 8:(t4 + 1) * 8],
                                       rt_m[:, t4 * 8:(t4 + 1) * 8],
                                       rt_g[:, t4 * 8:(t4 + 1) * 8],
                                       op=ALU.mult)
            v_rt.inc(ins)
            te.wait_ge(v_rt.sem, v_rt.n)
            if jj >= 1:
                te.wait_ge(s_cmb.sem, 2 * jj)
            for t4 in range(4):
                ins = te.transpose(ps_cr[0:8, t4 * 128:(t4 + 1) * 128],
                                   lg_sb[:, t4 * 8:(t4 + 1) * 8],
                                   ident_f[:])
            t_rt.inc(ins)
            sc.wait_ge(t_rt.sem, t_rt.n)
            for t4 in range(4):
                if jj >= 2 and t4 == 0:
                    sc.wait_ge(d_cmb.sem, (jj - 1) * 16)
                ins = sc.activation(
                    comb[:, pf * TT + t4 * 128:pf * TT + (t4 + 1) * 128],
                    ps_cr[0:8, t4 * 128:(t4 + 1) * 128], AF.Copy)
            s_cmb.inc(ins)
            sy.wait_ge(s_cmb.sem, s_cmb.n)
            sy.dma_start(out=comb_sp[0:8, jj * TT:(jj + 1) * TT],
                         in_=comb[:, pf * TT:(pf + 1) * TT]
                         ).then_inc(d_cmb.sem, 16)
            d_cmb.n += 16
            for e in range(NE):
                pe2 = e % 2
                ie = jj * NE + e
                if ie >= 2:
                    sy.wait_ge(v_moe.sem, ie - 1)
                sy.wait_ge(d_cmb.sem, d_cmb.n)
                sy.dma_start(out=crep_sb[:, pe2 * TT:(pe2 + 1) * TT],
                             in_=comb_sp[e, jj * TT:(jj + 1) * TT
                                         ].partition_broadcast(64)
                             ).then_inc(d_crp.sem, 16)
                d_crp.n += 16
                for m in range(8):
                    pm = m % 2
                    iem = (jj * NE + e) * 8 + m
                    te.wait_ge(s_cnt.sem, s_cnt.n)
                    if iem >= 2:
                        te.wait_ge(s_eh.sem, iem - 1)
                    ins = te.matmul(
                        ps_a[:, pm * TT:(pm + 1) * TT],
                        ew1_bf[:, e * 1024 + m * 128:
                               e * 1024 + (m + 1) * 128],
                        cont_bf[:, pf * TT:(pf + 1) * TT],
                        start=True, stop=True)
                    t_eh.inc(ins)
                    sc.wait_ge(t_eh.sem, t_eh.n)
                    s_eh.inc(sc.activation(
                        eh_bf[:, pm * TT:(pm + 1) * TT],
                        ps_a[:, pm * TT:(pm + 1) * TT], AF.Relu,
                        bias=eb1_c[:, e * 8 + m:e * 8 + m + 1]))
                    te.wait_ge(s_eh.sem, s_eh.n)
                    if m == 0 and (jj * NE + e) >= 1:
                        te.wait_ge(v_moe.sem, jj * NE + e)
                    ins = te.matmul(
                        ps_eo[0:64, 0:TT],
                        ew2_bf[:, (e * 8 + m) * 64:(e * 8 + m + 1) * 64],
                        eh_bf[:, pm * TT:(pm + 1) * TT],
                        start=(m == 0), stop=(m == 7))
                    if m == 7:
                        t_eo.inc(ins)
                ve.wait_ge(d_crp.sem, d_crp.n)
                ve.wait_ge(t_eo.sem, t_eo.n)
                ve.scalar_tensor_tensor(pe_sb[:, pf * TT:(pf + 1) * TT],
                                        ps_eo[0:64, 0:TT],
                                        eb2_c[:, e:e + 1],
                                        crep_sb[:, pe2 * TT:(pe2 + 1) * TT],
                                        op0=ALU.add, op1=ALU.mult)
                if e == 0 and jj >= 2:
                    ve.wait_ge(s_mbf.sem, jj - 1)
                if e == 0:
                    v_moe.inc(ve.tensor_copy(
                        moe_f[:, pf * TT:(pf + 1) * TT],
                        pe_sb[:, pf * TT:(pf + 1) * TT]))
                else:
                    v_moe.inc(ve.tensor_tensor(
                        moe_f[:, pf * TT:(pf + 1) * TT],
                        moe_f[:, pf * TT:(pf + 1) * TT],
                        pe_sb[:, pf * TT:(pf + 1) * TT], op=ALU.add))
            sc.wait_ge(v_moe.sem, v_moe.n)
            s_mbf.inc(sc.activation(moe_bf[:, pf * TT:(pf + 1) * TT],
                                    moe_f[:, pf * TT:(pf + 1) * TT],
                                    AF.Copy))
            te.wait_ge(s_mbf.sem, s_mbf.n)
            for m in range(16):
                pb = m % 2
                if jj * 16 + m >= 2:
                    te.wait_ge(s_rat.sem, jj * 16 + m - 1)
                t_c2s.inc(te.matmul(ps_a[:, pb * TT:(pb + 1) * TT],
                                    c2s_bf[:, m * 128:(m + 1) * 128],
                                    moe_bf[:, pf * TT:(pf + 1) * TT],
                                    start=True, stop=True))
                sc.wait_ge(t_c2s.sem, t_c2s.n)
                if jj >= 1 and m == 0:
                    sc.wait_ge(t_dec.sem, jj * 8)
                s_rat.inc(sc.activation(rat_bf[:, m * TT:(m + 1) * TT],
                                        ps_a[:, pb * TT:(pb + 1) * TT],
                                        AF.Sigmoid,
                                        bias=c2sb_c[:, m:m + 1]))
            te.wait_ge(s_rat.sem, s_rat.n)
            for m in range(8):
                pb = m % 2
                if jj * 8 + m >= 2:
                    te.wait_ge(s_dec.sem, jj * 8 + m - 1)
                for k in range(16):
                    ins = te.matmul(
                        ps_a[:, pb * TT:(pb + 1) * TT],
                        dec_bf[:, k * 1024 + m * 128:
                               k * 1024 + (m + 1) * 128],
                        rat_bf[:, k * TT:(k + 1) * TT],
                        start=(k == 0), stop=(k == 15))
                t_dec.inc(ins)
                sc.wait_ge(t_dec.sem, t_dec.n)
                if jj * 8 + m >= 2:
                    sc.wait_ge(d_dcs.sem, d_dcs.n)
                    sc.wait_ge(t_sum.sem, t_sum.n)
                s_dec.inc(sc.activation(dec_f[:, pb * TT:(pb + 1) * TT],
                                        ps_a[:, pb * TT:(pb + 1) * TT],
                                        AF.Sigmoid,
                                        bias=decb_c[:, m:m + 1]))
                sy.wait_ge(s_dec.sem, s_dec.n)
                sy.dma_start(out=dec_sp[:, m * LH + jj * TT:
                                        m * LH + (jj + 1) * TT],
                             in_=dec_f[:, pb * TT:(pb + 1) * TT]
                             ).then_inc(d_dcs.sem, 16)
                d_dcs.n += 16
                s_dcr.inc(sc.activation(dec_r[:, pb * TT:(pb + 1) * TT],
                                        dec_f[:, pb * TT:(pb + 1) * TT],
                                        AF.Copy))
                te.wait_ge(s_dcr.sem, s_dcr.n)
                t_sum.inc(te.matmul(ps_r[:, 0:TT], ones_r[:, 0:128],
                                    dec_r[:, pb * TT:(pb + 1) * TT],
                                    start=(m == 0), stop=(m == 7)))
                sc.wait_ge(t_sum.sem, t_sum.n)
                s_dcr.inc(sc.activation(dec_r[:, pb * TT:(pb + 1) * TT],
                                        dec_f[:, pb * TT:(pb + 1) * TT],
                                        AF.Square))
                te.wait_ge(s_dcr.sem, s_dcr.n)
                t_sum.inc(te.matmul(ps_r[:, TT:2 * TT], ones_r[:, 0:128],
                                    dec_r[:, pb * TT:(pb + 1) * TT],
                                    start=(m == 0), stop=(m == 7)))
            ve.wait_ge(t_sum.sem, t_sum.n)
            if jj >= 1:
                ve.wait_ge(d_st.sem, d_st.n)
            v_lnf.inc(ve.tensor_copy(st_row[0:1, 0:TT], ps_r[0:1, 0:TT]))
            v_lnf.inc(ve.tensor_copy(st_row[0:1, TT:2 * TT],
                                     ps_r[0:1, TT:2 * TT]))
            sy.wait_ge(v_lnf.sem, v_lnf.n)
            sy.dma_start(out=st2_sp[0, jj * TT:(jj + 1) * TT],
                         in_=st_row[0:1, 0:TT]).then_inc(d_st.sem, 16)
            d_st.n += 16
            sy.dma_start(out=st2_sp[1, jj * TT:(jj + 1) * TT],
                         in_=st_row[0:1, TT:2 * TT]).then_inc(d_st.sem, 16)
            d_st.n += 16
        _es5.close()
        phase_barrier()
        _es7 = ExitStack()
        g_stg = _es7.enter_context(nc.sbuf_tensor("g_stg", [128, 2 * 4 * TT], F32))
        g_snd = _es7.enter_context(nc.sbuf_tensor("g_snd", [128, 4 * LH], F32))
        g_rcv = _es7.enter_context(nc.sbuf_tensor("g_rcv", [128, 4 * LH], F32))
        dec_f = _es7.enter_context(nc.sbuf_tensor("dec_fg", [128, 2 * TT], F32))
        wk_g = _es7.enter_context(nc.sbuf_tensor("wk_g", [128, 2 * TT], F32))
        ps_rep = _es7.enter_context(nc.psum_tensor("ps_rep2", [128, 2 * TT], F32))
        st2_f = _es7.enter_context(nc.sbuf_tensor("st2_f", [1, 2 * LH], F32))
        mr_r = _es7.enter_context(nc.sbuf_tensor("mr_r", [1, 2 * LH], F32R))
        v_ln2 = sem("v_ln2")
        d_st2 = d_dt
        scr_row = _es7.enter_context(nc.sbuf_tensor("scr_row", [1, LH], F32))
        scr2_row = _es7.enter_context(nc.sbuf_tensor("scr2_row", [1, LH], F32))
        gp.wait_ge(d_st.sem, d_st.n)
        gp.dma_start(out=st2_f[0:1, 0:LH], in_=st2_sp[0, :]
                     ).then_inc(d_st2.sem, 16)
        gp.dma_start(out=st2_f[0:1, LH:2 * LH], in_=st2_sp[1, :]
                     ).then_inc(d_st2.sem, 16)
        d_st2.n += 32
        sc.wait_ge(d_st2.sem, d_st2.n)
        s_ln2 = sem("s_ln2")
        s_ln2.inc(sc.activation(scr_row[:], st2_f[0:1, 0:LH], AF.Copy,
                                scale=1.0 / DM))
        s_ln2.inc(sc.activation(st2_f[0:1, 0:LH], scr_row[:], AF.Square))
        s_ln2.inc(sc.activation(scr2_row[:], st2_f[0:1, LH:2 * LH], AF.Copy,
                                scale=1.0 / DM))
        gp.wait_ge(s_ln2.sem, 3)
        g_cmp.inc(gp.tensor_tensor(st2_f[0:1, LH:2 * LH], scr2_row[:],
                                   st2_f[0:1, 0:LH], op=ALU.subtract))
        sc.wait_ge(g_cmp.sem, g_cmp.n)
        sc.activation(st2_f[0:1, 0:LH], st2_f[0:1, LH:2 * LH],
                      AF.Ln, bias=cst[0:1, 0:1])
        s_ln2.inc(sc.activation(scr2_row[:], st2_f[0:1, 0:LH], AF.Exp,
                                scale=-0.5))
        ve.wait_ge(s_ln2.sem, 4)
        v_ln2.inc(ve.tensor_copy(mr_r[0:1, 0:LH], scr_row[:]))
        v_ln2.inc(ve.tensor_copy(mr_r[0:1, LH:2 * LH], scr2_row[:]))
        _rsend(gp, prep, None, None, rs_rdy, ls_loc, 2)
        d_dcr = d_xa
        t_rep = sem("t_rep")
        v_g1 = sem("v_g1")
        s_g = sem("s_g")
        for jj in range(NTH):
            pf = jj % 2
            te.wait_ge(v_ln2.sem, 2)
            te.wait_ge(v_one.sem, 2)
            if jj >= 1:
                te.wait_ge(v_g1.sem, jj * 8)
            te.matmul(ps_rep[:, 0:TT], ones_r[0:1, :],
                      mr_r[0:1, jj * TT:(jj + 1) * TT],
                      start=True, stop=True)
            t_rep.inc(te.matmul(ps_rep[:, TT:2 * TT], ones_r[0:1, :],
                                mr_r[0:1, LH + jj * TT:LH + (jj + 1) * TT],
                                start=True, stop=True))
            for q in range(8):
                iq = jj * 8 + q
                pq = iq % 2
                if iq >= 2:
                    sy.wait_ge(v_g1.sem, iq - 1)
                sy.dma_start(out=dec_f[:, pq * TT:(pq + 1) * TT],
                             in_=dec_sp[:, q * LH + jj * TT:
                                        q * LH + (jj + 1) * TT]
                             ).then_inc(d_dcr[pq].sem, 16)
                d_dcr[pq].n += 16
                ve.wait_ge(d_dcr[pq].sem, d_dcr[pq].n)
                ve.wait_ge(t_rep.sem, t_rep.n)
                if iq >= 2:
                    ve.wait_ge(s_g.sem, iq - 1)
                ve.tensor_tensor(wk_g[:, pq * TT:(pq + 1) * TT],
                                 dec_f[:, pq * TT:(pq + 1) * TT],
                                 ps_rep[:, 0:TT], op=ALU.subtract)
                v_g1.inc(ve.tensor_tensor(
                    wk_g[:, pq * TT:(pq + 1) * TT],
                    wk_g[:, pq * TT:(pq + 1) * TT],
                    ps_rep[:, TT:2 * TT], op=ALU.mult))
                sc.wait_ge(v_g1.sem, v_g1.n)
                if q < 4:
                    dst = g_stg[:, (pf * 4 + q) * TT:(pf * 4 + q + 1) * TT]
                    if jj >= 2 and q == 0:
                        sc.wait_ge(d_gsp.sem, (jj - 1) * 16)
                else:
                    dst = g_snd[:, (q - 4) * LH + jj * TT:
                                (q - 4) * LH + (jj + 1) * TT]
                s_g.inc(sc.activation(dst,
                                      wk_g[:, pq * TT:(pq + 1) * TT],
                                      AF.Silu, scale=lng_c[:, q:q + 1],
                                      bias=lnb_c[:, q:q + 1]))
            sy.wait_ge(s_g.sem, s_g.n)
            sy.dma_start(
                out=g_my_sp.rearrange("p (q l) -> p q l", q=4
                                      )[:, :, jj * TT:(jj + 1) * TT],
                in_=g_stg[:, pf * 4 * TT:(pf + 1) * 4 * TT
                          ].rearrange("p (q t) -> p q t", q=4)
            ).then_inc(d_gsp.sem, 16)
            d_gsp.n += 16
        gp.wait_ge(s_g.sem, s_g.n)
        gp.wait_ge(rs_rdy.sem, 4)
        _rsend(gp, prep, g_rcv[:], g_snd[:], rs_ga, ls_loc, 2)
        gp.wait_ge(rs_ga.sem, 2)
        gp.dma_start(out=g_pr_sp[:, :], in_=g_rcv[:]).then_inc(d_gsp.sem, 16)
        d_gsp.n += 16
        gp.wait_ge(ls_loc.sem, ls_loc.n)

        _es7.close()
        phase_barrier()
        _es6 = ExitStack()
        g_all = _es6.enter_context(nc.sbuf_tensor("g_all", [128, 4 * L], F32))
        x_f = _es6.enter_context(nc.sbuf_tensor("x_f", [128, 2 * 512], F32))
        xn_f = _es6.enter_context(nc.sbuf_tensor("xn_f", [128, 2 * 512], F32))
        mm_in = _es6.enter_context(nc.sbuf_tensor("mm_in", [128, 2 * 512], F32))
        s_f = _es6.enter_context(nc.sbuf_tensor("s_f", [128, 2 * 512], F32))
        o_f = _es6.enter_context(nc.sbuf_tensor("o_f", [128, 2 * 512], F32))
        ps_mt = _es6.enter_context(nc.psum_tensor("ps_mt", [128, 2 * 512], F32))
        ps_gt = _es6.enter_context(nc.psum_tensor("ps_gt", [128, 2 * 512], F32))
        sg_f = _es6.enter_context(nc.sbuf_tensor("sg_f", [128, 2 * 512], F32))
        d_ga = sem("d_ga")
        gp.wait_ge(d_gsp.sem, d_gsp.n)
        v_tb = gp.value_load(tb_sb.ap()[0:1, 0:1])
        v_pb = gp.value_load(tb_sb.ap()[0:1, 1:2])
        g3 = g_all.ap().rearrange("p (q l) -> p q l", q=4)
        gp.dma_start(out=g3[:, :, bass.ds(v_tb, LH)],
                     in_=g_my_sp.rearrange("p (q l) -> p q l", q=4)
                     ).then_inc(d_ga.sem, 16)
        d_ga.n += 16
        gp.dma_start(out=g3[:, :, bass.ds(v_pb, LH)],
                     in_=g_pr_sp.rearrange("p (q l) -> p q l", q=4)
                     ).then_inc(d_ga.sem, 16)
        d_ga.n += 16
        d_xf = d_xa[0]
        d_mf = d_xa[1]
        v_xn = v_st
        t_tf = s_sq
        vo_b = v_st2.n
        v_o = v_st2
        ttf_b = t_tf.n
        for tbi in range(32):
            p = tbi % 2
            if tbi >= 2:
                sy.wait_ge(v_o.sem, (tbi - 1) * 3)
            sy.dma_start(out=x_f[:, p * 512:(p + 1) * 512],
                         in_=x_true[tbi * 128:(tbi + 1) * 128, 0:512]
                         ).then_inc(d_xf.sem, 16)
            d_xf.n += 16
            sy.wait_ge(d_mm.sem, d_mm.n)
            sy.dma_start(out=mm_in[:, p * 512:(p + 1) * 512],
                         in_=mm_sp.rearrange("p (q l) -> p q l", q=4
                                             )[:, :, tbi * 128:(tbi + 1) * 128]
                         ).then_inc(d_mf.sem, 16)
            d_mf.n += 16
            ve.wait_ge(d_xf.sem, d_xf.n)
            if tbi >= 2:
                ve.wait_ge(t_tf.sem, ttf_b + tbi - 1)
            v_xn.inc(ve.tensor_scalar(xn_f[:, p * 512:(p + 1) * 512],
                                      x_f[:, p * 512:(p + 1) * 512],
                                      m_st[:, tbi:tbi + 1],
                                      r_st[:, tbi:tbi + 1],
                                      op0=ALU.subtract, op1=ALU.mult))
            te.wait_ge(d_mf.sem, d_mf.n)
            te.wait_ge(d_ga.sem, 32)
            if tbi >= 2:
                te.wait_ge(v_o.sem, vo_b + (tbi - 1) * 3)
            for q in range(4):
                te.transpose(
                    ps_mt[:, p * 512 + q * 128:p * 512 + (q + 1) * 128],
                    mm_in[:, p * 512 + q * 128:p * 512 + (q + 1) * 128],
                    ident_f[:])
                ins = te.transpose(
                    ps_gt[:, p * 512 + q * 128:p * 512 + (q + 1) * 128],
                    g_all[:, q * L + tbi * 128:q * L + (tbi + 1) * 128],
                    ident_f[:])
            t_tf.inc(ins)
            ve.wait_ge(t_tf.sem, t_tf.n)
            ve.wait_ge(v_xn.sem, v_xn.n)
            v_o.inc(ve.tensor_tensor(sg_f[:, p * 512:(p + 1) * 512],
                                     xn_f[:, p * 512:(p + 1) * 512],
                                     ps_mt[:, p * 512:(p + 1) * 512],
                                     op=ALU.add))
            v_o.inc(ve.tensor_tensor(s_f[:, p * 512:(p + 1) * 512],
                                     sg_f[:, p * 512:(p + 1) * 512],
                                     ps_gt[:, p * 512:(p + 1) * 512],
                                     op=ALU.mult))
            if tbi >= 2:
                ve.wait_ge(d_out.sem, (tbi - 1) * 16)
            v_o.inc(ve.tensor_tensor(o_f[:, p * 512:(p + 1) * 512],
                                     s_f[:, p * 512:(p + 1) * 512],
                                     x_f[:, p * 512:(p + 1) * 512],
                                     op=ALU.add))
            sy.wait_ge(v_o.sem, v_o.n)
            sy.dma_start(out=out[tbi * 128:(tbi + 1) * 128, :],
                         in_=o_f[:, p * 512:(p + 1) * 512]
                         ).then_inc(d_out.sem, 16)
            d_out.n += 16
        gp.wait_ge(d_out.sem, d_out.n)
        gp.wait_ge(ls_loc.sem, ls_loc.n)

        _es6.close()
    nc.compile()
    return nc


def _prep_inputs(inputs):
    xw = {k: np.asarray(v) for k, v in inputs.items()}
    x = xw["L0_attn_x"]
    in_maps = []
    ident = np.eye(128, dtype=np.float32)

    def colmaj(v, nch):
        return np.ascontiguousarray(v.reshape(nch, 128).T)

    for c in range(8):
        b, h = c // 2, c % 2
        perm = (np.arange(1024) if h == 0
                else np.concatenate([np.arange(512, 1024), np.arange(512)]))
        ds = slice(h * 1024, (h + 1) * 1024)
        dsp = slice((1 - h) * 1024, (2 - h) * 1024)
        xb = np.ascontiguousarray(x[b][:, perm])
        inw = np.concatenate([xw["L0_attn_in_w"][perm][:, ds],
                              xw["L0_attn_in_w"][perm][:, 2048 + h * 1024:
                                                       2048 + (h + 1) * 1024]],
                             axis=1)
        m = {
            "x_true": xb,
            "x_my": np.ascontiguousarray(xb[h * 2048:(h + 1) * 2048]),
            "tb": np.array([[h * 2048, (1 - h) * 2048]], np.uint32),
            "ident_in": ident,
            "zeros4": np.zeros((128, 4), np.float32),
            "in_w": np.ascontiguousarray(
                inw.reshape(8, 128, 2048).transpose(1, 0, 2).reshape(128, -1)),
            "norm_w": colmaj(xw["L0_attn_norm_w"][perm], 8),
            "conv_w": np.ascontiguousarray(
                xw["L0_attn_conv_w"][ds, 0, :].reshape(8, 128, 4)
                .transpose(1, 0, 2).reshape(128, 32)),
            "conv_b": colmaj(xw["L0_attn_conv_b"][ds], 8),
            "xproj": np.ascontiguousarray(
                np.pad(xw["L0_attn_xproj_w"][ds], ((0, 0), (0, 32)))
                .reshape(8, 128, 128).transpose(1, 0, 2).reshape(128, -1)),
            "dt_w": np.ascontiguousarray(xw["L0_attn_dt_w"][:, ds]),
            "dt_b": colmaj(xw["L0_attn_dt_b"][ds], 8),
            "log_A": np.ascontiguousarray(
                xw["L0_attn_log_A"][ds].reshape(8, 128, 16)
                .transpose(1, 0, 2).reshape(128, 128)),
            "D": colmaj(xw["L0_attn_D_param"][ds], 8),
            "out_w": np.ascontiguousarray(
                np.concatenate([xw["L0_attn_out_w"][ds],
                                xw["L0_attn_out_w"][dsp]], axis=0
                               )[:, perm[:512]]
                .reshape(16, 128, 512).transpose(1, 0, 2).reshape(128, -1)),
            "enc_w": np.ascontiguousarray(
                xw["L0_ffn_enc_w"][perm].reshape(8, 128, 2048)
                .transpose(1, 0, 2).reshape(128, -1)),
            "enc_b": colmaj(xw["L0_ffn_enc_b"], 16),
            "s2c_w": np.ascontiguousarray(
                xw["L0_ffn_s2c_w"].reshape(16, 128, 64)
                .transpose(1, 0, 2).reshape(128, -1)),
            "s2c_b": np.ascontiguousarray(xw["L0_ffn_s2c_b"][:, None]),
            "r_w1": np.ascontiguousarray(xw["L0_ffn_r_w1"]),
            "r_b1": np.ascontiguousarray(xw["L0_ffn_r_b1"][:, None]),
            "r_w2": np.ascontiguousarray(xw["L0_ffn_r_w2"]),
            "r_b2": np.ascontiguousarray(xw["L0_ffn_r_b2"][:, None]),
            "e_w1": np.ascontiguousarray(
                xw["L0_ffn_e_w1"].transpose(1, 0, 2).reshape(64, -1)),
            "e_b1": np.ascontiguousarray(
                xw["L0_ffn_e_b1"].reshape(8, 8, 128)
                .transpose(2, 0, 1).reshape(128, 64)),
            "e_w2": np.ascontiguousarray(
                xw["L0_ffn_e_w2"].reshape(8, 8, 128, 64)
                .transpose(2, 0, 1, 3).reshape(128, -1)),
            "e_b2": np.ascontiguousarray(xw["L0_ffn_e_b2"].T),
            "c2s_w": np.ascontiguousarray(xw["L0_ffn_c2s_w"]),
            "c2s_b": colmaj(xw["L0_ffn_c2s_b"], 16),
            "dec_w": np.ascontiguousarray(
                xw["L0_ffn_dec_w"][:, perm].reshape(16, 128, 1024)
                .transpose(1, 0, 2).reshape(128, -1)),
            "dec_b": colmaj(xw["L0_ffn_dec_b"][perm], 8),
            "ln_g": colmaj(xw["L0_ffn_ln_g"][perm], 8),
            "ln_b": colmaj(xw["L0_ffn_ln_b"][perm], 8),
        }
        in_maps.append(m)
    return in_maps


def kernel(**inputs):
    in_maps = _prep_inputs(inputs)
    nc = build_program()
    res = run_bass_kernel_spmd(nc, in_maps, core_ids=list(range(8)))
    out = np.empty((4, L, DM), np.float32)
    for c in range(8):
        b, h = c // 2, c % 2
        out[b][:, h * 512:(h + 1) * 512] = res.results[c]["out"]
    return out


# version 11
# speedup vs baseline: 1.1234x; 1.0757x over previous
import numpy as np
import concourse.bass as bass
import concourse.mybir as mybir
from concourse.bacc import Bacc
from concourse.bass_utils import run_bass_kernel_spmd
from concourse import library_config
from contextlib import ExitStack

F32 = mybir.dt.float32
F32R = mybir.dt.float32r
BF16 = mybir.dt.bfloat16
U32 = mybir.dt.uint32
AF = mybir.ActivationFunctionType
ALU = mybir.AluOpType

DM = 1024
DI = 1024
DS = 16
L = 4096
LH = 2048
TT = 512
NT = L // TT
NCH = DI // 128
HID = 2048
NE = 8
RD = [(0, 1)] + [None] * 7
DEBUG_SCRATCH = False


class Ctr:
    __slots__ = ("sem", "n")

    def __init__(self, sem):
        self.sem = sem
        self.n = 0

    def inc(self, ins, k=1):
        ins.then_inc(self.sem, k)
        self.n += k
        return self.n


def _rsend(gp, prep, data_out, data_in, rsem, lsem, sim_inc):
    if data_out is None:
        ins = gp.remote_sem_update_broadcast(rsem.sem, lsem.sem, rdests=RD,
                                             sim_remote_sem_inc=sim_inc)
    else:
        ins = gp.remote_dma_broadcast(data_out, data_in, rsem.sem, lsem.sem,
                                      rdests=RD, sim_remote_sem_inc=sim_inc)
    lsem.n += 16
    prep.inc(ins)
    rsem.n += sim_inc
    gp.wait_ge(prep.sem, prep.n)
    gp.trigger_dma(count=1)


def build_program():
    nc = Bacc("TRN2", target_bir_lowering=False, debug=False)

    def din(name, shape, dt=F32):
        return nc.dram_tensor(name, shape, dt, kind="ExternalInput").ap()

    def dint(name, shape, dt=F32):
        kind = "ExternalOutput" if DEBUG_SCRATCH else "Internal"
        return nc.dram_tensor(name, shape, dt, kind=kind).ap()

    x_true = din("x_true", [L, DM])
    x_my = din("x_my", [LH, DM])
    tb = din("tb", [1, 2], U32)
    ident_in = din("ident_in", [128, 128])
    zeros4 = din("zeros4", [128, 4])
    in_w = din("in_w", [128, NCH * 2048])
    norm_w = din("norm_w", [128, 8])
    conv_w = din("conv_w", [128, NCH * 4])
    conv_b = din("conv_b", [128, 8])
    xproj = din("xproj", [128, NCH * 128])
    dt_w = din("dt_w", [64, DI])
    dt_b = din("dt_b", [128, 8])
    log_A = din("log_A", [128, 128])
    D_in = din("D", [128, 8])
    out_w = din("out_w", [128, 16 * 512])
    enc_w = din("enc_w", [128, 8 * HID])
    enc_b = din("enc_b", [128, 16])
    s2c_w = din("s2c_w", [128, 16 * 64])
    s2c_b = din("s2c_b", [64, 1])
    r_w1 = din("r_w1", [64, 64])
    r_b1 = din("r_b1", [64, 1])
    r_w2 = din("r_w2", [64, 8])
    r_b2 = din("r_b2", [8, 1])
    e_w1 = din("e_w1", [64, NE * 1024])
    e_b1 = din("e_b1", [128, NE * 8])
    e_w2 = din("e_w2", [128, NE * 8 * 64])
    e_b2 = din("e_b2", [64, NE])
    c2s_w = din("c2s_w", [64, HID])
    c2s_b = din("c2s_b", [128, 16])
    dec_w = din("dec_w", [128, 16 * 1024])
    dec_b = din("dec_b", [128, 8])
    ln_g = din("ln_g", [128, 8])
    ln_b = din("ln_b", [128, 8])
    out = nc.dram_tensor("out", [L, 512], F32, kind="ExternalOutput").ap()

    xss_sp = dint("xss_sp", [128, NCH * L])
    sz_sp = dint("sz_sp", [128, NCH * L])
    bc_sp = dint("bc_sp", [96, L])
    yg_my_sp = dint("yg_my_sp", [128, NT * NCH * TT], BF16)
    yg_pr_sp = dint("yg_pr_sp", [128, NT * NCH * TT], BF16)
    mm_sp = dint("mm_sp", [128, 4 * L])
    dec_sp = dint("dec_sp", [128, 8 * LH])
    st2_sp = dint("st2_sp", [2, LH])
    comb_sp = dint("comb_sp", [8, LH])
    lg_sp = dint("lg_sp", [8, LH]) if DEBUG_SCRATCH else None
    spk_sp = dint("spk_sp", [128, 16 * LH], BF16) if DEBUG_SCRATCH else None
    xnf_sp = dint("xnf_sp", [128, 8 * LH], BF16) if DEBUG_SCRATCH else None
    cont_sp = dint("cont_sp", [64, LH], BF16) if DEBUG_SCRATCH else None
    g_my_sp = dint("g_my_sp", [128, 4 * LH])
    g_pr_sp = dint("g_pr_sp", [128, 4 * LH])
    dbg_st = dint("dbg_st", [128, 3 * 32]) if DEBUG_SCRATCH else None
    dbg_st2 = dint("dbg_st2", [128, 64]) if DEBUG_SCRATCH else None
    dbg_xnn = dint("dbg_xnn", [128, NCH * TT]) if DEBUG_SCRATCH else None
    dbg_xnt = dint("dbg_xnt", [128, DM]) if DEBUG_SCRATCH else None
    dbg_xtb = dint("dbg_xtb", [128, DM]) if DEBUG_SCRATCH else None
    dbg_str = dint("dbg_str", [128, NCH * (TT + 4)]) if DEBUG_SCRATCH else None
    dbg_w = dint("dbg_w", [128, 2048]) if DEBUG_SCRATCH else None

    sy, sc, ve, te, gp = nc.sync, nc.scalar, nc.vector, nc.tensor, nc.gpsimd

    es = ExitStack()
    with es:
        def sbuf(name, shape, dt=F32):
            return es.enter_context(nc.sbuf_tensor(name, shape, dt))

        def sem(name):
            return Ctr(es.enter_context(nc.semaphore(name)))

        ident_r = sbuf("ident_r", [128, 128], F32R)
        ident_f = sbuf("ident_f", [128, 128])
        m_st = sbuf("m_st", [128, 32])
        r_st = sbuf("r_st", [128, 32])
        rc_st = sbuf("rc_st", [128, 32])
        sums = sbuf("sums", [128, 32])
        sqs = sbuf("sqs", [128, 32])
        A_neg = sbuf("A_neg", [128, 128])
        D_col = sbuf("D_col", [128, 8])
        cvb_col = sbuf("cvb_col", [128, 8])
        dtb_col = sbuf("dtb_col", [128, 8])
        cw_col = sbuf("cw_col", [128, NCH * 4])
        nw_col = sbuf("nw_col", [128, 8])
        state = sbuf("state", [128, 128])
        tb_sb = sbuf("tb_sb", [1, 2], U32)
        ones_r = sbuf("ones_r", [128, 128], F32R)
        cst = sbuf("cst", [128, 4], F32)
        st_row = sbuf("st_row", [1, 2 * TT], F32)
        lng_c = sbuf("lng_c", [128, 8], F32)
        lnb_c = sbuf("lnb_c", [128, 8], F32)
        ve.memset(cst[:, 0:1], 1e-5)
        ve.memset(cst[:, 1:2], 1e-6)
        ve.memset(cst[:, 2:3], 1.0)
        ve.memset(cst[:, 3:4], 1e-11)

        ph_b = sem("ph_b")
        bar_t = sbuf("bar_t", [1, 8], F32)

        def phase_barrier():
            ve.memset(bar_t[0:1, 0:1], 0.0).then_inc(ph_b.sem, 1)
            sc.activation(bar_t[0:1, 1:2], bar_t[0:1, 2:3], AF.Copy
                          ).then_inc(ph_b.sem, 1)
            gp.memset(bar_t[0:1, 3:4], 0.0).then_inc(ph_b.sem, 1)
            sy.sem_inc(ph_b.sem, 1)
            te.sem_inc(ph_b.sem, 1)
            ph_b.n += 5
            for eng in (sy, sc, ve, te, gp):
                eng.wait_ge(ph_b.sem, ph_b.n)

        rs_xp = sem("rs_xp")
        rs_yg = sem("rs_yg")
        rs_ga = sem("rs_ga")
        rs_rdy = sem("rs_rdy")
        ls_loc = sem("ls_loc")
        prep = sem("prep")
        ack_yg = sem("ack_yg")
        d_w = sem("d_w")
        d_out = sem("d_out")
        d_sp = sem("d_sp")

        gp.load_library(library_config.proxy)

        for t_, s_ in ((tb_sb, tb), (ident_f, ident_in), (A_neg, log_A),
                       (D_col, D_in), (cvb_col, conv_b), (dtb_col, dt_b),
                       (cw_col, conv_w), (nw_col, norm_w)):
            gp.dma_start(out=t_[:], in_=s_).then_inc(d_w.sem, 16)
            d_w.n += 16
        n_w0 = d_w.n
        s_p0 = sem("s_p0")
        A_tmp = sbuf("A_tmp", [128, 128], F32)
        sc.wait_ge(d_w.sem, n_w0)
        s_p0.inc(sc.activation(A_tmp[:], A_neg[:], AF.Exp))
        v_cmp = sem("v_cmp")
        ve.wait_ge(d_w.sem, n_w0)
        ve.tensor_copy(ident_r[:], ident_f[:])
        ve.memset(state[:], 0.0)
        ones_f = sbuf("ones_f", [128, 128], F32)
        v_one = sem("v_one")
        v_one.inc(ve.memset(ones_f[:], 0.0))
        sc.wait_ge(v_one.sem, 1)
        v_one.inc(sc.activation(ones_r[:], ones_f[:], AF.Copy, scale=0.0,
                                bias=1.0))
        ve.wait_ge(s_p0.sem, 1)
        ve.tensor_scalar(A_neg[:], A_tmp[:], -1.0, None, op0=ALU.mult)
        mask_i = sbuf("mask_i", [128, 128], mybir.dt.int32)
        maskf = sbuf("maskf", [128, 128])
        g_cmp = sem("g_cmp")
        gp.iota(mask_i[:], pattern=[[1, 128]], base=0, channel_multiplier=-1)
        g_cmp.inc(gp.tensor_copy(mask_i[:], mask_i[:]))
        ve.wait_ge(g_cmp.sem, g_cmp.n)
        ve.tensor_scalar(maskf[:], mask_i[:], 0, None, op0=ALU.is_equal)

        stats_done = sem("stats_done")
        _es1 = ExitStack()
        xt_a = _es1.enter_context(nc.sbuf_tensor("xt_a", [128, 2 * DM], F32))
        sq_scr = _es1.enter_context(nc.sbuf_tensor("sq_scr", [128, 2 * DM], F32))
        var_t = _es1.enter_context(nc.sbuf_tensor("var_t", [128, 32], F32))
        tmp_t = _es1.enter_context(nc.sbuf_tensor("tmp_t", [128, 32], F32))
        d_xa = [sem("d_xa0"), sem("d_xa1")]
        v_st = sem("v_st")
        s_sq = sem("s_sq")
        for tbi in range(32):
            p = tbi % 2
            if tbi >= 2:
                sy.wait_ge(v_st.sem, 2 * tbi - 3)
                sy.wait_ge(s_sq.sem, tbi - 1)
            sy.dma_start(out=xt_a[:, p * DM:(p + 1) * DM],
                         in_=x_true[tbi * 128:(tbi + 1) * 128, :]
                         ).then_inc(d_xa[p].sem, 16)
            d_xa[p].n += 16
            ve.wait_ge(d_xa[p].sem, d_xa[p].n)
            v_st.inc(ve.tensor_reduce(sums[:, tbi:tbi + 1],
                                      xt_a[:, p * DM:(p + 1) * DM],
                                      axis=mybir.AxisListType.X,
                                      op=ALU.add))
            sc.wait_ge(d_xa[p].sem, d_xa[p].n)
            if tbi >= 2:
                sc.wait_ge(v_st.sem, 2 * tbi - 2)
            s_sq.inc(sc.activation(sq_scr[:, p * DM:(p + 1) * DM],
                                   xt_a[:, p * DM:(p + 1) * DM], AF.Square))
            ve.wait_ge(s_sq.sem, tbi + 1)
            v_st.inc(ve.tensor_reduce(sqs[:, tbi:tbi + 1],
                                      sq_scr[:, p * DM:(p + 1) * DM],
                                      axis=mybir.AxisListType.X,
                                      op=ALU.add))
        ve.wait_ge(s_sq.sem, 32)
        sc.wait_ge(v_st.sem, 64)
        s_st = sem("s_st")
        s_st.inc(sc.activation(m_st[:], sums[:], AF.Copy, scale=1.0 / DM))
        s_st.inc(sc.activation(tmp_t[:], m_st[:], AF.Square))
        s_st.inc(sc.activation(var_t[:], sqs[:], AF.Copy, scale=1.0 / DM))
        gp.wait_ge(s_st.sem, 3)
        g_cmp.inc(gp.tensor_tensor(sums[:], var_t[:], tmp_t[:],
                                   op=ALU.subtract))
        v_st2 = sem("v_st2")
        sc.wait_ge(g_cmp.sem, g_cmp.n)
        v_st2.inc(sc.activation(tmp_t[:], sums[:], AF.Ln, bias=cst[:, 0:1]))
        s_st.inc(sc.activation(r_st[:], tmp_t[:], AF.Exp, scale=-0.5))
        sc.activation(tmp_t[:], sums[:], AF.Ln, scale=1.0 + 1e-6,
                      bias=cst[:, 3:4])
        s_st.inc(sc.activation(rc_st[:], tmp_t[:], AF.Exp, scale=-0.5))
        ve.wait_ge(s_st.sem, 5)
        stats_done.inc(ve.tensor_copy(sqs[:], rc_st[:]))

        _es1.close()
        phase_barrier()
        s_dtr = sem("s_dtr")
        d_bcs = sem("d_bcs")
        _es2 = ExitStack()
        w_stg = _es2.enter_context(nc.sbuf_tensor("w_stg", [128, 2048], F32))
        in_w_r = _es2.enter_context(nc.sbuf_tensor("in_w_r", [128, NCH * 2048], F32R))
        xproj_r = _es2.enter_context(nc.sbuf_tensor("xproj_r", [128, NCH * 128], F32))
        xt_b = _es2.enter_context(nc.sbuf_tensor("xt_b", [128, 2 * DM], F32))
        xnn_t = _es2.enter_context(nc.sbuf_tensor("xnn_t", [128, 2 * DM], F32R))
        xnn_fm = _es2.enter_context(nc.sbuf_tensor("xnn_fm", [128, NCH * TT], F32R))
        strips = _es2.enter_context(nc.sbuf_tensor("strips", [128, NCH * (TT + 4)], F32R))
        xss_f = _es2.enter_context(nc.sbuf_tensor("xss_f", [128, 2 * TT], F32))
        xss_r = _es2.enter_context(nc.sbuf_tensor("xss_r", [128, 2 * TT], F32R))
        sz_f = _es2.enter_context(nc.sbuf_tensor("sz_f", [128, 2 * TT], F32))
        xp_sb = _es2.enter_context(nc.sbuf_tensor("xp_sb", [128, L], F32))
        xp_pr = _es2.enter_context(nc.sbuf_tensor("xp_pr", [128, L], F32))
        diag_w = _es2.enter_context(
            nc.sbuf_tensor("diag_w", [128, NCH * 4 * 128], F32R))
        ps_xz = _es2.enter_context(nc.psum_tensor("ps_xz", [128, 2 * TT], F32))
        ps_tr = _es2.enter_context(nc.psum_tensor("ps_tr", [128, 2 * TT], F32))
        ps_cv = _es2.enter_context(nc.psum_tensor("ps_cv", [128, 2 * TT], F32))
        ps_xp = _es2.enter_context(nc.psum_tensor("ps_xp", [128, TT], F32))
        for c in range(NCH):
            for k in range(4):
                i = c * 4 + k
                v_cmp.inc(ve.tensor_scalar(diag_w[:, i * 128:(i + 1) * 128],
                                           maskf[:], cw_col[:, i:i + 1], None,
                                           op0=ALU.mult))
        s_wp = sem("s_wp")
        v_rnd = sem("v_rnd")
        for c in range(NCH):
            gp.wait_ge(s_wp.sem, c)
            gp.dma_start(out=w_stg[:], in_=in_w[:, c * 2048:(c + 1) * 2048]
                         ).then_inc(d_w.sem, 16)
            d_w.n += 16
            sc.wait_ge(d_w.sem, d_w.n)
            s_wp.inc(sc.activation(in_w_r[:, c * 2048:(c + 1) * 2048],
                                   w_stg[:], AF.Copy,
                                   scale=nw_col[:, c:c + 1]))
        gp.dma_start(out=xproj_r[:], in_=xproj).then_inc(d_w.sem, 16)
        d_w.n += 16
        n_w_p1x = d_w.n
        v_rnd.inc(ve.tensor_copy(bar_t[0:1, 4:5], bar_t[0:1, 5:6]))

        d_xb = d_xa
        v_xnn = sem("v_xnn")
        t_tr = sem("t_tr")
        s_evk = sem("s_evk")
        t_mz = sem("t_mz")
        s_ev2 = sem("s_ev2")
        v_f32r = sem("v_f32r")
        t_cv = sem("t_cv")
        s_cvs = sem("s_cvs")
        v_hl = sem("v_hl")
        t_xp = sem("t_xp")
        s_xpe = sem("s_xpe")

        d_z4 = sem("d_z4")
        for _c in range(NCH):
            gp.dma_start(out=strips[:, _c * (TT + 4):_c * (TT + 4) + 4],
                         in_=zeros4).then_inc(d_z4.sem, 16)
            d_z4.n += 16
        ev2_hist = []
        for j in range(NT):
            p = j % 2
            for t4 in range(4):
                tbi = j * 4 + t4
                pp = t4 % 2
                if tbi >= 2:
                    sy.wait_ge(v_xnn.sem, tbi - 1)
                sy.dma_start(out=xt_b[:, pp * DM:(pp + 1) * DM],
                             in_=x_true[tbi * 128:(tbi + 1) * 128, :]
                             ).then_inc(d_xb[pp].sem, 16)
                d_xb[pp].n += 16
                ve.wait_ge(d_xb[pp].sem, d_xb[pp].n)
                ve.wait_ge(stats_done.sem, 1)
                if tbi >= 2:
                    ve.wait_ge(t_tr.sem, tbi - 1)
                v_xnn.inc(ve.tensor_scalar(
                    xnn_t[:, pp * DM:(pp + 1) * DM],
                    xt_b[:, pp * DM:(pp + 1) * DM],
                    m_st[:, tbi:tbi + 1], rc_st[:, tbi:tbi + 1],
                    op0=ALU.subtract, op1=ALU.mult))
                if DEBUG_SCRATCH and tbi == 12:
                    sy.wait_ge(v_xnn.sem, 1)
                    sy.dma_start(out=dbg_xnt,
                                 in_=xnn_t[:, pp * DM:(pp + 1) * DM
                                           ].bitcast(F32)
                                 ).then_inc(d_bcs.sem, 16)
                    sy.dma_start(out=dbg_xtb, in_=xt_b[:, 0:DM]
                                 ).then_inc(d_bcs.sem, 16)
                    ve.wait_ge(d_bcs.sem, 32)
                te.wait_ge(v_xnn.sem, v_xnn.n)
                if tbi >= 1:
                    te.wait_ge(s_evk.sem, tbi)
                for q in range(8):
                    ins = te.transpose(
                        ps_tr[:, q * 128:(q + 1) * 128].bitcast(F32R),
                        xnn_t[:, pp * DM + q * 128:pp * DM + (q + 1) * 128],
                        ident_r[:])
                t_tr.inc(ins)
                sc.wait_ge(t_tr.sem, t_tr.n)
                if j >= 1 and t4 == 0:
                    sc.wait_ge(t_mz.sem, j * 16)
                for q in range(8):
                    ins = sc.activation(
                        xnn_fm[:, q * TT + t4 * 128:
                               q * TT + (t4 + 1) * 128],
                        ps_tr[:, q * 128:(q + 1) * 128],
                        AF.Copy)
                s_evk.inc(ins)
            te.wait_ge(s_evk.sem, s_evk.n)
            te.wait_ge(s_wp.sem, NCH)
            for m in range(16):
                pb = m % 2
                if len(ev2_hist) >= 2:
                    te.wait_ge(s_ev2.sem, ev2_hist[-2])
                for k in range(NCH):
                    ins = te.matmul(
                        ps_xz[:, pb * TT:(pb + 1) * TT],
                        in_w_r[:, k * 2048 + m * 128:k * 2048 + (m + 1) * 128],
                        xnn_fm[:, k * TT:(k + 1) * TT],
                        start=(k == 0), stop=(k == NCH - 1))
                t_mz.inc(ins)
                sc.wait_ge(t_mz.sem, t_mz.n)
                if m < 8:
                    c = m
                    if j > 0:
                        sc.wait_ge(v_hl.sem, (j - 1) * 8 + c + 1)
                        sc.wait_ge(t_cv.sem, (j - 1) * 8 + c + 1)
                    s_ev2.inc(sc.activation(
                        strips[:, c * (TT + 4) + 4:(c + 1) * (TT + 4)],
                        ps_xz[:, pb * TT:(pb + 1) * TT], AF.Copy))
                else:
                    c = m - 8
                    pz = c % 2
                    if j * 8 + c >= 2:
                        sc.wait_ge(d_sp.sem, d_sp.n)
                    s_ev2.inc(sc.activation(sz_f[:, pz * TT:(pz + 1) * TT],
                                            ps_xz[:, pb * TT:(pb + 1) * TT],
                                            AF.Silu))
                    sy.wait_ge(s_ev2.sem, s_ev2.n)
                    sy.dma_start(
                        out=sz_sp[:, c * L + j * TT:c * L + (j + 1) * TT],
                        in_=sz_f[:, pz * TT:(pz + 1) * TT]
                    ).then_inc(d_sp.sem, 16)
                    d_sp.n += 16
                ev2_hist.append(s_ev2.n)
            if DEBUG_SCRATCH and j == 3:
                d_dbg = d_bcs
                sy.wait_ge(s_ev2.sem, s_ev2.n)
                sy.dma_start(out=dbg_str, in_=strips[:].bitcast(F32)
                             ).then_inc(d_dbg.sem, 16)
                sy.wait_ge(s_wp.sem, NCH)
                sy.dma_start(out=dbg_w, in_=in_w_r[:, 0:2048].bitcast(F32)
                             ).then_inc(d_dbg.sem, 16)
                d_dbg.n += 32
                sy.wait_ge(s_evk.sem, 4)
                sy.dma_start(out=dbg_xnn,
                             in_=xnn_fm[:].bitcast(F32)).then_inc(d_dbg.sem, 16)
                sy.dma_start(out=dbg_st[:, 0:32], in_=m_st[:]).then_inc(d_dbg.sem, 16)
                sy.dma_start(out=dbg_st[:, 32:64], in_=rc_st[:]).then_inc(d_dbg.sem, 16)
                sy.dma_start(out=dbg_st[:, 64:96], in_=r_st[:]).then_inc(d_dbg.sem, 16)
                d_dbg.n += 64
                sc.wait_ge(d_dbg.sem, 64)
            for c in range(NCH):
                pc = c % 2
                te.wait_ge(s_ev2.sem, ev2_hist[j * 16 + c])
                te.wait_ge(d_z4.sem, d_z4.n)
                te.wait_ge(v_cmp.sem, 32)
                if j * 8 + c >= 2:
                    te.wait_ge(s_cvs.sem, j * 8 + c - 1)
                for k in range(4):
                    ins = te.matmul(
                        ps_cv[:, pc * TT:(pc + 1) * TT],
                        diag_w[:, (c * 4 + k) * 128:(c * 4 + k + 1) * 128],
                        strips[:, c * (TT + 4) + k + 1:
                               c * (TT + 4) + k + 1 + TT],
                        start=(k == 0), stop=(k == 3))
                t_cv.inc(ins)
                sc.wait_ge(t_cv.sem, t_cv.n)
                v_hl.inc(sc.activation(
                    strips[:, c * (TT + 4) + 1:c * (TT + 4) + 4],
                    strips.ap().bitcast(F32)[:, c * (TT + 4) + 513:
                                             c * (TT + 4) + 516],
                    AF.Copy))
                sc.wait_ge(t_cv.sem, t_cv.n)
                if j * 8 + c >= 2:
                    sc.wait_ge(d_sp.sem, d_sp.n)
                    sc.wait_ge(t_xp.sem, 8 * j + c - 1)
                s_cvs.inc(sc.activation(xss_f[:, pc * TT:(pc + 1) * TT],
                                        ps_cv[:, pc * TT:(pc + 1) * TT],
                                        AF.Silu, bias=cvb_col[:, c:c + 1]))
                sy.wait_ge(s_cvs.sem, s_cvs.n)
                sy.dma_start(
                    out=xss_sp[:, c * L + j * TT:c * L + (j + 1) * TT],
                    in_=xss_f[:, pc * TT:(pc + 1) * TT]
                ).then_inc(d_sp.sem, 16)
                d_sp.n += 16
                v_f32r.inc(ve.tensor_copy(bar_t[0:1, 4:5],
                                           bar_t[0:1, 5:6]))
                te.wait_ge(s_cvs.sem, s_cvs.n)
                te.wait_ge(d_w.sem, n_w_p1x)
                if c == 0 and j > 0:
                    te.wait_ge(s_xpe.sem, j)
                t_xp.inc(te.matmul(ps_xp[:],
                                   xproj_r[:, c * 128:(c + 1) * 128],
                                   xss_f[:, pc * TT:(pc + 1) * TT],
                                   start=(c == 0), stop=(c == NCH - 1)))
            sc.wait_ge(t_xp.sem, t_xp.n)
            s_xpe.inc(sc.activation(xp_sb[:, j * TT:(j + 1) * TT],
                                    ps_xp[:], AF.Copy))

        gp.wait_ge(s_xpe.sem, NT)
        _rsend(gp, prep, None, None, rs_rdy, ls_loc, 2)
        gp.wait_ge(rs_rdy.sem, 2)
        _rsend(gp, prep, xp_pr[:], xp_sb[:], rs_xp, ls_loc, 2)
        ve.wait_ge(rs_xp.sem, 2)
        v_xpadd = sem("v_xpadd")
        v_xpadd.inc(ve.tensor_tensor(xp_pr[:], xp_sb[:], xp_pr[:],
                                     op=ALU.add))
        sy.wait_ge(v_xpadd.sem, 1)
        sy.dma_start(out=bc_sp[:, :], in_=xp_pr[0:96, :]
                     ).then_inc(d_bcs.sem, 16)
        d_bcs.n += 16
        gp.wait_ge(ls_loc.sem, ls_loc.n)

        _es2.close()
        phase_barrier()
        d_ygs = sem("d_ygs")
        _es3 = ExitStack()
        Brep = _es3.enter_context(nc.sbuf_tensor("Brep", [128, DS * TT], F32))
        Crep = _es3.enter_context(nc.sbuf_tensor("Crep", [128, DS * TT], F32))
        del_c = _es3.enter_context(nc.sbuf_tensor("del_c", [128, 2 * TT], F32))
        dx_c = _es3.enter_context(nc.sbuf_tensor("dx_c", [128, 2 * TT], F32))
        xs_c = _es3.enter_context(nc.sbuf_tensor("xs_c", [128, 2 * TT], F32))
        sz_c = _es3.enter_context(nc.sbuf_tensor("sz_c", [128, 2 * TT], F32))
        sp_a = _es3.enter_context(nc.sbuf_tensor("sp_a", [128, 2 * TT], F32))
        sp_l = _es3.enter_context(nc.sbuf_tensor("sp_l", [128, 2 * TT], F32))
        dA_b = _es3.enter_context(nc.sbuf_tensor("dA_b", [128, 4 * TT], F32))
        u_b = _es3.enter_context(nc.sbuf_tensor("u_b", [128, 2 * TT], F32))
        h_b = _es3.enter_context(nc.sbuf_tensor("h_b", [128, 4 * TT], F32))
        p_b = _es3.enter_context(nc.sbuf_tensor("p_b", [128, 2 * TT], F32R))
        p_g = _es3.enter_context(nc.sbuf_tensor("p_g", [128, 2 * TT], F32))
        yg_t = _es3.enter_context(nc.sbuf_tensor("yg_t", [128, 2 * NCH * TT], BF16))
        yg_rx = _es3.enter_context(nc.sbuf_tensor("yg_rx", [128, 2 * NCH * TT], BF16))
        ps_d = _es3.enter_context(nc.psum_tensor("ps_d", [128, TT], F32))
        ps_y = _es3.enter_context(nc.psum_tensor("ps_y", [128, 2 * TT], F32))
        dtw_r = _es3.enter_context(nc.sbuf_tensor("dtw_r", [64, DI], F32))
        dt_r = _es3.enter_context(nc.sbuf_tensor("dt_r", [64, L], F32))
        dt_stg = _es3.enter_context(nc.sbuf_tensor("dt_stg", [64, L], F32))
        d_dt = sem("d_dt")
        v_dtw = sem("v_dtw")
        gp.dma_start(out=dt_stg[0:64, 0:DI], in_=dt_w).then_inc(d_dt.sem, 16)
        d_dt.n += 16
        ve.wait_ge(d_dt.sem, 16)
        v_dtw.inc(ve.tensor_copy(dtw_r[:], dt_stg[0:64, 0:DI]))
        gp.wait_ge(d_bcs.sem, 16)
        gp.wait_ge(v_dtw.sem, 1)
        gp.dma_start(out=dt_stg[:], in_=bc_sp[0:64, :]).then_inc(d_dt.sem, 16)
        d_dt.n += 16
        ve.wait_ge(d_dt.sem, 32)
        s_dtr.n = 0
        s_dtr.inc(ve.tensor_copy(dt_r[:], dt_stg[:]))
        d_bc = sem("d_bc")
        d_dx = [sem("d_dx0"), sem("d_dx1")]
        s_dA = sem("s_dA")
        g_u = sem("g_u")
        v_h = sem("v_h")
        s_stc = sem("s_stc")
        v_p = sem("v_p")
        t_y = sem("t_y")
        t_dl = sem("t_dl")
        s_sp4 = sem("s_sp4")
        v_dl = sem("v_dl")
        v_dx = sem("v_dx")
        v_gt = sem("v_gt")
        d_ygd = sem("d_ygd")

        _rsend(gp, prep, None, None, ack_yg, ls_loc, 4)

        NTOT = NT * NCH * DS
        bcs_base = d_bcs.n

        def bcB_thr(j2, s2):
            return (j2 * DS + s2 + 1) * 16

        def bcC_thr(j2, s2):
            return bcs_base + (j2 * DS + s2 + 1) * 16

        def emit_bc(j2):
            for s2 in range(DS):
                if j2 == 0 and s2 == 0:
                    sy.wait_ge(d_bcs.sem, bcs_base)
                if j2 > 0:
                    sy.wait_ge(v_h.sem, (j2 - 1) * 128 + 7 * DS + s2 + 1)
                    sy.wait_ge(v_p.sem, (j2 - 1) * 128 + 7 * DS + s2 + 1)
                sy.dma_start(
                    out=Brep[:, s2 * TT:(s2 + 1) * TT],
                    in_=bc_sp[64 + s2, j2 * TT:(j2 + 1) * TT
                              ].partition_broadcast(128)
                ).then_inc(d_bc.sem, 16)
                d_bc.n += 16
                sy.dma_start(
                    out=Crep[:, s2 * TT:(s2 + 1) * TT],
                    in_=bc_sp[80 + s2, j2 * TT:(j2 + 1) * TT
                              ].partition_broadcast(128)
                ).then_inc(d_bcs.sem, 16)
                d_bcs.n += 16

        def emit_loads(jc2):
            j2, c2 = divmod(jc2, NCH)
            pc2 = c2 % 2
            if jc2 >= 2:
                sy.wait_ge(v_h.sem, (jc2 - 1) * DS)
                sy.wait_ge(v_gt.sem, jc2 - 1)
            sy.dma_start(out=xs_c[:, pc2 * TT:(pc2 + 1) * TT],
                         in_=xss_sp[:, c2 * L + j2 * TT:c2 * L + (j2 + 1) * TT]
                         ).then_inc(d_dx[pc2].sem, 16)
            d_dx[pc2].n += 16
            sy.dma_start(out=sz_c[:, pc2 * TT:(pc2 + 1) * TT],
                         in_=sz_sp[:, c2 * L + j2 * TT:c2 * L + (j2 + 1) * TT]
                         ).then_inc(d_dx[pc2].sem, 16)
            d_dx[pc2].n += 16

        def emit_prep(jc2):
            j2, c2 = divmod(jc2, NCH)
            pc2 = c2 % 2
            te.wait_ge(s_dtr.sem, 1)
            if jc2 >= 1:
                te.wait_ge(s_sp4.sem, 2 * jc2)
            t_dl.inc(te.matmul(ps_d[:], dtw_r[:, c2 * 128:(c2 + 1) * 128],
                               dt_r[:, j2 * TT:(j2 + 1) * TT],
                               start=True, stop=True))
            sc.wait_ge(t_dl.sem, t_dl.n)
            if jc2 >= 2:
                sc.wait_ge(v_dl.sem, jc2 - 1)
            sc.activation(sp_a[:, pc2 * TT:(pc2 + 1) * TT], ps_d[:],
                          AF.Abs, bias=dtb_col[:, c2:c2 + 1])
            sc.activation(sp_l[:, pc2 * TT:(pc2 + 1) * TT],
                          sp_a[:, pc2 * TT:(pc2 + 1) * TT],
                          AF.Exp, scale=-1.0)
            s_sp4.inc(sc.activation(sp_a[:, pc2 * TT:(pc2 + 1) * TT],
                                    sp_l[:, pc2 * TT:(pc2 + 1) * TT],
                                    AF.Ln, bias=cst[:, 2:3]))
            s_sp4.inc(sc.activation(sp_l[:, pc2 * TT:(pc2 + 1) * TT],
                                    ps_d[:], AF.Relu,
                                    bias=dtb_col[:, c2:c2 + 1]))
            ve.wait_ge(s_sp4.sem, s_sp4.n)
            if jc2 >= 2:
                ve.wait_ge(s_dA.sem, (jc2 - 1) * DS)
            v_dl.inc(ve.tensor_tensor(del_c[:, pc2 * TT:(pc2 + 1) * TT],
                                      sp_l[:, pc2 * TT:(pc2 + 1) * TT],
                                      sp_a[:, pc2 * TT:(pc2 + 1) * TT],
                                      op=ALU.add))
            ve.wait_ge(d_dx[pc2].sem, d_dx[pc2].n)
            v_dx.inc(ve.tensor_tensor(dx_c[:, pc2 * TT:(pc2 + 1) * TT],
                                      del_c[:, pc2 * TT:(pc2 + 1) * TT],
                                      xs_c[:, pc2 * TT:(pc2 + 1) * TT],
                                      op=ALU.mult))

        def emit_dA(it2):
            jc2 = it2 // DS
            c2 = jc2 % NCH
            sl = it2 % 4
            col2 = c2 * DS + (it2 % DS)
            if it2 >= 4:
                sc.wait_ge(v_h.sem, it2 - 3)
            sc.wait_ge(v_dl.sem, jc2 + 1)
            s_dA.inc(sc.activation(dA_b[:, sl * TT:(sl + 1) * TT],
                                   del_c[:, (c2 % 2) * TT:(c2 % 2) * TT + TT],
                                   AF.Exp, scale=A_neg[:, col2:col2 + 1]))

        for j in range(NT):
            if j == 0:
                emit_loads(0)
                emit_loads(1)
                emit_bc(0)
                emit_prep(0)
                for w in range(3):
                    emit_dA(w)
            for c in range(NCH):
                jc = j * NCH + c
                pc = c % 2
                for s in range(DS):
                    it = jc * DS + s
                    pb = it % 2
                    col = c * DS + s
                    if it + 3 < NTOT:
                        emit_dA(it + 3)
                    hl = it % 4
                    ve.wait_ge(d_bc.sem, bcB_thr(j, s))
                    ve.tensor_tensor(u_b[:, pb * TT:(pb + 1) * TT],
                                     dx_c[:, pc * TT:(pc + 1) * TT],
                                     Brep[:, s * TT:(s + 1) * TT],
                                     op=ALU.mult)
                    ve.wait_ge(s_dA.sem, it + 1)
                    if it >= 4:
                        ve.wait_ge(s_stc.sem, it - 3)
                    if it >= 2:
                        ve.wait_ge(t_y.sem, it - 1)
                    v_h.inc(ve.tensor_tensor_scan(
                        h_b[:, hl * TT:(hl + 1) * TT],
                        dA_b[:, (it % 4) * TT:(it % 4) * TT + TT],
                        u_b[:, pb * TT:(pb + 1) * TT],
                        state[:, col:col + 1],
                        op0=ALU.mult, op1=ALU.add))
                    if it >= 3:
                        itp = it - 3
                        hlp = itp % 4
                        colp = ((itp // DS) % NCH) * DS + itp % DS
                        sc.wait_ge(v_h.sem, it + 1)
                        s_stc.inc(sc.activation(
                            state[:, colp:colp + 1],
                            h_b[:, hlp * TT + TT - 1:hlp * TT + TT], AF.Copy))
                    ve.wait_ge(d_bcs.sem, bcC_thr(j, s))
                    v_p.inc(ve.tensor_tensor(
                        p_b[:, pb * TT:(pb + 1) * TT],
                        h_b[:, hl * TT:(hl + 1) * TT],
                        Crep[:, s * TT:(s + 1) * TT],
                        op=ALU.mult))
                    te.wait_ge(v_p.sem, it + 1)
                    if s == 0 and jc >= 2:
                        te.wait_ge(v_gt.sem, jc - 1)
                    t_y.inc(te.matmul(ps_y[:, pc * TT:(pc + 1) * TT],
                                      ident_r[:],
                                      p_b[:, pb * TT:(pb + 1) * TT],
                                      start=(s == 0), stop=(s == DS - 1)))
                    if s == 7 and jc + 1 < NT * NCH:
                        emit_prep(jc + 1)
                ve.wait_ge(t_y.sem, (jc + 1) * DS)
                ve.scalar_tensor_tensor(dx_c[:, pc * TT:(pc + 1) * TT],
                                        xs_c[:, pc * TT:(pc + 1) * TT],
                                        D_col[:, c:c + 1],
                                        ps_y[:, pc * TT:(pc + 1) * TT],
                                        op0=ALU.mult, op1=ALU.add)
                if j >= 2 and c == 0:
                    ve.wait_ge(d_ygs.sem, (j - 1) * 16)
                    ve.wait_ge(ls_loc.sem, 16 + (j - 1) * 16)
                v_gt.inc(ve.tensor_tensor(
                    yg_t[:, (j % 2) * NCH * TT + c * TT:
                         (j % 2) * NCH * TT + (c + 1) * TT],
                    dx_c[:, pc * TT:(pc + 1) * TT],
                    sz_c[:, pc * TT:(pc + 1) * TT], op=ALU.mult))
                if jc + 2 < NT * NCH:
                    emit_loads(jc + 2)
                if c == 6 and j + 1 < NT:
                    emit_bc(j + 1)
            jm = j % 2
            sy.wait_ge(v_gt.sem, (j + 1) * NCH)
            sy.dma_start(out=yg_my_sp[:, j * NCH * TT:(j + 1) * NCH * TT],
                         in_=yg_t[:, jm * NCH * TT:(jm + 1) * NCH * TT]
                         ).then_inc(d_ygs.sem, 16)
            d_ygs.n += 16
            gp.wait_ge(v_gt.sem, (j + 1) * NCH)
            if j >= 2:
                gp.wait_ge(ack_yg.sem, 4 + 2 * (j - 1))
            _rsend(gp, prep,
                   yg_rx[:, jm * NCH * TT:(jm + 1) * NCH * TT],
                   yg_t[:, jm * NCH * TT:(jm + 1) * NCH * TT],
                   rs_yg, ls_loc, 2)
            gp.wait_ge(rs_yg.sem, 2 * (j + 1))
            gp.dma_start(out=yg_pr_sp[:, j * NCH * TT:(j + 1) * NCH * TT],
                         in_=yg_rx[:, jm * NCH * TT:(jm + 1) * NCH * TT]
                         ).then_inc(d_ygd.sem, 16)
            d_ygd.n += 16
            gp.wait_ge(d_ygd.sem, d_ygd.n)
            _rsend(gp, prep, None, None, ack_yg, ls_loc, 2)

        _es3.close()
        phase_barrier()
        d_mm = sem("d_mm")
        _es4 = ExitStack()
        ow_bf = _es4.enter_context(nc.sbuf_tensor("ow_bf", [128, 16 * 512], BF16))
        yg_in = _es4.enter_context(nc.sbuf_tensor("yg_in", [128, 2 * 16 * TT], BF16))
        mm_f = _es4.enter_context(nc.sbuf_tensor("mm_f", [128, 2 * 4 * TT], F32))
        ps_mm = _es4.enter_context(nc.psum_tensor("ps_mm", [128, 2 * TT], F32))
        gp.dma_start(out=ow_bf[:], in_=out_w).then_inc(d_w.sem, 16)
        d_w.n += 16
        n_w_ow = d_w.n
        d_ygr = [sem("d_ygr0"), sem("d_ygr1")]
        t_mm = sem("t_mm")
        s_mme = sem("s_mme")
        for j in range(NT):
            jp = j % 2
            if j >= 2:
                sy.wait_ge(t_mm.sem, (j - 1) * 4)
            sy.wait_ge(d_ygs.sem, 16 * (j + 1))
            sy.dma_start(out=yg_in[:, jp * 16 * TT:jp * 16 * TT + 8 * TT],
                         in_=yg_my_sp[:, j * NCH * TT:(j + 1) * NCH * TT]
                         ).then_inc(d_ygr[jp].sem, 16)
            d_ygr[jp].n += 16
            sy.dma_start(out=yg_in[:, jp * 16 * TT + 8 * TT:
                                   (jp + 1) * 16 * TT],
                         in_=yg_pr_sp[:, j * NCH * TT:(j + 1) * NCH * TT]
                         ).then_inc(d_ygr[jp].sem, 16)
            d_ygr[jp].n += 16
            te.wait_ge(d_ygr[jp].sem, d_ygr[jp].n)
            te.wait_ge(d_w.sem, n_w_ow)
            for m in range(4):
                pm = m % 2
                if j * 4 + m >= 2:
                    te.wait_ge(s_mme.sem, j * 4 + m - 1)
                for k in range(16):
                    ins = te.matmul(
                        ps_mm[:, pm * TT:(pm + 1) * TT],
                        ow_bf[:, k * 512 + m * 128:k * 512 + (m + 1) * 128],
                        yg_in[:, jp * 16 * TT + k * TT:
                              jp * 16 * TT + (k + 1) * TT],
                        start=(k == 0), stop=(k == 15))
                t_mm.inc(ins)
                sc.wait_ge(t_mm.sem, t_mm.n)
                if j >= 2 and m == 0:
                    sc.wait_ge(d_mm.sem, (j - 1) * 16)
                s_mme.inc(sc.activation(
                    mm_f[:, (jp * 4 + m) * TT:(jp * 4 + m + 1) * TT],
                    ps_mm[:, pm * TT:(pm + 1) * TT], AF.Copy))
            sy.wait_ge(s_mme.sem, s_mme.n)
            sy.dma_start(
                out=mm_sp.rearrange("p (q l) -> p q l", q=4
                                    )[:, :, j * TT:(j + 1) * TT],
                in_=mm_f[:, jp * 4 * TT:(jp + 1) * 4 * TT
                         ].rearrange("p (q t) -> p q t", q=4)
            ).then_inc(d_mm.sem, 16)
            d_mm.n += 16

        _es4.close()
        phase_barrier()
        d_gsp = sem("d_gsp")
        _es5 = ExitStack()
        enc_bf = _es5.enter_context(nc.sbuf_tensor("enc_bf", [128, 8 * HID], BF16))
        dec_bf = _es5.enter_context(nc.sbuf_tensor("dec_bf", [128, 16 * 1024], BF16))
        ew1_bf = _es5.enter_context(nc.sbuf_tensor("ew1_bf", [64, NE * 1024], BF16))
        ew2_bf = _es5.enter_context(nc.sbuf_tensor("ew2_bf", [128, NE * 8 * 64], BF16))
        s2c_bf = _es5.enter_context(nc.sbuf_tensor("s2c_bf", [128, 16 * 64], BF16))
        c2s_bf = _es5.enter_context(nc.sbuf_tensor("c2s_bf", [64, HID], BF16))
        r1_bf = _es5.enter_context(nc.sbuf_tensor("r1_bf", [64, 64], BF16))
        r2_bf = _es5.enter_context(nc.sbuf_tensor("r2_bf", [64, 8], BF16))
        encb_c = _es5.enter_context(nc.sbuf_tensor("encb_c", [128, 16], F32))
        s2cb_c = _es5.enter_context(nc.sbuf_tensor("s2cb_c", [64, 1], F32))
        rb1_c = _es5.enter_context(nc.sbuf_tensor("rb1_c", [64, 1], F32))
        rb2_c = _es5.enter_context(nc.sbuf_tensor("rb2_c", [8, 1], F32))
        eb1_c = _es5.enter_context(nc.sbuf_tensor("eb1_c", [128, NE * 8], F32))
        eb2_c = _es5.enter_context(nc.sbuf_tensor("eb2_c", [64, NE], F32))
        c2sb_c = _es5.enter_context(nc.sbuf_tensor("c2sb_c", [128, 16], F32))
        decb_c = _es5.enter_context(nc.sbuf_tensor("decb_c", [128, 8], F32))
        sums2 = _es5.enter_context(nc.sbuf_tensor("sums2", [128, 16], F32))
        sqs2 = _es5.enter_context(nc.sbuf_tensor("sqs2", [128, 16], F32))
        m2_st = _es5.enter_context(nc.sbuf_tensor("m2_st", [128, 16], F32))
        r2_st = _es5.enter_context(nc.sbuf_tensor("r2_st", [128, 16], F32))
        sq_sc2 = _es5.enter_context(nc.sbuf_tensor("sq_sc2", [128, 2 * DM], F32))
        xm_t = _es5.enter_context(nc.sbuf_tensor("xm_t", [128, 2 * DM], F32))
        xn2_t = _es5.enter_context(nc.sbuf_tensor("xn2_t", [128, 2 * DM], F32R))
        xnf_bf = _es5.enter_context(nc.sbuf_tensor("xnf_bf", [128, 8 * TT], BF16))
        spk_bf = _es5.enter_context(nc.sbuf_tensor("spk_bf", [128, 16 * TT], BF16))
        cont_bf = _es5.enter_context(nc.sbuf_tensor("cont_bf", [64, 2 * TT], BF16))
        rh_bf = _es5.enter_context(nc.sbuf_tensor("rh_bf", [64, 2 * TT], BF16))
        lgits = _es5.enter_context(nc.sbuf_tensor("lgits", [8, 2 * TT], F32))
        comb = _es5.enter_context(nc.sbuf_tensor("comb", [8, 2 * TT], F32))
        lg_sb = _es5.enter_context(nc.sbuf_tensor("lg_sb", [128, 4 * 8], F32))
        rt_m = _es5.enter_context(nc.sbuf_tensor("rt_m", [128, 4 * 8], F32))
        rt_g = _es5.enter_context(nc.sbuf_tensor("rt_g", [128, 4 * 8], F32))
        rt_s = _es5.enter_context(nc.sbuf_tensor("rt_s", [128, 4 * 4], F32))
        eh_bf = _es5.enter_context(nc.sbuf_tensor("eh_bf", [128, 2 * TT], BF16))
        crep_sb = _es5.enter_context(nc.sbuf_tensor("crep_sb", [64, 2 * TT], F32))
        pe_sb = _es5.enter_context(nc.sbuf_tensor("pe_sb", [64, 2 * TT], F32))
        moe_f = _es5.enter_context(nc.sbuf_tensor("moe_f", [64, 2 * TT], F32))
        moe_bf = _es5.enter_context(nc.sbuf_tensor("moe_bf", [64, 2 * TT], BF16))
        rat_bf = _es5.enter_context(nc.sbuf_tensor("rat_bf", [128, 16 * TT], BF16))
        dec_f = _es5.enter_context(nc.sbuf_tensor("dec_f", [128, 2 * TT], F32))
        dec_r = _es5.enter_context(nc.sbuf_tensor("dec_r", [128, 2 * TT], F32R))
        ps_a = _es5.enter_context(nc.psum_tensor("ps_a", [128, 2 * TT], F32))
        ps_eo = _es5.enter_context(nc.psum_tensor("ps_eo", [128, TT], F32))
        ps_cr = _es5.enter_context(nc.psum_tensor("ps_cr", [128, TT], F32))
        ps_r = _es5.enter_context(nc.psum_tensor("ps_r", [128, 2 * TT], F32))
        ps_rep = _es5.enter_context(nc.psum_tensor("ps_rep", [128, 2 * TT], F32))
        for t_, s_ in ((enc_bf, enc_w), (dec_bf, dec_w), (ew1_bf, e_w1),
                       (ew2_bf, e_w2), (s2c_bf, s2c_w), (c2s_bf, c2s_w),
                       (r1_bf, r_w1), (r2_bf, r_w2), (encb_c, enc_b),
                       (s2cb_c, s2c_b), (rb1_c, r_b1), (rb2_c, r_b2),
                       (eb1_c, e_b1), (eb2_c, e_b2), (c2sb_c, c2s_b),
                       (decb_c, dec_b), (lng_c, ln_g), (lnb_c, ln_b)):
            gp.dma_start(out=t_[:], in_=s_).then_inc(d_w.sem, 16)
            d_w.n += 16
        n_w_ffn = d_w.n

        d_xm = d_xa
        v_s2 = sem("v_s2")
        s_q2 = sem("s_q2")
        for tbi in range(16):
            p = tbi % 2
            if tbi >= 2:
                sy.wait_ge(v_s2.sem, 2 * tbi - 3)
                sy.wait_ge(s_q2.sem, tbi - 1)
            sy.dma_start(out=xm_t[:, p * DM:(p + 1) * DM],
                         in_=x_my[tbi * 128:(tbi + 1) * 128, :]
                         ).then_inc(d_xm[p].sem, 16)
            d_xm[p].n += 16
            ve.wait_ge(d_xm[p].sem, d_xm[p].n)
            v_s2.inc(ve.tensor_reduce(sums2[:, tbi:tbi + 1],
                                      xm_t[:, p * DM:(p + 1) * DM],
                                      axis=mybir.AxisListType.X,
                                      op=ALU.add))
            sc.wait_ge(d_xm[p].sem, d_xm[p].n)
            if tbi >= 2:
                sc.wait_ge(v_s2.sem, 2 * tbi - 2)
            s_q2.inc(sc.activation(sq_sc2[:, p * DM:(p + 1) * DM],
                                   xm_t[:, p * DM:(p + 1) * DM], AF.Square))
            ve.wait_ge(s_q2.sem, tbi + 1)
            v_s2.inc(ve.tensor_reduce(sqs2[:, tbi:tbi + 1],
                                      sq_sc2[:, p * DM:(p + 1) * DM],
                                      axis=mybir.AxisListType.X,
                                      op=ALU.add))
        ve.wait_ge(s_q2.sem, 16)
        sc.wait_ge(v_s2.sem, 32)
        v_s2b = sem("v_s2b")
        v_s2b.inc(sc.activation(m2_st[:], sums2[:], AF.Copy, scale=1.0 / DM))
        v_s2b.inc(sc.activation(r2_st[:], m2_st[:], AF.Square))
        v_s2b.inc(sc.activation(sq_sc2[:, 0:16], sqs2[:], AF.Copy,
                                scale=1.0 / DM))
        gp.wait_ge(v_s2b.sem, 3)
        g_cmp.inc(gp.tensor_tensor(sums2[:], sq_sc2[:, 0:16], r2_st[:],
                                   op=ALU.subtract))
        sc.wait_ge(g_cmp.sem, g_cmp.n)
        sc.activation(sq_sc2[:, 16:32], sums2[:], AF.Ln, bias=cst[:, 0:1])
        s_st2 = sem("s_st2")
        s_st2.inc(sc.activation(r2_st[:], sq_sc2[:, 16:32], AF.Exp,
                                scale=-0.5))
        if DEBUG_SCRATCH:
            ve.wait_ge(s_st2.sem, 1)
            ve.tensor_copy(sq_sc2[:, 64:80], m2_st[:])
            ve.tensor_copy(sq_sc2[:, 80:96], r2_st[:])
            ve.tensor_copy(sq_sc2[:, 96:112], sums2[:])
            v_s2b.inc(ve.tensor_copy(sq_sc2[:, 112:128], sqs2[:]))
            gp.wait_ge(v_s2b.sem, 4)
            gp.dma_start(out=dbg_st2[:, :], in_=sq_sc2[:, 64:128]
                         ).then_inc(d_xm[0].sem, 16)
            d_xm[0].n += 16
            ve.wait_ge(d_xm[0].sem, d_xm[0].n)

        v_xn2 = sem("v_xn2")
        t_tr2 = sem("t_tr2")
        s_ek2 = sem("s_ek2")
        t_enc = sem("t_enc")
        s_spk = sem("s_spk")
        t_s2c = sem("t_s2c")
        s_cnt = sem("s_cnt")
        t_rt = sem("t_rt")
        s_rt = sem("s_rt")
        v_rt = sem("v_rt")
        s_cmb = sem("s_cmb")
        t_eh = sem("t_eh")
        d_cmb = sem("d_cmb")
        d_crp = sem("d_crp")
        s_eh = sem("s_eh")
        t_eo = sem("t_eo")
        s_cre = sem("s_cre")
        v_moe = sem("v_moe")
        s_mbf = sem("s_mbf")
        t_c2s = sem("t_c2s")
        s_rat = sem("s_rat")
        t_dec = sem("t_dec")
        s_dec = sem("s_dec")
        s_dcr = sem("s_dcr")
        t_sum = sem("t_sum")
        v_lnf = sem("v_lnf")
        d_st = sem("d_st")
        d_dcs = sem("d_dcs")
        NTH = LH // TT
        for jj in range(NTH):
            pf = jj % 2
            for t4 in range(4):
                tbi = jj * 4 + t4
                pp = t4 % 2
                if v_xn2.n < 2:
                    sy.wait_ge(v_s2.sem, 32)
                    sy.wait_ge(s_q2.sem, 16)
                else:
                    sy.wait_ge(v_xn2.sem, v_xn2.n - 1)
                sy.dma_start(out=xm_t[:, pp * DM:(pp + 1) * DM],
                             in_=x_my[tbi * 128:(tbi + 1) * 128, :]
                             ).then_inc(d_xm[pp].sem, 16)
                d_xm[pp].n += 16
                ve.wait_ge(d_xm[pp].sem, d_xm[pp].n)
                ve.wait_ge(s_st2.sem, 1)
                if v_xn2.n >= 2:
                    ve.wait_ge(t_tr2.sem, 2 * (v_xn2.n - 1))
                v_xn2.inc(ve.tensor_scalar(
                    xn2_t[:, pp * DM:(pp + 1) * DM],
                    xm_t[:, pp * DM:(pp + 1) * DM],
                    m2_st[:, tbi:tbi + 1], r2_st[:, tbi:tbi + 1],
                    op0=ALU.subtract, op1=ALU.mult))
                te.wait_ge(v_xn2.sem, v_xn2.n)
                if t_tr2.n >= 2:
                    te.wait_ge(s_ek2.sem, t_tr2.n - 1)
                for q in range(8):
                    ins = te.transpose(
                        ps_rep[:, pp * TT + ((q % 4) * 128):
                               pp * TT + ((q % 4) + 1) * 128].bitcast(F32R)
                        if q < 4 else
                        ps_rep[:, pp * TT + ((q - 4) * 128):
                               pp * TT + ((q - 4) + 1) * 128].bitcast(F32R),
                        xn2_t[:, pp * DM + q * 128:pp * DM + (q + 1) * 128],
                        ident_r[:])
                    if q == 3 or q == 7:
                        t_tr2.inc(ins)
                    if q == 3:
                        sc.wait_ge(t_tr2.sem, t_tr2.n)
                        if jj >= 1 and t4 == 0:
                            sc.wait_ge(t_enc.sem, jj * 16)
                        for q2 in range(4):
                            ins2 = sc.activation(
                                xnf_bf[:, q2 * TT + t4 * 128:
                                       q2 * TT + (t4 + 1) * 128],
                                ps_rep[:, pp * TT + q2 * 128:
                                       pp * TT + (q2 + 1) * 128],
                                AF.Copy)
                        s_ek2.inc(ins2)
                        te.wait_ge(s_ek2.sem, s_ek2.n)
                sc.wait_ge(t_tr2.sem, t_tr2.n)
                if jj >= 1 and t4 == 0:
                    sc.wait_ge(t_enc.sem, jj * 16)
                for q2 in range(4):
                    ins2 = sc.activation(
                        xnf_bf[:, (4 + q2) * TT + t4 * 128:
                               (4 + q2) * TT + (t4 + 1) * 128],
                        ps_rep[:, pp * TT + q2 * 128:
                               pp * TT + (q2 + 1) * 128],
                        AF.Copy)
                s_ek2.inc(ins2)
            te.wait_ge(s_ek2.sem, s_ek2.n)
            te.wait_ge(d_w.sem, n_w_ffn)
            for m in range(16):
                pb = m % 2
                if m == 0 and jj >= 1:
                    te.wait_ge(s_dec.sem, 8 * jj)
                if jj * 16 + m >= 2:
                    te.wait_ge(s_spk.sem, jj * 16 + m - 1)
                for k in range(8):
                    ins = te.matmul(
                        ps_a[:, pb * TT:(pb + 1) * TT],
                        enc_bf[:, k * HID + m * 128:k * HID + (m + 1) * 128],
                        xnf_bf[:, k * TT:(k + 1) * TT],
                        start=(k == 0), stop=(k == 7))
                t_enc.inc(ins)
                sc.wait_ge(t_enc.sem, t_enc.n)
                if jj >= 1 and m == 0:
                    sc.wait_ge(t_s2c.sem, jj)
                s_spk.inc(sc.activation(spk_bf[:, m * TT:(m + 1) * TT],
                                        ps_a[:, pb * TT:(pb + 1) * TT],
                                        AF.Sigmoid,
                                        bias=encb_c[:, m:m + 1]))
            if DEBUG_SCRATCH:
                sy.wait_ge(s_spk.sem, s_spk.n)
                sy.dma_start(out=spk_sp[:, jj * 16 * TT:(jj + 1) * 16 * TT],
                             in_=spk_bf[:]).then_inc(d_cmb.sem, 16)
                sy.wait_ge(s_ek2.sem, s_ek2.n)
                sy.dma_start(out=xnf_sp[:, jj * 8 * TT:(jj + 1) * 8 * TT],
                             in_=xnf_bf[:]).then_inc(d_cmb.sem, 16)
                d_cmb.n += 32
            te.wait_ge(s_spk.sem, s_spk.n)
            if jj >= 1:
                te.wait_ge(v_lnf.sem, 2 * jj)
            for k in range(16):
                ins = te.matmul(ps_r[0:64, 0:TT],
                                s2c_bf[:, k * 64:(k + 1) * 64],
                                spk_bf[:, k * TT:(k + 1) * TT],
                                start=(k == 0), stop=(k == 15))
            t_s2c.inc(ins)
            sc.wait_ge(t_s2c.sem, t_s2c.n)
            if jj >= 2:
                sc.wait_ge(t_eh.sem, (jj - 1) * 64)
            s_cnt.inc(sc.activation(cont_bf[:, pf * TT:(pf + 1) * TT],
                                    ps_r[0:64, 0:TT], AF.Identity,
                                    bias=s2cb_c[:]))
            te.wait_ge(s_cnt.sem, s_cnt.n)
            if jj >= 1:
                te.wait_ge(v_lnf.sem, 2 * jj)
            if jj >= 1:
                te.wait_ge(s_rt.sem, 2 * jj)
            t_rt.inc(te.matmul(ps_r[0:64, TT:2 * TT], r1_bf[:],
                               cont_bf[:, pf * TT:(pf + 1) * TT],
                               start=True, stop=True))
            sc.wait_ge(t_rt.sem, t_rt.n)
            if jj >= 2:
                sc.wait_ge(t_rt.sem, t_rt.n)
            s_rt.inc(sc.activation(rh_bf[:, pf * TT:(pf + 1) * TT],
                                   ps_r[0:64, TT:2 * TT], AF.Relu,
                                   bias=rb1_c[:]))
            te.wait_ge(s_rt.sem, s_rt.n)
            t_rt.inc(te.matmul(ps_r[0:8, TT:2 * TT], r2_bf[:],
                               rh_bf[:, pf * TT:(pf + 1) * TT],
                               start=True, stop=True))
            sc.wait_ge(t_rt.sem, t_rt.n)
            if jj >= 2:
                sc.wait_ge(t_rt.sem, 10 * (jj - 1))
            s_rt.inc(sc.activation(lgits[:, pf * TT:(pf + 1) * TT],
                                   ps_r[0:8, TT:2 * TT], AF.Identity,
                                   bias=rb2_c[:]))
            if DEBUG_SCRATCH:
                sy.wait_ge(s_rt.sem, s_rt.n)
                sy.dma_start(out=lg_sp[0:8, jj * TT:(jj + 1) * TT],
                             in_=lgits[:, pf * TT:(pf + 1) * TT]
                             ).then_inc(d_cmb.sem, 16)
                d_cmb.n += 16
                sy.wait_ge(s_cnt.sem, s_cnt.n)
                sy.dma_start(out=cont_sp[0:64, jj * TT:(jj + 1) * TT],
                             in_=cont_bf[:, pf * TT:(pf + 1) * TT]
                             ).then_inc(d_cmb.sem, 16)
                d_cmb.n += 16
            for t4 in range(4):
                it4 = jj * 4 + t4
                te.wait_ge(s_rt.sem, s_rt.n)
                if t4 == 0 and jj >= 1:
                    te.wait_ge(v_moe.sem, NE * jj)
                if t4 == 0 and jj >= 2:
                    te.wait_ge(s_cmb.sem, 2 * (jj - 1))
                if t4 >= 1:
                    te.wait_ge(v_rt.sem, v_rt.n)
                t_rt.inc(te.transpose(
                    ps_eo[:, t4 * 8:(t4 + 1) * 8],
                    lgits[:, pf * TT + t4 * 128:pf * TT + (t4 + 1) * 128],
                    ident_f[0:8, 0:8]))
                ve.wait_ge(t_rt.sem, t_rt.n)
                if t4 == 0 and jj >= 1:
                    ve.wait_ge(s_cmb.sem, 2 * jj - 1)
                v_rt.inc(ve.tensor_copy(lg_sb[:, t4 * 8:(t4 + 1) * 8],
                                        ps_eo[:, t4 * 8:(t4 + 1) * 8]))
            for t4 in range(4):
                ve.reduce_max(rt_s[:, t4 * 4:t4 * 4 + 1],
                              lg_sb[:, t4 * 8:(t4 + 1) * 8],
                              axis=mybir.AxisListType.X)
            for t4 in range(4):
                ve.tensor_scalar(rt_m[:, t4 * 8:(t4 + 1) * 8],
                                 lg_sb[:, t4 * 8:(t4 + 1) * 8],
                                 rt_s[:, t4 * 4:t4 * 4 + 1], None,
                                 op0=ALU.is_equal)
            for t4 in range(4):
                ve.scalar_tensor_tensor(rt_g[:, t4 * 8:(t4 + 1) * 8],
                                        rt_m[:, t4 * 8:(t4 + 1) * 8],
                                        -1e30,
                                        lg_sb[:, t4 * 8:(t4 + 1) * 8],
                                        op0=ALU.mult, op1=ALU.add)
            for t4 in range(4):
                ve.reduce_max(rt_s[:, t4 * 4 + 1:t4 * 4 + 2],
                              rt_g[:, t4 * 8:(t4 + 1) * 8],
                              axis=mybir.AxisListType.X)
            for t4 in range(4):
                ve.tensor_tensor(rt_s[:, t4 * 4 + 2:t4 * 4 + 3],
                                 rt_s[:, t4 * 4:t4 * 4 + 1],
                                 rt_s[:, t4 * 4 + 1:t4 * 4 + 2],
                                 op=ALU.add)
            for t4 in range(4):
                ins = ve.tensor_scalar(rt_s[:, t4 * 4 + 3:t4 * 4 + 4],
                                       rt_s[:, t4 * 4 + 2:t4 * 4 + 3],
                                       -1.0, None, op0=ALU.mult)
            v_rt.inc(ins)
            sc.wait_ge(v_rt.sem, v_rt.n)
            for t4 in range(4):
                ins = sc.activation(rt_m[:, t4 * 8:(t4 + 1) * 8],
                                    lg_sb[:, t4 * 8:(t4 + 1) * 8],
                                    AF.Sigmoid, scale=2.0,
                                    bias=rt_s[:, t4 * 4 + 3:t4 * 4 + 4])
            s_cmb.inc(ins)
            ve.wait_ge(s_cmb.sem, s_cmb.n)
            for t4 in range(4):
                ve.tensor_scalar(rt_g[:, t4 * 8:(t4 + 1) * 8],
                                 lg_sb[:, t4 * 8:(t4 + 1) * 8],
                                 rt_s[:, t4 * 4 + 1:t4 * 4 + 2], None,
                                 op0=ALU.is_ge)
            for t4 in range(4):
                ins = ve.tensor_tensor(lg_sb[:, t4 * 8:(t4 + 1) * 8],
                                       rt_m[:, t4 * 8:(t4 + 1) * 8],
                                       rt_g[:, t4 * 8:(t4 + 1) * 8],
                                       op=ALU.mult)
            v_rt.inc(ins)
            te.wait_ge(v_rt.sem, v_rt.n)
            if jj >= 1:
                te.wait_ge(s_cmb.sem, 2 * jj)
            for t4 in range(4):
                ins = te.transpose(ps_cr[0:8, t4 * 128:(t4 + 1) * 128],
                                   lg_sb[:, t4 * 8:(t4 + 1) * 8],
                                   ident_f[:])
            t_rt.inc(ins)
            sc.wait_ge(t_rt.sem, t_rt.n)
            for t4 in range(4):
                if jj >= 2 and t4 == 0:
                    sc.wait_ge(d_cmb.sem, (jj - 1) * 16)
                ins = sc.activation(
                    comb[:, pf * TT + t4 * 128:pf * TT + (t4 + 1) * 128],
                    ps_cr[0:8, t4 * 128:(t4 + 1) * 128], AF.Copy)
            s_cmb.inc(ins)
            sy.wait_ge(s_cmb.sem, s_cmb.n)
            sy.dma_start(out=comb_sp[0:8, jj * TT:(jj + 1) * TT],
                         in_=comb[:, pf * TT:(pf + 1) * TT]
                         ).then_inc(d_cmb.sem, 16)
            d_cmb.n += 16
            for e in range(NE):
                pe2 = e % 2
                ie = jj * NE + e
                if ie >= 2:
                    sy.wait_ge(v_moe.sem, ie - 1)
                sy.wait_ge(d_cmb.sem, d_cmb.n)
                sy.dma_start(out=crep_sb[:, pe2 * TT:(pe2 + 1) * TT],
                             in_=comb_sp[e, jj * TT:(jj + 1) * TT
                                         ].partition_broadcast(64)
                             ).then_inc(d_crp.sem, 16)
                d_crp.n += 16
                for m in range(8):
                    pm = m % 2
                    iem = (jj * NE + e) * 8 + m
                    te.wait_ge(s_cnt.sem, s_cnt.n)
                    if iem >= 2:
                        te.wait_ge(s_eh.sem, iem - 1)
                    ins = te.matmul(
                        ps_a[:, pm * TT:(pm + 1) * TT],
                        ew1_bf[:, e * 1024 + m * 128:
                               e * 1024 + (m + 1) * 128],
                        cont_bf[:, pf * TT:(pf + 1) * TT],
                        start=True, stop=True)
                    t_eh.inc(ins)
                    sc.wait_ge(t_eh.sem, t_eh.n)
                    s_eh.inc(sc.activation(
                        eh_bf[:, pm * TT:(pm + 1) * TT],
                        ps_a[:, pm * TT:(pm + 1) * TT], AF.Relu,
                        bias=eb1_c[:, e * 8 + m:e * 8 + m + 1]))
                    te.wait_ge(s_eh.sem, s_eh.n)
                    if m == 0 and (jj * NE + e) >= 1:
                        te.wait_ge(v_moe.sem, jj * NE + e)
                    ins = te.matmul(
                        ps_eo[0:64, 0:TT],
                        ew2_bf[:, (e * 8 + m) * 64:(e * 8 + m + 1) * 64],
                        eh_bf[:, pm * TT:(pm + 1) * TT],
                        start=(m == 0), stop=(m == 7))
                    if m == 7:
                        t_eo.inc(ins)
                ve.wait_ge(d_crp.sem, d_crp.n)
                ve.wait_ge(t_eo.sem, t_eo.n)
                ve.scalar_tensor_tensor(pe_sb[:, pf * TT:(pf + 1) * TT],
                                        ps_eo[0:64, 0:TT],
                                        eb2_c[:, e:e + 1],
                                        crep_sb[:, pe2 * TT:(pe2 + 1) * TT],
                                        op0=ALU.add, op1=ALU.mult)
                if e == 0 and jj >= 2:
                    ve.wait_ge(s_mbf.sem, jj - 1)
                if e == 0:
                    v_moe.inc(ve.tensor_copy(
                        moe_f[:, pf * TT:(pf + 1) * TT],
                        pe_sb[:, pf * TT:(pf + 1) * TT]))
                else:
                    v_moe.inc(ve.tensor_tensor(
                        moe_f[:, pf * TT:(pf + 1) * TT],
                        moe_f[:, pf * TT:(pf + 1) * TT],
                        pe_sb[:, pf * TT:(pf + 1) * TT], op=ALU.add))
            sc.wait_ge(v_moe.sem, v_moe.n)
            s_mbf.inc(sc.activation(moe_bf[:, pf * TT:(pf + 1) * TT],
                                    moe_f[:, pf * TT:(pf + 1) * TT],
                                    AF.Copy))
            te.wait_ge(s_mbf.sem, s_mbf.n)
            for m in range(16):
                pb = m % 2
                if jj * 16 + m >= 2:
                    te.wait_ge(s_rat.sem, jj * 16 + m - 1)
                t_c2s.inc(te.matmul(ps_a[:, pb * TT:(pb + 1) * TT],
                                    c2s_bf[:, m * 128:(m + 1) * 128],
                                    moe_bf[:, pf * TT:(pf + 1) * TT],
                                    start=True, stop=True))
                sc.wait_ge(t_c2s.sem, t_c2s.n)
                if jj >= 1 and m == 0:
                    sc.wait_ge(t_dec.sem, jj * 8)
                s_rat.inc(sc.activation(rat_bf[:, m * TT:(m + 1) * TT],
                                        ps_a[:, pb * TT:(pb + 1) * TT],
                                        AF.Sigmoid,
                                        bias=c2sb_c[:, m:m + 1]))
            te.wait_ge(s_rat.sem, s_rat.n)
            for m in range(8):
                pb = m % 2
                if jj * 8 + m >= 2:
                    te.wait_ge(s_dec.sem, jj * 8 + m - 1)
                for k in range(16):
                    ins = te.matmul(
                        ps_a[:, pb * TT:(pb + 1) * TT],
                        dec_bf[:, k * 1024 + m * 128:
                               k * 1024 + (m + 1) * 128],
                        rat_bf[:, k * TT:(k + 1) * TT],
                        start=(k == 0), stop=(k == 15))
                t_dec.inc(ins)
                sc.wait_ge(t_dec.sem, t_dec.n)
                if jj * 8 + m >= 2:
                    sc.wait_ge(d_dcs.sem, d_dcs.n)
                    sc.wait_ge(t_sum.sem, t_sum.n)
                s_dec.inc(sc.activation(dec_f[:, pb * TT:(pb + 1) * TT],
                                        ps_a[:, pb * TT:(pb + 1) * TT],
                                        AF.Sigmoid,
                                        bias=decb_c[:, m:m + 1]))
                sy.wait_ge(s_dec.sem, s_dec.n)
                sy.dma_start(out=dec_sp[:, m * LH + jj * TT:
                                        m * LH + (jj + 1) * TT],
                             in_=dec_f[:, pb * TT:(pb + 1) * TT]
                             ).then_inc(d_dcs.sem, 16)
                d_dcs.n += 16
                s_dcr.inc(sc.activation(dec_r[:, pb * TT:(pb + 1) * TT],
                                        dec_f[:, pb * TT:(pb + 1) * TT],
                                        AF.Copy))
                te.wait_ge(s_dcr.sem, s_dcr.n)
                t_sum.inc(te.matmul(ps_r[:, 0:TT], ones_r[:, 0:128],
                                    dec_r[:, pb * TT:(pb + 1) * TT],
                                    start=(m == 0), stop=(m == 7)))
                sc.wait_ge(t_sum.sem, t_sum.n)
                s_dcr.inc(sc.activation(dec_r[:, pb * TT:(pb + 1) * TT],
                                        dec_f[:, pb * TT:(pb + 1) * TT],
                                        AF.Square))
                te.wait_ge(s_dcr.sem, s_dcr.n)
                t_sum.inc(te.matmul(ps_r[:, TT:2 * TT], ones_r[:, 0:128],
                                    dec_r[:, pb * TT:(pb + 1) * TT],
                                    start=(m == 0), stop=(m == 7)))
            ve.wait_ge(t_sum.sem, t_sum.n)
            if jj >= 1:
                ve.wait_ge(d_st.sem, d_st.n)
            v_lnf.inc(ve.tensor_copy(st_row[0:1, 0:TT], ps_r[0:1, 0:TT]))
            v_lnf.inc(ve.tensor_copy(st_row[0:1, TT:2 * TT],
                                     ps_r[0:1, TT:2 * TT]))
            sy.wait_ge(v_lnf.sem, v_lnf.n)
            sy.dma_start(out=st2_sp[0, jj * TT:(jj + 1) * TT],
                         in_=st_row[0:1, 0:TT]).then_inc(d_st.sem, 16)
            d_st.n += 16
            sy.dma_start(out=st2_sp[1, jj * TT:(jj + 1) * TT],
                         in_=st_row[0:1, TT:2 * TT]).then_inc(d_st.sem, 16)
            d_st.n += 16
        _es5.close()
        phase_barrier()
        _es7 = ExitStack()
        g_stg = _es7.enter_context(nc.sbuf_tensor("g_stg", [128, 2 * 4 * TT], F32))
        g_snd = _es7.enter_context(nc.sbuf_tensor("g_snd", [128, 4 * LH], F32))
        g_rcv = _es7.enter_context(nc.sbuf_tensor("g_rcv", [128, 4 * LH], F32))
        dec_f = _es7.enter_context(nc.sbuf_tensor("dec_fg", [128, 2 * TT], F32))
        wk_g = _es7.enter_context(nc.sbuf_tensor("wk_g", [128, 2 * TT], F32))
        ps_rep = _es7.enter_context(nc.psum_tensor("ps_rep2", [128, 2 * TT], F32))
        st2_f = _es7.enter_context(nc.sbuf_tensor("st2_f", [1, 2 * LH], F32))
        mr_r = _es7.enter_context(nc.sbuf_tensor("mr_r", [1, 2 * LH], F32R))
        v_ln2 = sem("v_ln2")
        d_st2 = d_dt
        scr_row = _es7.enter_context(nc.sbuf_tensor("scr_row", [1, LH], F32))
        scr2_row = _es7.enter_context(nc.sbuf_tensor("scr2_row", [1, LH], F32))
        gp.wait_ge(d_st.sem, d_st.n)
        gp.dma_start(out=st2_f[0:1, 0:LH], in_=st2_sp[0, :]
                     ).then_inc(d_st2.sem, 16)
        gp.dma_start(out=st2_f[0:1, LH:2 * LH], in_=st2_sp[1, :]
                     ).then_inc(d_st2.sem, 16)
        d_st2.n += 32
        sc.wait_ge(d_st2.sem, d_st2.n)
        s_ln2 = sem("s_ln2")
        s_ln2.inc(sc.activation(scr_row[:], st2_f[0:1, 0:LH], AF.Copy,
                                scale=1.0 / DM))
        s_ln2.inc(sc.activation(st2_f[0:1, 0:LH], scr_row[:], AF.Square))
        s_ln2.inc(sc.activation(scr2_row[:], st2_f[0:1, LH:2 * LH], AF.Copy,
                                scale=1.0 / DM))
        gp.wait_ge(s_ln2.sem, 3)
        g_cmp.inc(gp.tensor_tensor(st2_f[0:1, LH:2 * LH], scr2_row[:],
                                   st2_f[0:1, 0:LH], op=ALU.subtract))
        sc.wait_ge(g_cmp.sem, g_cmp.n)
        sc.activation(st2_f[0:1, 0:LH], st2_f[0:1, LH:2 * LH],
                      AF.Ln, bias=cst[0:1, 0:1])
        s_ln2.inc(sc.activation(scr2_row[:], st2_f[0:1, 0:LH], AF.Exp,
                                scale=-0.5))
        ve.wait_ge(s_ln2.sem, 4)
        v_ln2.inc(ve.tensor_copy(mr_r[0:1, 0:LH], scr_row[:]))
        v_ln2.inc(ve.tensor_copy(mr_r[0:1, LH:2 * LH], scr2_row[:]))
        _rsend(gp, prep, None, None, rs_rdy, ls_loc, 2)
        d_dcr = d_xa
        t_rep = sem("t_rep")
        v_g1 = sem("v_g1")
        s_g = sem("s_g")
        for jj in range(NTH):
            pf = jj % 2
            te.wait_ge(v_ln2.sem, 2)
            te.wait_ge(v_one.sem, 2)
            if jj >= 1:
                te.wait_ge(v_g1.sem, jj * 8)
            te.matmul(ps_rep[:, 0:TT], ones_r[0:1, :],
                      mr_r[0:1, jj * TT:(jj + 1) * TT],
                      start=True, stop=True)
            t_rep.inc(te.matmul(ps_rep[:, TT:2 * TT], ones_r[0:1, :],
                                mr_r[0:1, LH + jj * TT:LH + (jj + 1) * TT],
                                start=True, stop=True))
            for q in range(8):
                iq = jj * 8 + q
                pq = iq % 2
                if iq >= 2:
                    sy.wait_ge(v_g1.sem, iq - 1)
                sy.dma_start(out=dec_f[:, pq * TT:(pq + 1) * TT],
                             in_=dec_sp[:, q * LH + jj * TT:
                                        q * LH + (jj + 1) * TT]
                             ).then_inc(d_dcr[pq].sem, 16)
                d_dcr[pq].n += 16
                ve.wait_ge(d_dcr[pq].sem, d_dcr[pq].n)
                ve.wait_ge(t_rep.sem, t_rep.n)
                if iq >= 2:
                    ve.wait_ge(s_g.sem, iq - 1)
                ve.tensor_tensor(wk_g[:, pq * TT:(pq + 1) * TT],
                                 dec_f[:, pq * TT:(pq + 1) * TT],
                                 ps_rep[:, 0:TT], op=ALU.subtract)
                v_g1.inc(ve.tensor_tensor(
                    wk_g[:, pq * TT:(pq + 1) * TT],
                    wk_g[:, pq * TT:(pq + 1) * TT],
                    ps_rep[:, TT:2 * TT], op=ALU.mult))
                sc.wait_ge(v_g1.sem, v_g1.n)
                if q < 4:
                    dst = g_stg[:, (pf * 4 + q) * TT:(pf * 4 + q + 1) * TT]
                    if jj >= 2 and q == 0:
                        sc.wait_ge(d_gsp.sem, (jj - 1) * 16)
                else:
                    dst = g_snd[:, (q - 4) * LH + jj * TT:
                                (q - 4) * LH + (jj + 1) * TT]
                s_g.inc(sc.activation(dst,
                                      wk_g[:, pq * TT:(pq + 1) * TT],
                                      AF.Silu, scale=lng_c[:, q:q + 1],
                                      bias=lnb_c[:, q:q + 1]))
            sy.wait_ge(s_g.sem, s_g.n)
            sy.dma_start(
                out=g_my_sp.rearrange("p (q l) -> p q l", q=4
                                      )[:, :, jj * TT:(jj + 1) * TT],
                in_=g_stg[:, pf * 4 * TT:(pf + 1) * 4 * TT
                          ].rearrange("p (q t) -> p q t", q=4)
            ).then_inc(d_gsp.sem, 16)
            d_gsp.n += 16
        gp.wait_ge(s_g.sem, s_g.n)
        gp.wait_ge(rs_rdy.sem, 4)
        _rsend(gp, prep, g_rcv[:], g_snd[:], rs_ga, ls_loc, 2)
        gp.wait_ge(rs_ga.sem, 2)
        gp.dma_start(out=g_pr_sp[:, :], in_=g_rcv[:]).then_inc(d_gsp.sem, 16)
        d_gsp.n += 16
        gp.wait_ge(ls_loc.sem, ls_loc.n)

        _es7.close()
        phase_barrier()
        _es6 = ExitStack()
        g_all = _es6.enter_context(nc.sbuf_tensor("g_all", [128, 4 * L], F32))
        x_f = _es6.enter_context(nc.sbuf_tensor("x_f", [128, 2 * 512], F32))
        xn_f = _es6.enter_context(nc.sbuf_tensor("xn_f", [128, 2 * 512], F32))
        mm_in = _es6.enter_context(nc.sbuf_tensor("mm_in", [128, 2 * 512], F32))
        s_f = _es6.enter_context(nc.sbuf_tensor("s_f", [128, 2 * 512], F32))
        o_f = _es6.enter_context(nc.sbuf_tensor("o_f", [128, 2 * 512], F32))
        ps_mt = _es6.enter_context(nc.psum_tensor("ps_mt", [128, 2 * 512], F32))
        ps_gt = _es6.enter_context(nc.psum_tensor("ps_gt", [128, 2 * 512], F32))
        sg_f = _es6.enter_context(nc.sbuf_tensor("sg_f", [128, 2 * 512], F32))
        d_ga = sem("d_ga")
        gp.wait_ge(d_gsp.sem, d_gsp.n)
        v_tb = gp.value_load(tb_sb.ap()[0:1, 0:1])
        v_pb = gp.value_load(tb_sb.ap()[0:1, 1:2])
        g3 = g_all.ap().rearrange("p (q l) -> p q l", q=4)
        gp.dma_start(out=g3[:, :, bass.ds(v_tb, LH)],
                     in_=g_my_sp.rearrange("p (q l) -> p q l", q=4)
                     ).then_inc(d_ga.sem, 16)
        d_ga.n += 16
        gp.dma_start(out=g3[:, :, bass.ds(v_pb, LH)],
                     in_=g_pr_sp.rearrange("p (q l) -> p q l", q=4)
                     ).then_inc(d_ga.sem, 16)
        d_ga.n += 16
        d_xf = d_xa[0]
        d_mf = d_xa[1]
        v_xn = v_st
        t_tf = s_sq
        vo_b = v_st2.n
        v_o = v_st2
        ttf_b = t_tf.n
        for tbi in range(32):
            p = tbi % 2
            if tbi >= 2:
                sy.wait_ge(v_o.sem, (tbi - 1) * 3)
            sy.dma_start(out=x_f[:, p * 512:(p + 1) * 512],
                         in_=x_true[tbi * 128:(tbi + 1) * 128, 0:512]
                         ).then_inc(d_xf.sem, 16)
            d_xf.n += 16
            sy.wait_ge(d_mm.sem, d_mm.n)
            sy.dma_start(out=mm_in[:, p * 512:(p + 1) * 512],
                         in_=mm_sp.rearrange("p (q l) -> p q l", q=4
                                             )[:, :, tbi * 128:(tbi + 1) * 128]
                         ).then_inc(d_mf.sem, 16)
            d_mf.n += 16
            ve.wait_ge(d_xf.sem, d_xf.n)
            if tbi >= 2:
                ve.wait_ge(t_tf.sem, ttf_b + tbi - 1)
            v_xn.inc(ve.tensor_scalar(xn_f[:, p * 512:(p + 1) * 512],
                                      x_f[:, p * 512:(p + 1) * 512],
                                      m_st[:, tbi:tbi + 1],
                                      r_st[:, tbi:tbi + 1],
                                      op0=ALU.subtract, op1=ALU.mult))
            te.wait_ge(d_mf.sem, d_mf.n)
            te.wait_ge(d_ga.sem, 32)
            if tbi >= 2:
                te.wait_ge(v_o.sem, vo_b + (tbi - 1) * 3)
            for q in range(4):
                te.transpose(
                    ps_mt[:, p * 512 + q * 128:p * 512 + (q + 1) * 128],
                    mm_in[:, p * 512 + q * 128:p * 512 + (q + 1) * 128],
                    ident_f[:])
                ins = te.transpose(
                    ps_gt[:, p * 512 + q * 128:p * 512 + (q + 1) * 128],
                    g_all[:, q * L + tbi * 128:q * L + (tbi + 1) * 128],
                    ident_f[:])
            t_tf.inc(ins)
            ve.wait_ge(t_tf.sem, t_tf.n)
            ve.wait_ge(v_xn.sem, v_xn.n)
            v_o.inc(ve.tensor_tensor(sg_f[:, p * 512:(p + 1) * 512],
                                     xn_f[:, p * 512:(p + 1) * 512],
                                     ps_mt[:, p * 512:(p + 1) * 512],
                                     op=ALU.add))
            v_o.inc(ve.tensor_tensor(s_f[:, p * 512:(p + 1) * 512],
                                     sg_f[:, p * 512:(p + 1) * 512],
                                     ps_gt[:, p * 512:(p + 1) * 512],
                                     op=ALU.mult))
            if tbi >= 2:
                ve.wait_ge(d_out.sem, (tbi - 1) * 16)
            v_o.inc(ve.tensor_tensor(o_f[:, p * 512:(p + 1) * 512],
                                     s_f[:, p * 512:(p + 1) * 512],
                                     x_f[:, p * 512:(p + 1) * 512],
                                     op=ALU.add))
            sy.wait_ge(v_o.sem, v_o.n)
            sy.dma_start(out=out[tbi * 128:(tbi + 1) * 128, :],
                         in_=o_f[:, p * 512:(p + 1) * 512]
                         ).then_inc(d_out.sem, 16)
            d_out.n += 16
        gp.wait_ge(d_out.sem, d_out.n)
        gp.wait_ge(ls_loc.sem, ls_loc.n)

        _es6.close()
    nc.compile()
    return nc


def _prep_inputs(inputs):
    xw = {k: np.asarray(v) for k, v in inputs.items()}
    x = xw["L0_attn_x"]
    in_maps = []
    ident = np.eye(128, dtype=np.float32)

    def colmaj(v, nch):
        return np.ascontiguousarray(v.reshape(nch, 128).T)

    for c in range(8):
        b, h = c // 2, c % 2
        perm = (np.arange(1024) if h == 0
                else np.concatenate([np.arange(512, 1024), np.arange(512)]))
        ds = slice(h * 1024, (h + 1) * 1024)
        dsp = slice((1 - h) * 1024, (2 - h) * 1024)
        xb = np.ascontiguousarray(x[b][:, perm])
        inw = np.concatenate([xw["L0_attn_in_w"][perm][:, ds],
                              xw["L0_attn_in_w"][perm][:, 2048 + h * 1024:
                                                       2048 + (h + 1) * 1024]],
                             axis=1)
        m = {
            "x_true": xb,
            "x_my": np.ascontiguousarray(xb[h * 2048:(h + 1) * 2048]),
            "tb": np.array([[h * 2048, (1 - h) * 2048]], np.uint32),
            "ident_in": ident,
            "zeros4": np.zeros((128, 4), np.float32),
            "in_w": np.ascontiguousarray(
                inw.reshape(8, 128, 2048).transpose(1, 0, 2).reshape(128, -1)),
            "norm_w": colmaj(xw["L0_attn_norm_w"][perm], 8),
            "conv_w": np.ascontiguousarray(
                xw["L0_attn_conv_w"][ds, 0, :].reshape(8, 128, 4)
                .transpose(1, 0, 2).reshape(128, 32)),
            "conv_b": colmaj(xw["L0_attn_conv_b"][ds], 8),
            "xproj": np.ascontiguousarray(
                np.pad(xw["L0_attn_xproj_w"][ds], ((0, 0), (0, 32)))
                .reshape(8, 128, 128).transpose(1, 0, 2).reshape(128, -1)),
            "dt_w": np.ascontiguousarray(xw["L0_attn_dt_w"][:, ds]),
            "dt_b": colmaj(xw["L0_attn_dt_b"][ds], 8),
            "log_A": np.ascontiguousarray(
                xw["L0_attn_log_A"][ds].reshape(8, 128, 16)
                .transpose(1, 0, 2).reshape(128, 128)),
            "D": colmaj(xw["L0_attn_D_param"][ds], 8),
            "out_w": np.ascontiguousarray(
                np.concatenate([xw["L0_attn_out_w"][ds],
                                xw["L0_attn_out_w"][dsp]], axis=0
                               )[:, perm[:512]]
                .reshape(16, 128, 512).transpose(1, 0, 2).reshape(128, -1)),
            "enc_w": np.ascontiguousarray(
                xw["L0_ffn_enc_w"][perm].reshape(8, 128, 2048)
                .transpose(1, 0, 2).reshape(128, -1)),
            "enc_b": colmaj(xw["L0_ffn_enc_b"], 16),
            "s2c_w": np.ascontiguousarray(
                xw["L0_ffn_s2c_w"].reshape(16, 128, 64)
                .transpose(1, 0, 2).reshape(128, -1)),
            "s2c_b": np.ascontiguousarray(xw["L0_ffn_s2c_b"][:, None]),
            "r_w1": np.ascontiguousarray(xw["L0_ffn_r_w1"]),
            "r_b1": np.ascontiguousarray(xw["L0_ffn_r_b1"][:, None]),
            "r_w2": np.ascontiguousarray(xw["L0_ffn_r_w2"]),
            "r_b2": np.ascontiguousarray(xw["L0_ffn_r_b2"][:, None]),
            "e_w1": np.ascontiguousarray(
                xw["L0_ffn_e_w1"].transpose(1, 0, 2).reshape(64, -1)),
            "e_b1": np.ascontiguousarray(
                xw["L0_ffn_e_b1"].reshape(8, 8, 128)
                .transpose(2, 0, 1).reshape(128, 64)),
            "e_w2": np.ascontiguousarray(
                xw["L0_ffn_e_w2"].reshape(8, 8, 128, 64)
                .transpose(2, 0, 1, 3).reshape(128, -1)),
            "e_b2": np.ascontiguousarray(xw["L0_ffn_e_b2"].T),
            "c2s_w": np.ascontiguousarray(xw["L0_ffn_c2s_w"]),
            "c2s_b": colmaj(xw["L0_ffn_c2s_b"], 16),
            "dec_w": np.ascontiguousarray(
                xw["L0_ffn_dec_w"][:, perm].reshape(16, 128, 1024)
                .transpose(1, 0, 2).reshape(128, -1)),
            "dec_b": colmaj(xw["L0_ffn_dec_b"][perm], 8),
            "ln_g": colmaj(xw["L0_ffn_ln_g"][perm], 8),
            "ln_b": colmaj(xw["L0_ffn_ln_b"][perm], 8),
        }
        in_maps.append(m)
    return in_maps


def kernel(**inputs):
    in_maps = _prep_inputs(inputs)
    nc = build_program()
    res = run_bass_kernel_spmd(nc, in_maps, core_ids=list(range(8)))
    out = np.empty((4, L, DM), np.float32)
    for c in range(8):
        b, h = c // 2, c % 2
        out[b][:, h * 512:(h + 1) * 512] = res.results[c]["out"]
    return out
